# version 04
# baseline (speedup 1.0000x reference)
import math
import numpy as np
import ml_dtypes
from contextlib import ExitStack

import concourse.bass as bass
import concourse.mybir as mybir
from concourse.bass import IndirectOffsetOnAxis
from concourse.bass_utils import run_bass_kernel_spmd

F32 = mybir.dt.float32
BF16 = mybir.dt.bfloat16
I32 = mybir.dt.int32
AF = mybir.ActivationFunctionType
ALU = mybir.AluOpType
AX = mybir.AxisListType
BF16NP = ml_dtypes.bfloat16

NEG = -1.0e30
EPS_RMS = 1.0e-6
EPS_LN = 1.0e-5

REAL = dict(D=4096, RQ=1024, KVR=256, H=32, S=4096, E=64)


def derived(cfg):
    g = dict(cfg)
    g["VD"] = 128
    g["KVD"] = 192
    g["NB"] = 8
    g["BS"] = cfg["S"] // 8
    assert g["BS"] % 128 == 0 and g["BS"] <= 512
    g["TPB"] = g["BS"] // 128
    g["NKEY"] = 9 * g["BS"]
    g["NKT"] = g["NKEY"] // 128
    g["OWN"] = 2 * g["BS"]
    g["OWNT"] = g["OWN"] // 128
    g["DC"] = cfg["D"] // 128
    g["RC"] = cfg["RQ"] // 128
    g["KC"] = cfg["KVR"] // 128
    g["WKA"] = cfg["KVR"] + 64
    g["MSC"] = 0.1 * math.log(128.0) + 1.0
    g["SCALE"] = (128.0 ** -0.5) * g["MSC"] * g["MSC"]
    g["ETW"] = min(512, cfg["H"] * 128)
    g["NET"] = cfg["H"] * 128 // g["ETW"]
    g["HPE"] = g["ETW"] // 128
    g["G1B"] = min(2, g["OWNT"])
    g["TQG"] = min(4, g["OWNT"])
    g["DQW"] = 1024 if cfg["D"] % 1024 == 0 else cfg["D"]
    assert g["TQG"] * (g["DQW"] // 512) <= 8
    return g


def yarn_cos_sin(seq_len, dim=64, base=10000.0, factor=128.0, beta_fast=32.0,
                 beta_slow=1.0, orig_max=8192, mscale=1.0, mscale_all_dim=1.0):
    def corr_dim(r):
        return dim * math.log(orig_max / (r * 2 * math.pi)) / (2 * math.log(base))
    low = max(math.floor(corr_dim(beta_fast)), 0)
    high = min(math.ceil(corr_dim(beta_slow)), dim - 1)
    hi = high + 0.001 if low == high else float(high)
    ramp = np.clip((np.arange(dim // 2, dtype=np.float32) - low) / (hi - low), 0.0, 1.0)
    inv_freq_mask = 1.0 - ramp
    freq_extra = 1.0 / base ** (np.arange(0, dim, 2, dtype=np.float32) / dim)
    freq_inter = freq_extra / factor
    inv_freq = freq_inter * (1.0 - inv_freq_mask) + freq_extra * inv_freq_mask
    t = np.arange(seq_len, dtype=np.float32)
    freqs = np.outer(t, inv_freq)
    emb = np.concatenate([freqs, freqs], axis=-1)

    def gm(s, m):
        return 1.0 if s <= 1 else 0.1 * m * math.log(s) + 1.0
    ms = gm(factor, mscale) / gm(factor, mscale_all_dim)
    return ((np.cos(emb) * np.float32(ms)).astype(np.float32),
            (np.sin(emb) * np.float32(ms)).astype(np.float32))


def core_blocks(c):
    return ([(0, j) for j in range(c)] + [(1, j) for j in range(7 - c)]
            + [(0, c), (1, 7 - c)])



class Buf:
    __slots__ = ("writers", "readers")

    def __init__(self):
        self.writers = []
        self.readers = []

    @property
    def writer(self):
        return self.writers[-1] if self.writers else None


class PB:
    ENGS = ("sync", "tensor", "vector", "scalar", "gpsimd")
    QPOOL = {"sync": 8, "scalarq": 8, "gpsimd": 4}

    def __init__(self, nc, stack):
        self.nc = nc
        self.q = {e: [] for e in self.ENGS}
        sems = ["tensor", "vector", "scalar"]
        for qn, k in self.QPOOL.items():
            sems += [f"{qn}{i}" for i in range(k)]
        self.sem = {s: stack.enter_context(nc.semaphore("sem_" + s)) for s in sems}
        self.cnt = {s: 0 for s in sems}
        self.qidx = {qn: 0 for qn in self.QPOOL}
        self.opidx = {e: 0 for e in self.ENGS}
        self.last_drain = {e: 0 for e in self.ENGS}
        self.hiw = {e: {} for e in self.ENGS}

    def _wait(self, eng, peng, val):
        if self.hiw[eng].get(peng, 0) >= val:
            return
        self.hiw[eng][peng] = val
        sem = self.sem[peng]
        self.opidx[eng] += 1
        self.q[eng].append(lambda e, sem=sem, val=val: e.wait_ge(sem, val))

    def _drain(self, eng):
        self.opidx[eng] += 1
        self.q[eng].append(lambda e: e.drain())
        self.last_drain[eng] = self.opidx[eng]

    def dep(self, eng, stamp):
        if stamp is None:
            return
        kind, peng, oidx, cnt = stamp
        if kind == "c" and peng == eng:
            if oidx > self.last_drain[eng]:
                self._drain(eng)
        else:
            assert cnt is not None, f"cross-engine dep {peng}->{eng} without inc"
            self._wait(eng, peng, cnt)

    def op(self, eng, fname, *args, deps=(), inc=None, bufw=(), bufr=(),
           bufd=(), bufwar=(), **kw):
        for st in deps:
            self.dep(eng, st)
        for b in tuple(bufw) + tuple(bufwar):
            if b.readers:
                for r in b.readers:
                    self.dep(eng, r)
                for w in b.writers:
                    self.dep(eng, w)
                b.readers = []
                b.writers = []
        for b in bufr:
            for w in b.writers:
                self.dep(eng, w)
        for b in bufd:
            for w in b.writers:
                self.dep(eng, w)
        if inc is None:
            inc = eng in ("vector", "scalar")
        self.opidx[eng] += 1
        oidx = self.opidx[eng]
        cnt = None
        isdma = "dma" in fname
        semkey = eng
        if isdma:
            qn = {"scalar": "scalarq", "sync": "sync",
                  "gpsimd": "gpsimd"}[eng]
            k = self.QPOOL[qn]
            semkey = f"{qn}{self.qidx[qn] % k}"
            self.qidx[qn] += 1
            if self.cnt[semkey] > 0:
                self._wait(eng, semkey, self.cnt[semkey])
            assert inc, "every DMA must inc"
        if inc:
            n = 16 if isdma else 1
            self.cnt[semkey] += n
            cnt = self.cnt[semkey]
            sem = self.sem[semkey]
            self.q[eng].append(
                lambda e, a=args, k=kw, f=fname, sem=sem, n=n:
                getattr(e, f)(*a, **k).then_inc(sem, n))
        else:
            self.q[eng].append(
                lambda e, a=args, k=kw, f=fname: getattr(e, f)(*a, **k))
        st = ("d" if isdma else "c", semkey, oidx, cnt)
        for b in bufw:
            b.writers.append(st)
        for b in bufr:
            b.readers.append(st)
        return st

    def barrier(self):
        for eng in self.ENGS:
            for sk, v in self.cnt.items():
                if sk != eng and v > 0:
                    self._wait(eng, sk, v)
            self._drain(eng)

    def realize(self, blk):
        for name in self.ENGS:
            ops = self.q[name]

            def runner(e, ops=ops):
                for f in ops:
                    f(e)
            getattr(blk, name)(runner)



def build_program(cfg, phases=5, sub=9):
    g = derived(cfg)
    D, RQ, KVR, H, E, S = cfg["D"], cfg["RQ"], cfg["KVR"], cfg["H"], cfg["E"], cfg["S"]
    VD, KVD = g["VD"], g["KVD"]
    BS, TPB, NKEY, NKT = g["BS"], g["TPB"], g["NKEY"], g["NKT"]
    OWN, OWNT = g["OWN"], g["OWNT"]
    DC, RC, KC, WKA = g["DC"], g["RC"], g["KC"], g["WKA"]
    SCALE = g["SCALE"]
    ETW, NET, HPE = g["ETW"], g["NET"], g["HPE"]
    G1B, TQG, DQW = g["G1B"], g["TQG"], g["DQW"]
    VST = 132
    OST = 132

    nc = bass.Bass("TRN2", target_bir_lowering=False, debug=False)

    def din(name, shape, dt=F32):
        return nc.dram_tensor(name, shape, dt, kind="ExternalInput")

    XK = din("xk", [NKEY, D])
    WQAT = din("wqat", [D, RQ])
    WQBT = din("wqbt", [RQ, H * 128])
    WKVAT = din("wkvat", [D, WKA])
    WKVBT = din("wkvbt", [KVR, H * KVD])
    WOT = din("wot", [H * VD, D])
    RWT = din("rwt", [D, E])
    RB = din("rb", [1, E])
    EB = din("eb", [E, D])
    QAW = din("qaw", [RC, 128])
    KVW = din("kvw", [KC, 128])
    COSK = din("cosk", [NKEY, 32])
    SINK = din("sink", [NKEY, 32])
    TRI = din("tri", [128, 128])
    ID32 = din("id32", [128, 128])
    IDBF = din("idbf", [128, 128], BF16)
    SELA = din("sela", [128, 9])
    SELB = din("selb", [128, 9])
    QSEL = din("qsel", [128, H * 7], I32)
    ONESB = din("onesb", [1, 128], BF16)

    Y = nc.dram_tensor("y", [OWN, D], F32, kind="ExternalOutput")

    qsp_hi = nc.dram_tensor("qsp_hi", [H, 256, TPB * 128], BF16, kind="Internal")
    qsp_lo = nc.dram_tensor("qsp_lo", [H, 256, TPB * 128], BF16, kind="Internal")
    osp_hi = nc.dram_tensor("osp_hi", [H, OWNT, 128, 128], BF16, kind="Internal")
    osp_lo = nc.dram_tensor("osp_lo", [H, OWNT, 128, 128], BF16, kind="Internal")
    wkva_hi = nc.dram_tensor("wkva_hi", [D, WKA], BF16, kind="Internal")
    wkva_lo = nc.dram_tensor("wkva_lo", [D, WKA], BF16, kind="Internal")
    wkvb_hi = nc.dram_tensor("wkvb_hi", [KVR, H * KVD], BF16, kind="Internal")
    wkvb_lo = nc.dram_tensor("wkvb_lo", [KVR, H * KVD], BF16, kind="Internal")

    stack = ExitStack()
    with stack:
        pb = PB(nc, stack)
        B_ = Buf

        def SBx(st, name, shape, dt=F32):
            return st.enter_context(nc.sbuf_tensor(name, shape, dt))

        def PSx(st, name, shape, dt=F32):
            return st.enter_context(nc.psum_tensor(name, shape, dt))

        cosk = SBx(stack, "cosk_sb", [128, NKT * 32])
        sink = SBx(stack, "sink_sb", [128, NKT * 32])
        tri = SBx(stack, "tri_sb", [128, 128])
        id32 = SBx(stack, "id32_sb", [128, 128])
        idbf = SBx(stack, "idbf_sb", [128, 128], BF16)
        sela = SBx(stack, "sela_sb", [128, 9])
        selb = SBx(stack, "selb_sb", [128, 9])
        qsel = SBx(stack, "qsel_sb", [128, H * 7], I32)
        onesb = SBx(stack, "onesb_sb", [1, 128], BF16)
        qaw = SBx(stack, "qaw_sb", [128, RC])
        kvw = SBx(stack, "kvw_sb", [128, KC])
        b_consts = B_()
        b_qsp, b_osp = B_(), B_()
        b_wkva, b_wkvb = B_(), B_()
        b_y = B_()

        def ldc(out_ap, in_ap):
            return pb.op("sync", "dma_start", out=out_ap, in_=in_ap,
                         inc=True, bufw=(b_consts,))

        ckv_view = COSK.ap().rearrange("(n p) w -> n p w", p=128)
        skv_view = SINK.ap().rearrange("(n p) w -> n p w", p=128)
        for n in range(NKT):
            ldc(cosk[:, n * 32:(n + 1) * 32], ckv_view[n])
            ldc(sink[:, n * 32:(n + 1) * 32], skv_view[n])
        ldc(tri[:], TRI[:])
        ldc(id32[:], ID32[:])
        ldc(idbf[:], IDBF[:])
        ldc(sela[:], SELA[:])
        ldc(selb[:], SELB[:])
        ldc(qsel[:], QSEL[:])
        ldc(onesb[:], ONESB[:])
        pb.op("sync", "dma_start", out=qaw[:],
              in_=QAW.ap().rearrange("a b -> b a"), inc=True,
              bufw=(b_consts,), allow_slow_non_contiguous=True)
        pb.op("sync", "dma_start", out=kvw[:],
              in_=KVW.ap().rearrange("a b -> b a"), inc=True,
              bufw=(b_consts,), allow_slow_non_contiguous=True)

        def evac_hilo(src_ap, hi_ap, lo_ap, src_bufs, hi_buf, lo_buf, deps=(),
                      scale=1.0):
            plain = isinstance(scale, float) and scale == 1.0
            if plain:
                h = pb.op("scalar", "copy", hi_ap, src_ap, deps=deps,
                          bufw=(hi_buf,), bufr=src_bufs)
            else:
                h = pb.op("scalar", "activation", hi_ap, src_ap, AF.Copy, scale=scale, deps=deps,
                          bufw=(hi_buf,), bufr=src_bufs)
            l = pb.op("vector", "scalar_tensor_tensor", lo_ap, src_ap,
                      1.0 if plain else scale, hi_ap,
                      op0=ALU.mult, op1=ALU.subtract, deps=deps + (h,),
                      bufw=(lo_buf,), bufr=src_bufs + (hi_buf,))
            return h, l

        with ExitStack() as ph0:
            wsf = SBx(ph0, "wsf", [128, 2048])
            wsh = SBx(ph0, "wsh", [128, 2048], BF16)
            wsl = SBx(ph0, "wsl", [128, 2048], BF16)
            b_wsf, b_wsh, b_wsl = B_(), B_(), B_()

            def presplit(src, dst_hi, dst_lo, rows, cols, bdst):
                assert rows % 128 == 0
                for rchunk in range(rows // 128):
                    for c0 in range(0, cols, 2048):
                        w = min(2048, cols - c0)
                        s = pb.op("sync", "dma_start", out=wsf[:, 0:w],
                                  in_=src[rchunk * 128:(rchunk + 1) * 128,
                                          c0:c0 + w],
                                  inc=True, bufw=(b_wsf,))
                        h, l = evac_hilo(wsf[:, 0:w], wsh[:, 0:w], wsl[:, 0:w],
                                         (b_wsf,), b_wsh, b_wsl, deps=(s,))
                        pb.op("scalar", "dma_start",
                              out=dst_hi[rchunk * 128:(rchunk + 1) * 128,
                                         c0:c0 + w],
                              in_=wsh[:, 0:w], deps=(h,), inc=True,
                              bufw=(bdst,), bufr=(b_wsh,))
                        pb.op("scalar", "dma_start",
                              out=dst_lo[rchunk * 128:(rchunk + 1) * 128,
                                         c0:c0 + w],
                              in_=wsl[:, 0:w], deps=(l,), inc=True,
                              bufw=(bdst,), bufr=(b_wsl,))

            if phases >= 1:
                presplit(WKVAT, wkva_hi, wkva_lo, D, WKA, b_wkva)
                presplit(WKVBT, wkvb_hi, wkvb_lo, KVR, H * KVD, b_wkvb)
            pb.barrier()

        with ExitStack() as s123:
            cpk_hi = SBx(s123, "cpk_hi", [128, KC * NKEY], BF16)
            cpk_lo = SBx(s123, "cpk_lo", [128, KC * NKEY], BF16)
            kpeT_hi = SBx(s123, "kpeT_hi", [128, NKEY], BF16)
            kpeT_lo = SBx(s123, "kpeT_lo", [128, NKEY], BF16)
            cq_hi = SBx(s123, "cq_hi", [128, RC * OWN], BF16)
            cq_lo = SBx(s123, "cq_lo", [128, RC * OWN], BF16)
            b_cpk_h, b_cpk_l = B_(), B_()
            b_kpeT_h, b_kpeT_l = B_(), B_()
            b_cq_h, b_cq_l = B_(), B_()
            pbank = [PSx(s123, f"pb{i}", [128, 512]) for i in range(6)]
            b_pb = [B_() for _ in range(6)]
            pbb = [PSx(s123, f"pbb{i}", [128, 512]) for i in range(2)]
            b_pbb = [B_() for _ in range(2)]

            statb2 = [SBx(s123, f"statb{i}", [128, 8 * 6]) for i in range(2)]
            mvv2 = [SBx(s123, f"mvv{i}", [128, 8]) for i in range(2)]
            b_stat2 = [B_(), B_()]
            b_mvv2 = [B_(), B_()]
            def set_stat(par):
                return dict(statb=statb2[par], mvv=mvv2[par],
                            b_stat=b_stat2[par], b_mvv=b_mvv2[par])

            CUR = {}

            def invsigma_ops(meansq_src_col, eps, dst_col, deps):
                mvv, b_mvv = CUR["mvv"], CUR["b_mvv"]
                """mvv[dst] = 1/sqrt(mvv[src]+eps) via exp(-0.5 ln(x+eps))."""
                s0 = pb.op("vector", "tensor_scalar_add",
                           mvv[:, dst_col:dst_col + 1],
                           mvv[:, meansq_src_col:meansq_src_col + 1],
                           float(eps), deps=deps, bufw=(b_mvv,), bufr=(b_mvv,))
                s1 = pb.op("scalar", "activation", mvv[:, dst_col:dst_col + 1],
                           mvv[:, dst_col:dst_col + 1], AF.Ln,
                           scale=1.0, deps=(s0,),
                           bufw=(b_mvv,), bufr=(b_mvv,))
                return pb.op("scalar", "activation", mvv[:, dst_col:dst_col + 1],
                             mvv[:, dst_col:dst_col + 1], AF.Exp,
                             scale=-0.5, deps=(s1,),
                             bufw=(b_mvv,), bufr=(b_mvv,))

            def ln_stats(src_ap_fn, ncol, src_bufs, eps, deps):
                statb, mvv = CUR["statb"], CUR["mvv"]
                b_stat, b_mvv = CUR["b_stat"], CUR["b_mvv"]
                nch = (ncol + 511) // 512
                sts = None
                for ci in range(nch):
                    w = min(512, ncol - ci * 512)
                    sts = pb.op("vector", "bn_stats", statb[:, ci * 6:(ci + 1) * 6],
                                src_ap_fn(ci * 512, w), deps=tuple(deps),
                                bufw=(b_stat,), bufr=src_bufs)
                sa = pb.op("vector", "bn_aggr", mvv[:, 0:2], statb[:, 0:6 * nch],
                           deps=(sts,), bufw=(b_mvv,), bufr=(b_stat,))
                return invsigma_ops(1, eps, 3, (sa,))

            def rms_stats(src_ap_fn, ncol, src_bufs, eps, deps):
                statb, mvv = CUR["statb"], CUR["mvv"]
                b_stat, b_mvv = CUR["b_stat"], CUR["b_mvv"]
                nch = (ncol + 511) // 512
                sts = None
                for ci in range(nch):
                    w = min(512, ncol - ci * 512)
                    sts = pb.op("vector", "bn_stats", statb[:, ci * 6:(ci + 1) * 6],
                                src_ap_fn(ci * 512, w), deps=tuple(deps),
                                bufw=(b_stat,), bufr=src_bufs)
                sa = pb.op("vector", "bn_aggr", mvv[:, 0:2], statb[:, 0:6 * nch],
                           deps=(sts,), bufw=(b_mvv,), bufr=(b_stat,))
                sm = pb.op("vector", "scalar_tensor_tensor", mvv[:, 4:5],
                           mvv[:, 0:1], mvv[:, 0:1], mvv[:, 1:2],
                           op0=ALU.mult, op1=ALU.add, deps=(sa,),
                           bufw=(b_mvv,), bufr=(b_mvv,))
                return invsigma_ops(4, eps, 6, (sm,))

            with ExitStack() as ph1:
                xb = [SBx(ph1, f"xb{i}", [128, D]) for i in range(2)]
                b_xb = [B_(), B_()]
                xhat2 = [SBx(ph1, f"xhat{i}", [128, D]) for i in range(2)]
                b_xhat2 = [B_(), B_()]
                xfm_h = [SBx(ph1, f"xfm_h{i}", [128, 128], BF16) for i in range(4)]
                xfm_l = [SBx(ph1, f"xfm_l{i}", [128, 128], BF16) for i in range(4)]
                b_xfm_h = [B_() for _ in range(4)]
                b_xfm_l = [B_() for _ in range(4)]
                wa_h = [SBx(ph1, f"wa_h{i}", [128, WKA], BF16) for i in range(2)]
                wa_l = [SBx(ph1, f"wa_l{i}", [128, WKA], BF16) for i in range(2)]
                b_wa_h = [B_(), B_()]
                b_wa_l = [B_(), B_()]
                cbuf = SBx(ph1, "cbuf", [128, KVR])
                b_cbuf = B_()
                krop = SBx(ph1, "krop", [128, 64])
                ktmp = SBx(ph1, "ktmp", [128, 64])
                b_krop, b_ktmp = B_(), B_()
                kst_h = SBx(ph1, "kst_h", [128, 128], BF16)
                kst_l = SBx(ph1, "kst_l", [128, 128], BF16)
                b_kst_h, b_kst_l = B_(), B_()

                for kt in range(NKT if phases >= 2 else 0):
                    bx = kt % 2
                    CUR.update(set_stat(bx))
                    statb, mvv = CUR["statb"], CUR["mvv"]
                    b_stat, b_mvv = CUR["b_stat"], CUR["b_mvv"]
                    xhat, b_xhat = xhat2[bx], b_xhat2[bx]
                    lx = pb.op("sync", "dma_start", out=xb[bx][:],
                               in_=XK[kt * 128:(kt + 1) * 128, :],
                               inc=True, bufw=(b_xb[bx],))
                    siv = ln_stats(lambda o, w, _b=xb[bx]: _b[:, o:o + w],
                                   D, (b_xb[bx],), EPS_LN, (lx,))
                    sx = pb.op("vector", "tensor_scalar", xhat[:], xb[bx][:],
                               mvv[:, 0:1], mvv[:, 3:4],
                               op0=ALU.subtract, op1=ALU.mult,
                               deps=(siv,), bufw=(b_xhat,),
                               bufr=(b_xb[bx], b_mvv))
                    pc = kt % 2
                    for dc in range(DC):
                        pt = dc % 2
                        lwh = pb.op("sync", "dma_start", out=wa_h[pt][:],
                                    in_=wkva_hi[dc * 128:(dc + 1) * 128, :],
                                    inc=True, bufw=(b_wa_h[pt],), bufr=(b_wkva,))
                        lwl = pb.op("sync", "dma_start", out=wa_l[pt][:],
                                    in_=wkva_lo[dc * 128:(dc + 1) * 128, :],
                                    inc=True, bufw=(b_wa_l[pt],), bufr=(b_wkva,))
                        stp = pb.op("tensor", "transpose", pbank[4 + pt][:, 0:128],
                                    xhat[:, dc * 128:(dc + 1) * 128], id32[:],
                                    deps=(sx,), inc=True,
                                    bufw=(b_pb[4 + pt],), bufr=(b_xhat, b_consts))
                        eh, el = evac_hilo(pbank[4 + pt][:, 0:128],
                                           xfm_h[pt][:], xfm_l[pt][:],
                                           (b_pb[4 + pt],), b_xfm_h[pt],
                                           b_xfm_l[pt], deps=(stp,))
                        first = dc == 0
                        last = dc == DC - 1
                        pb.op("tensor", "matmul", pbank[pc][:, 0:WKA],
                              xfm_h[pt][:], wa_h[pt][:], start=first, stop=False,
                              deps=(eh, lwh, lwl),
                              bufwar=(b_pb[pc],) if first else ())
                        pb.op("tensor", "matmul", pbank[pc][:, 0:WKA],
                              xfm_h[pt][:], wa_l[pt][:], start=False, stop=False)
                        pb.op("tensor", "matmul", pbank[pc][:, 0:WKA],
                              xfm_l[pt][:], wa_h[pt][:], start=False, stop=last,
                              deps=(el,), inc=True,
                              bufw=(b_pb[pc],) if last else (),
                              bufr=(b_xfm_h[pt], b_xfm_l[pt], b_wa_h[pt],
                                    b_wa_l[pt]))
                    mm_done = b_pb[pc].writer
                    sr = rms_stats(lambda o, w, _p=pbank[pc]: _p[:, o:o + w],
                                   KVR, (b_pb[pc],), EPS_RMS, (mm_done,))
                    sc = pb.op("scalar", "activation", cbuf[:], pbank[pc][:, 0:KVR],
                               AF.Copy, scale=mvv[:, 6:7],
                               deps=(sr, mm_done),
                               bufw=(b_cbuf,), bufr=(b_pb[pc], b_mvv))
                    ap_ev = bass.AP(pbank[pc], KVR, [[512, 128], [2, 32]])
                    ap_od = bass.AP(pbank[pc], KVR + 1, [[512, 128], [2, 32]])
                    cos_t = cosk[:, kt * 32:(kt + 1) * 32]
                    sin_t = sink[:, kt * 32:(kt + 1) * 32]
                    r1 = pb.op("vector", "tensor_tensor", krop[:, 0:32], ap_ev,
                               cos_t, op=ALU.mult, deps=(mm_done,),
                               bufw=(b_krop,), bufr=(b_pb[pc], b_consts))
                    r2 = pb.op("vector", "tensor_tensor", krop[:, 32:64], ap_od,
                               cos_t, op=ALU.mult, deps=(mm_done,),
                               bufw=(b_krop,), bufr=(b_pb[pc], b_consts))
                    r3 = pb.op("vector", "tensor_tensor", ktmp[:, 0:32], ap_od,
                               sin_t, op=ALU.mult, deps=(mm_done,),
                               bufw=(b_ktmp,), bufr=(b_pb[pc], b_consts))
                    r4 = pb.op("vector", "tensor_tensor", ktmp[:, 32:64], ap_ev,
                               sin_t, op=ALU.mult, deps=(mm_done,),
                               bufw=(b_ktmp,), bufr=(b_pb[pc], b_consts))
                    r5 = pb.op("vector", "tensor_sub", krop[:, 0:32], krop[:, 0:32],
                               ktmp[:, 0:32], deps=(r1, r3),
                               bufw=(b_krop,), bufr=(b_ktmp,))
                    r6 = pb.op("vector", "tensor_add", krop[:, 32:64], krop[:, 32:64],
                               ktmp[:, 32:64], deps=(r2, r4),
                               bufw=(b_krop,), bufr=(b_ktmp,))
                    for rc in range(KC):
                        pt = rc % 2
                        stc = pb.op("tensor", "transpose", pbank[4 + pt][:, 0:128],
                                    cbuf[:, rc * 128:(rc + 1) * 128], id32[:],
                                    deps=(sc,), inc=True,
                                    bufw=(b_pb[4 + pt],), bufr=(b_cbuf, b_consts))
                        col = rc * NKEY + kt * 128
                        evac_hilo(pbank[4 + pt][:, 0:128],
                                  cpk_hi[:, col:col + 128], cpk_lo[:, col:col + 128],
                                  (b_pb[4 + pt],), b_cpk_h, b_cpk_l, deps=(stc,),
                                  scale=kvw[:, rc:rc + 1])
                    ptk = 4 + kt % 2
                    stk = pb.op("tensor", "transpose", pbank[ptk][0:64, 0:128],
                                krop[:, 0:64], id32[:], deps=(r5, r6), inc=True,
                                bufw=(b_pb[ptk],), bufr=(b_krop, b_consts))
                    kh, kl = evac_hilo(pbank[ptk][0:64, 0:128],
                                       kst_h[0:64, :], kst_l[0:64, :],
                                       (b_pb[ptk],), b_kst_h, b_kst_l,
                                       deps=(stk,))
                    pb.op("scalar", "dma_start",
                          out=kpeT_hi[64:128, kt * 128:(kt + 1) * 128],
                          in_=kst_h[0:64, :], deps=(kh,), inc=True,
                          bufw=(b_kpeT_h,), bufr=(b_kst_h,))
                    pb.op("scalar", "dma_start",
                          out=kpeT_lo[64:128, kt * 128:(kt + 1) * 128],
                          in_=kst_l[0:64, :], deps=(kl,), inc=True,
                          bufw=(b_kpeT_l,), bufr=(b_kst_l,))

                wqa_f = [SBx(ph1, f"wqa_f{i}", [128, RQ]) for i in range(2)]
                wqa_h = [SBx(ph1, f"wqa_h{i}", [128, RQ], BF16) for i in range(2)]
                wqa_l = [SBx(ph1, f"wqa_l{i}", [128, RQ], BF16) for i in range(2)]
                b_wqa_f = [B_(), B_()]
                b_wqa_h = [B_(), B_()]
                b_wqa_l = [B_(), B_()]
                xhg = [xhat2[i % 2] for i in range(G1B)]
                b_xhg = [b_xhat2[i % 2] for i in range(G1B)]
                cqt2 = [SBx(ph1, f"cqt{i}", [128, RQ]) for i in range(2)]
                b_cqt2 = [B_(), B_()]
                NQB = max(1, RQ // 512)
                QW = min(512, RQ)

                for grp in range(OWNT // G1B if phases >= 2 else 0):
                    tiles = [grp * G1B + i for i in range(G1B)]
                    for gi, towm in enumerate(tiles):
                        kt = 7 * TPB + towm
                        bx = towm % 2
                        CUR.update(set_stat(bx))
                        statb, mvv = CUR["statb"], CUR["mvv"]
                        b_stat, b_mvv = CUR["b_stat"], CUR["b_mvv"]
                        lx = pb.op("sync", "dma_start", out=xb[bx][:],
                                   in_=XK[kt * 128:(kt + 1) * 128, :],
                                   inc=True, bufw=(b_xb[bx],))
                        siv = ln_stats(lambda o, w, _b=xb[bx]: _b[:, o:o + w],
                                       D, (b_xb[bx],), EPS_LN, (lx,))
                        pb.op("vector", "tensor_scalar", xhg[gi][:], xb[bx][:],
                              mvv[:, 0:1], mvv[:, 3:4],
                              op0=ALU.subtract, op1=ALU.mult,
                              deps=(siv,), bufw=(b_xhg[gi],),
                              bufr=(b_xb[bx], b_mvv))
                    for dc in range(DC):
                        pt = dc % 2
                        lw = pb.op("sync", "dma_start", out=wqa_f[pt][:],
                                   in_=WQAT[dc * 128:(dc + 1) * 128, :],
                                   inc=True, bufw=(b_wqa_f[pt],))
                        wh, wl = evac_hilo(wqa_f[pt][:], wqa_h[pt][:], wqa_l[pt][:],
                                           (b_wqa_f[pt],), b_wqa_h[pt],
                                           b_wqa_l[pt], deps=(lw,))
                        for gi in range(G1B):
                            xi = pt * 2 + gi
                            stp = pb.op("tensor", "transpose",
                                        pbank[4 + pt][:, 0:128],
                                        xhg[gi][:, dc * 128:(dc + 1) * 128],
                                        id32[:], deps=(b_xhg[gi].writer,),
                                        inc=True, bufw=(b_pb[4 + pt],),
                                        bufr=(b_xhg[gi], b_consts))
                            eh, el = evac_hilo(pbank[4 + pt][:, 0:128],
                                               xfm_h[xi][:], xfm_l[xi][:],
                                               (b_pb[4 + pt],), b_xfm_h[xi],
                                               b_xfm_l[xi], deps=(stp,))
                            first = dc == 0
                            last = dc == DC - 1
                            for nq in range(NQB):
                                bk = gi * NQB + nq
                                pb.op("tensor", "matmul", pbank[bk][:, 0:QW],
                                      xfm_h[xi][:],
                                      wqa_h[pt][:, nq * 512:nq * 512 + QW],
                                      start=first, stop=False,
                                      deps=(eh, wh, wl),
                                      bufwar=(b_pb[bk],) if first else ())
                                pb.op("tensor", "matmul", pbank[bk][:, 0:QW],
                                      xfm_h[xi][:],
                                      wqa_l[pt][:, nq * 512:nq * 512 + QW],
                                      start=False, stop=False)
                                lastnq = nq == NQB - 1
                                pb.op("tensor", "matmul", pbank[bk][:, 0:QW],
                                      xfm_l[xi][:],
                                      wqa_h[pt][:, nq * 512:nq * 512 + QW],
                                      start=False, stop=last,
                                      deps=(el,), inc=lastnq or last,
                                      bufw=(b_pb[bk],) if last else (),
                                      bufr=(b_xfm_h[xi], b_xfm_l[xi],
                                            b_wqa_h[pt], b_wqa_l[pt])
                                      if lastnq else ())
                    for gi, towm in enumerate(tiles):
                        CUR.update(set_stat(towm % 2))
                        statb, mvv = CUR["statb"], CUR["mvv"]
                        b_stat, b_mvv = CUR["b_stat"], CUR["b_mvv"]
                        cqt, b_cqt = cqt2[towm % 2], b_cqt2[towm % 2]
                        bks = [gi * NQB + nq for nq in range(NQB)]
                        mmst = tuple(b_pb[bk].writer for bk in bks)
                        sts = None
                        for nq, bk in enumerate(bks):
                            sts = pb.op("vector", "bn_stats",
                                        statb[:, nq * 6:(nq + 1) * 6],
                                        pbank[bk][:, 0:QW], deps=mmst,
                                        bufw=(b_stat,), bufr=(b_pb[bk],))
                        sa = pb.op("vector", "bn_aggr", mvv[:, 0:2],
                                   statb[:, 0:6 * NQB], deps=(sts,),
                                   bufw=(b_mvv,), bufr=(b_stat,))
                        sm = pb.op("vector", "scalar_tensor_tensor", mvv[:, 4:5],
                                   mvv[:, 0:1], mvv[:, 0:1], mvv[:, 1:2],
                                   op0=ALU.mult, op1=ALU.add, deps=(sa,),
                                   bufw=(b_mvv,), bufr=(b_mvv,))
                        s2 = invsigma_ops(4, EPS_RMS, 6, (sm,))
                        scs = None
                        for nq, bk in enumerate(bks):
                            scs = pb.op("scalar", "activation",
                                        cqt[:, nq * 512:nq * 512 + QW],
                                        pbank[bk][:, 0:QW], AF.Copy,
                                        scale=mvv[:, 6:7],
                                        deps=(s2,) + mmst,
                                        bufw=(b_cqt,), bufr=(b_pb[bk], b_mvv))
                        for rc in range(RC):
                            pt = rc % 2
                            stq = pb.op("tensor", "transpose",
                                        pbank[4 + pt][:, 0:128],
                                        cqt[:, rc * 128:(rc + 1) * 128], id32[:],
                                        deps=(scs,), inc=True,
                                        bufw=(b_pb[4 + pt],),
                                        bufr=(b_cqt, b_consts))
                            col = rc * OWN + towm * 128
                            evac_hilo(pbank[4 + pt][:, 0:128],
                                      cq_hi[:, col:col + 128],
                                      cq_lo[:, col:col + 128],
                                      (b_pb[4 + pt],), b_cq_h, b_cq_l,
                                      deps=(stq,), scale=qaw[:, rc:rc + 1])
                pb.barrier()

            with ExitStack() as ph2:
                wqb_f = [SBx(ph2, f"wqb_f{i}", [128, RC * ETW]) for i in range(2)]
                wqb_h = [SBx(ph2, f"wqb_h{i}", [128, RC * ETW], BF16) for i in range(2)]
                wqb_l = [SBx(ph2, f"wqb_l{i}", [128, RC * ETW], BF16) for i in range(2)]
                b_wqb_f = [B_(), B_()]
                b_wqb_h = [B_(), B_()]
                b_wqb_l = [B_(), B_()]
                qtmp = SBx(ph2, "qtmp", [128, ETW])
                b_qtmp = B_()
                qrt = SBx(ph2, "qrt", [128, HPE * 64])
                b_qrt = B_()
                qtile = SBx(ph2, "qtile", [128, ETW])
                b_qtile = B_()
                qst_h = SBx(ph2, "qst_h", [128, 128], BF16)
                qst_l = SBx(ph2, "qst_l", [128, 128], BF16)
                b_qst_h, b_qst_l = B_(), B_()
                wqbt_v = WQBT.ap().rearrange("(c p) w -> p c w", p=128)

                for et in range(NET if phases >= 3 else 0):
                    wb = et % 2
                    lw = pb.op("sync", "dma_start",
                               out=wqb_f[wb].ap().rearrange("p (c w) -> p c w", c=RC),
                               in_=wqbt_v[:, :, et * ETW:(et + 1) * ETW],
                               inc=True, bufw=(b_wqb_f[wb],))
                    wh, wl = evac_hilo(wqb_f[wb][:], wqb_h[wb][:], wqb_l[wb][:],
                                       (b_wqb_f[wb],), b_wqb_h[wb], b_wqb_l[wb],
                                       deps=(lw,))
                    for towm in range(OWNT):
                        pc = towm % 2
                        for rc in range(RC):
                            first = rc == 0
                            last = rc == RC - 1
                            cq_sl = slice(rc * OWN + towm * 128,
                                          rc * OWN + (towm + 1) * 128)
                            pb.op("tensor", "matmul", pbank[pc][:, 0:ETW],
                                  cq_hi[:, cq_sl],
                                  wqb_h[wb][:, rc * ETW:(rc + 1) * ETW],
                                  start=first, stop=False,
                                  deps=(wh, wl, b_cq_h.writer, b_cq_l.writer),
                                  bufwar=(b_pb[pc],) if first else ())
                            pb.op("tensor", "matmul", pbank[pc][:, 0:ETW],
                                  cq_hi[:, cq_sl],
                                  wqb_l[wb][:, rc * ETW:(rc + 1) * ETW],
                                  start=False, stop=False)
                            pb.op("tensor", "matmul", pbank[pc][:, 0:ETW],
                                  cq_lo[:, cq_sl],
                                  wqb_h[wb][:, rc * ETW:(rc + 1) * ETW],
                                  start=False, stop=last, inc=last,
                                  bufw=(b_pb[pc],) if last else (),
                                  bufr=(b_cq_h, b_cq_l, b_wqb_h[wb],
                                        b_wqb_l[wb]) if last else ())
                        mmq = b_pb[pc].writer
                        sq = pb.op("scalar", "activation", qtmp[:, 0:ETW],
                                   pbank[pc][:, 0:ETW], AF.Copy,
                                   scale=SCALE, deps=(mmq,),
                                   bufw=(b_qtmp,), bufr=(b_pb[pc],))
                        ap_nope_i = bass.AP(qtmp, 0, [[ETW, 128], [128, HPE], [1, 64]])
                        ap_nope_o = bass.AP(qtile, 0, [[ETW, 128], [128, HPE], [1, 64]])
                        c0 = pb.op("vector", "tensor_copy", out=ap_nope_o,
                                   in_=ap_nope_i, deps=(sq,),
                                   bufw=(b_qtile,), bufr=(b_qtmp,))
                        ap_ev = bass.AP(qtmp, 64, [[ETW, 128], [128, HPE], [2, 32]])
                        ap_od = bass.AP(qtmp, 65, [[ETW, 128], [128, HPE], [2, 32]])
                        kt_own = 7 * TPB + towm
                        cos_t = bass.AP(cosk, kt_own * 32,
                                        [[NKT * 32, 128], [0, HPE], [1, 32]])
                        sin_t = bass.AP(sink, kt_own * 32,
                                        [[NKT * 32, 128], [0, HPE], [1, 32]])
                        o1 = bass.AP(qtile, 64, [[ETW, 128], [128, HPE], [1, 32]])
                        o2 = bass.AP(qtile, 96, [[ETW, 128], [128, HPE], [1, 32]])
                        t3 = bass.AP(qrt, 0, [[HPE * 64, 128], [64, HPE], [1, 32]])
                        t4 = bass.AP(qrt, 32, [[HPE * 64, 128], [64, HPE], [1, 32]])
                        g1 = pb.op("vector", "tensor_tensor", o1, ap_ev, cos_t,
                                   op=ALU.mult, deps=(sq,), bufw=(b_qtile,),
                                   bufr=(b_qtmp, b_consts))
                        g2 = pb.op("vector", "tensor_tensor", o2, ap_od, cos_t,
                                   op=ALU.mult, deps=(sq,), bufw=(b_qtile,),
                                   bufr=(b_qtmp, b_consts))
                        g3 = pb.op("vector", "tensor_tensor", t3, ap_od, sin_t,
                                   op=ALU.mult, deps=(sq,), bufw=(b_qrt,),
                                   bufr=(b_qtmp, b_consts))
                        g4 = pb.op("vector", "tensor_tensor", t4, ap_ev, sin_t,
                                   op=ALU.mult, deps=(sq,), bufw=(b_qrt,),
                                   bufr=(b_qtmp, b_consts))
                        g5 = pb.op("vector", "tensor_sub", o1, o1, t3,
                                   deps=(g1, g3), bufw=(b_qtile,), bufr=(b_qrt,))
                        g6 = pb.op("vector", "tensor_add", o2, o2, t4,
                                   deps=(g2, g4, g5), bufw=(b_qtile,),
                                   bufr=(b_qrt,))
                        side = towm // TPB
                        itile = towm % TPB
                        for hh in range(HPE):
                            h = et * HPE + hh
                            pt2 = hh % 2
                            stq = pb.op("tensor", "transpose",
                                        pbank[4 + pt2][:, 0:128],
                                        qtile[:, hh * 128:(hh + 1) * 128],
                                        id32[:], deps=(g6, c0), inc=True,
                                        bufw=(b_pb[4 + pt2],),
                                        bufr=(b_qtile, b_consts))
                            qh, ql = evac_hilo(pbank[4 + pt2][:, 0:128],
                                               qst_h[:], qst_l[:],
                                               (b_pb[4 + pt2],), b_qst_h,
                                               b_qst_l, deps=(stq,))
                            pb.op("scalar", "dma_start",
                                  out=qsp_hi[h, side * 128:(side + 1) * 128,
                                             itile * 128:(itile + 1) * 128],
                                  in_=qst_h[:], deps=(qh,), inc=True,
                                  bufw=(b_qsp,), bufr=(b_qst_h,))
                            pb.op("scalar", "dma_start",
                                  out=qsp_lo[h, side * 128:(side + 1) * 128,
                                             itile * 128:(itile + 1) * 128],
                                  in_=qst_l[:], deps=(ql,), inc=True,
                                  bufw=(b_qsp,), bufr=(b_qst_l,))
                pb.barrier()

            with ExitStack() as ph3:
                v_hi = SBx(ph3, "v_hi", [128, NKT * VST], BF16)
                v_lo = SBx(ph3, "v_lo", [128, NKT * VST], BF16)
                vstage = [SBx(ph3, f"vstage{i}", [128, 512]) for i in range(2)]
                b_vst = [B_(), B_()]
                b_v_h, b_v_l = B_(), B_()
                b_kn_h, b_kn_l = b_kpeT_h, b_kpeT_l
                knw_h = SBx(ph3, "knw_h", [128, KC * 64], BF16)
                knw_l = SBx(ph3, "knw_l", [128, KC * 64], BF16)
                vw_h = SBx(ph3, "vw_h", [128, KC * VD], BF16)
                vw_l = SBx(ph3, "vw_l", [128, KC * VD], BF16)
                b_knw_h, b_knw_l, b_vw_h, b_vw_l = B_(), B_(), B_(), B_()
                qsl_h = SBx(ph3, "qsl_h", [128, 7 * 512], BF16)
                qsl_l = SBx(ph3, "qsl_l", [128, 7 * 512], BF16)
                b_qsl_h, b_qsl_l = B_(), B_()
                qow_h = SBx(ph3, "qow_h", [128, OWN], BF16)
                qow_l = SBx(ph3, "qow_l", [128, OWN], BF16)
                b_qow_h, b_qow_l = B_(), B_()
                p32 = [SBx(ph3, f"p32_{i}", [128, 512]) for i in range(3)]
                b_p32 = [B_() for _ in range(3)]
                pt_h = [SBx(ph3, f"pt_h{i}", [128, 512], BF16) for i in range(3)]
                pt_l = [SBx(ph3, f"pt_l{i}", [128, 512], BF16) for i in range(3)]
                b_pth = [B_() for _ in range(3)]
                b_ptl = [B_() for _ in range(3)]
                oslot2 = [SBx(ph3, f"oslot{i}", [128, 9 * OST]) for i in range(2)]
                b_oslot2 = [B_(), B_()]
                negm2 = [SBx(ph3, f"negm{i}", [128, 12]) for i in range(2)]
                b_negm2 = [B_(), B_()]
                msel2 = [SBx(ph3, f"msel{i}", [128, 12]) for i in range(2)]
                coefs2 = [SBx(ph3, f"coefs{i}", [128, 20]) for i in range(2)]
                b_msel2 = [B_(), B_()]
                b_coef2 = [B_(), B_()]
                oacc2 = [SBx(ph3, f"oacc{i}", [128, 128]) for i in range(2)]
                b_oacc2 = [B_(), B_()]
                ofst_h2 = [SBx(ph3, f"ofst_h{i}", [128, 128], BF16) for i in range(2)]
                ofst_l2 = [SBx(ph3, f"ofst_l{i}", [128, 128], BF16) for i in range(2)]
                b_ofst_h2 = [B_(), B_()]
                b_ofst_l2 = [B_(), B_()]

                pb.op("vector", "memset",
                      bass.AP(v_hi, 128, [[NKT * VST, 128], [VST, NKT]]), 1.0,
                      bufw=(b_v_h,))
                pb.op("vector", "memset",
                      bass.AP(v_lo, 128, [[NKT * VST, 128], [VST, NKT]]), 0.0,
                      bufw=(b_v_l,))

                wkvb_hv = wkvb_hi.ap().rearrange("(c p) w -> p c w", p=128)
                wkvb_lv = wkvb_lo.ap().rearrange("(c p) w -> p c w", p=128)

                for h in range(H if phases >= 4 else 0):
                    l1 = pb.op("sync", "dma_start",
                               out=knw_h.ap().rearrange("p (c w) -> p c w", c=KC),
                               in_=wkvb_hv[:, :, h * KVD:h * KVD + 64],
                               inc=True, bufw=(b_knw_h,), bufr=(b_wkvb,))
                    l2 = pb.op("sync", "dma_start",
                               out=knw_l.ap().rearrange("p (c w) -> p c w", c=KC),
                               in_=wkvb_lv[:, :, h * KVD:h * KVD + 64],
                               inc=True, bufw=(b_knw_l,), bufr=(b_wkvb,))
                    l3 = pb.op("sync", "dma_start",
                               out=vw_h.ap().rearrange("p (c w) -> p c w", c=KC),
                               in_=wkvb_hv[:, :, h * KVD + 64:(h + 1) * KVD],
                               inc=True, bufw=(b_vw_h,), bufr=(b_wkvb,))
                    l4 = pb.op("sync", "dma_start",
                               out=vw_l.ap().rearrange("p (c w) -> p c w", c=KC),
                               in_=wkvb_lv[:, :, h * KVD + 64:(h + 1) * KVD],
                               inc=True, bufw=(b_vw_l,), bufr=(b_wkvb,))
                    gq = None
                    qsp_hi_flat = qsp_hi.ap().rearrange("h r w -> (h r) w")
                    qsp_lo_flat = qsp_lo.ap().rearrange("h r w -> (h r) w")
                    for j in range(7):
                        pb.op("gpsimd", "indirect_dma_start",
                              out=qsl_h[:, j * 512:j * 512 + TPB * 128],
                              out_offset=None, in_=qsp_hi_flat,
                              in_offset=IndirectOffsetOnAxis(
                                  ap=qsel[:, h * 7 + j:h * 7 + j + 1], axis=0),
                              inc=True, bufw=(b_qsl_h,),
                              bufr=(b_qsp, b_consts))
                        pb.op("gpsimd", "indirect_dma_start",
                              out=qsl_l[:, j * 512:j * 512 + TPB * 128],
                              out_offset=None, in_=qsp_lo_flat,
                              in_offset=IndirectOffsetOnAxis(
                                  ap=qsel[:, h * 7 + j:h * 7 + j + 1], axis=0),
                              inc=True, bufw=(b_qsl_l,),
                              bufr=(b_qsp, b_consts))
                    for side in range(2):
                        pb.op("scalar", "dma_start",
                              out=qow_h[:, side * (TPB * 128):
                                        (side + 1) * (TPB * 128)],
                              in_=qsp_hi[h, side * 128:(side + 1) * 128, :],
                              inc=True, bufw=(b_qow_h,), bufr=(b_qsp,))
                        gq = pb.op("scalar", "dma_start",
                                   out=qow_l[:, side * (TPB * 128):
                                             (side + 1) * (TPB * 128)],
                                   in_=qsp_lo[h, side * 128:(side + 1) * 128, :],
                                   inc=True, bufw=(b_qow_l,), bufr=(b_qsp,))
                    for t9 in range((NKEY + 511) // 512):
                        w9 = min(512, NKEY - t9 * 512)
                        pc = 4 + t9 % 2
                        for rc in range(KC):
                            first = rc == 0
                            last = rc == KC - 1
                            cp_sl = slice(rc * NKEY + t9 * 512,
                                          rc * NKEY + t9 * 512 + w9)
                            pb.op("tensor", "matmul", pbank[pc][0:64, 0:w9],
                                  knw_h[:, rc * 64:(rc + 1) * 64],
                                  cpk_hi[:, cp_sl], start=first, stop=False,
                                  deps=(l1, l2),
                                  bufd=(b_cpk_h, b_cpk_l),
                                  bufwar=(b_pb[pc],) if first else ())
                            pb.op("tensor", "matmul", pbank[pc][0:64, 0:w9],
                                  knw_h[:, rc * 64:(rc + 1) * 64],
                                  cpk_lo[:, cp_sl], start=False, stop=False)
                            pb.op("tensor", "matmul", pbank[pc][0:64, 0:w9],
                                  knw_l[:, rc * 64:(rc + 1) * 64],
                                  cpk_hi[:, cp_sl], start=False, stop=last,
                                  inc=last,
                                  bufw=(b_pb[pc],) if last else (),
                                  bufr=(b_knw_h, b_knw_l, b_cpk_h, b_cpk_l)
                                  if last else ())
                        evac_hilo(pbank[pc][0:64, 0:w9],
                                  kpeT_hi[0:64, t9 * 512:t9 * 512 + w9],
                                  kpeT_lo[0:64, t9 * 512:t9 * 512 + w9],
                                  (b_pb[pc],), b_kn_h, b_kn_l,
                                  deps=(b_pb[pc].writer,))
                    for t9 in range((NKEY + 511) // 512):
                        w9 = min(512, NKEY - t9 * 512)
                        nch9 = w9 // 128
                        pc = 4 + t9 % 2
                        for rc in range(KC):
                            first = rc == 0
                            last = rc == KC - 1
                            cp_sl = slice(rc * NKEY + t9 * 512,
                                          rc * NKEY + t9 * 512 + w9)
                            pb.op("tensor", "matmul", pbank[pc][:, 0:w9],
                                  vw_h[:, rc * VD:(rc + 1) * VD],
                                  cpk_hi[:, cp_sl],
                                  start=first, stop=False, deps=(l3, l4),
                                  bufd=(b_cpk_h, b_cpk_l),
                                  bufwar=(b_pb[pc],) if first else ())
                            pb.op("tensor", "matmul", pbank[pc][:, 0:w9],
                                  vw_h[:, rc * VD:(rc + 1) * VD],
                                  cpk_lo[:, cp_sl],
                                  start=False, stop=False)
                            pb.op("tensor", "matmul", pbank[pc][:, 0:w9],
                                  vw_l[:, rc * VD:(rc + 1) * VD],
                                  cpk_hi[:, cp_sl],
                                  start=False, stop=last, inc=last,
                                  bufw=(b_pb[pc],) if last else (),
                                  bufr=(b_vw_h, b_vw_l, b_cpk_h, b_cpk_l)
                                  if last else ())
                        sv9 = pb.op("scalar", "copy", vstage[t9 % 2][:, 0:w9],
                                    pbank[pc][:, 0:w9],
                                    deps=(b_pb[pc].writer,),
                                    bufw=(b_vst[t9 % 2],), bufr=(b_pb[pc],))
                        for kc9 in range(nch9):
                            vt = t9 * 4 + kc9
                            tb9 = vt % 2
                            stv = pb.op("tensor", "transpose",
                                        pbb[tb9][:, 0:128],
                                        vstage[t9 % 2][:, kc9 * 128:
                                                       (kc9 + 1) * 128],
                                        id32[:], deps=(sv9,), inc=True,
                                        bufw=(b_pbb[tb9],),
                                        bufr=(b_vst[t9 % 2], b_consts))
                            evac_hilo(pbb[tb9][:, 0:128],
                                      v_hi[:, vt * VST:vt * VST + VD],
                                      v_lo[:, vt * VST:vt * VST + VD],
                                      (b_pbb[tb9],), b_v_h, b_v_l,
                                      deps=(stv,))

                    def do_job(jslot, qh_ap, ql_ap, kcol, klen, qbufs, with_tri,
                               oslot, b_oslot, negm, b_negm):
                        spc = 2 + jslot % 2
                        opc = jslot % 3
                        nchunk = klen // 128
                        pb.op("tensor", "matmul", pbank[spc][:, 0:klen],
                              qh_ap(0, 128), kpeT_hi[0:128, kcol:kcol + klen],
                              start=True, stop=False,
                              bufd=qbufs + (b_kpeT_h, b_kpeT_l),
                              bufwar=(b_pb[spc],))
                        pb.op("tensor", "matmul", pbank[spc][:, 0:klen],
                              qh_ap(0, 128), kpeT_lo[0:128, kcol:kcol + klen],
                              start=False, stop=False)
                        smm = pb.op("tensor", "matmul", pbank[spc][:, 0:klen],
                                    ql_ap(0, 128), kpeT_hi[0:128, kcol:kcol + klen],
                                    start=False, stop=True,
                                    inc=True, bufw=(b_pb[spc],),
                                    bufr=qbufs + (b_kpeT_h, b_kpeT_l))
                        if sub < 2:
                            return
                        if with_tri:
                            smm = pb.op("vector", "tensor_add",
                                        pbank[spc][:, klen - 128:klen],
                                        pbank[spc][:, klen - 128:klen], tri[:],
                                        deps=(smm,), bufw=(b_pb[spc],),
                                        bufr=(b_consts,))
                        nm = pb.op("vector", "reduce_max", negm[:, jslot:jslot + 1],
                                   pbank[spc][:, 0:klen], axis=AX.X, negate=True,
                                   deps=(smm,), bufw=(b_negm,),
                                   bufr=(b_pb[spc],))
                        ex = pb.op("scalar", "activation", p32[opc][:, 0:klen],
                                   pbank[spc][:, 0:klen], AF.Exp,
                                   bias=negm[:, jslot:jslot + 1], scale=1.0,
                                   deps=(nm,), bufw=(b_p32[opc],),
                                   bufr=(b_pb[spc], b_negm))
                        if sub < 4:
                            return
                        tb = jslot % 2
                        tps = None
                        for kc2 in range(nchunk):
                            last2 = kc2 == nchunk - 1
                            tps = pb.op("tensor", "transpose",
                                        pbb[tb][:, kc2 * 128:(kc2 + 1) * 128],
                                        p32[opc][:, kc2 * 128:(kc2 + 1) * 128],
                                        id32[:], deps=(ex,), inc=last2,
                                        bufwar=(b_pbb[tb],) if kc2 == 0 else (),
                                        bufw=(b_pbb[tb],) if last2 else (),
                                        bufr=(b_p32[opc], b_consts)
                                        if last2 else (b_consts,))
                        if sub < 4.5:
                            return
                        e1, e2 = evac_hilo(pbb[tb][:, 0:nchunk * 128],
                                           pt_h[opc][:, 0:nchunk * 128],
                                           pt_l[opc][:, 0:nchunk * 128],
                                           (b_pbb[tb],), b_pth[opc], b_ptl[opc],
                                           deps=(tps,))
                        if sub < 5:
                            return
                        oc = jslot % 2
                        for kc2 in range(nchunk):
                            vt = kcol // 128 + kc2
                            first = kc2 == 0
                            last2 = kc2 == nchunk - 1
                            pb.op("tensor", "matmul", pbank[oc][:, 0:VD + 1],
                                  pt_h[opc][:, kc2 * 128:(kc2 + 1) * 128],
                                  v_hi[:, vt * VST:vt * VST + VD + 1],
                                  start=first, stop=False, deps=(e1, e2),
                                  bufwar=(b_pb[oc],) if first else ())
                            pb.op("tensor", "matmul", pbank[oc][:, 0:VD + 1],
                                  pt_h[opc][:, kc2 * 128:(kc2 + 1) * 128],
                                  v_lo[:, vt * VST:vt * VST + VD + 1],
                                  start=False, stop=False)
                            pb.op("tensor", "matmul", pbank[oc][:, 0:VD + 1],
                                  pt_l[opc][:, kc2 * 128:(kc2 + 1) * 128],
                                  v_hi[:, vt * VST:vt * VST + VD + 1],
                                  start=False, stop=last2, inc=last2,
                                  bufw=(b_pb[oc],) if last2 else (),
                                  bufr=(b_pth[opc], b_ptl[opc], b_v_h, b_v_l)
                                  if last2 else ())
                        if sub < 6:
                            return
                        pb.op("vector", "tensor_copy",
                              oslot[:, jslot * OST:jslot * OST + VD + 1],
                              pbank[oc][:, 0:VD + 1], deps=(b_pb[oc].writer,),
                              bufw=(b_oslot,), bufr=(b_pb[oc],))

                    for it in range(TPB):
                        itp = it % 2
                        oslot, b_oslot = oslot2[itp], b_oslot2[itp]
                        negm, b_negm = negm2[itp], b_negm2[itp]
                        msel, b_msel = msel2[itp], b_msel2[itp]
                        coefs, b_coef = coefs2[itp], b_coef2[itp]
                        for j in range(7):
                            qc = j * 512 + it * 128
                            do_job(j,
                                   lambda a, b, qc=qc: qsl_h[a:b, qc:qc + 128],
                                   lambda a, b, qc=qc: qsl_l[a:b, qc:qc + 128],
                                   j * BS, BS, (b_qsl_h, b_qsl_l), False,
                                   oslot, b_oslot, negm, b_negm)
                        for side in range(2):
                            qc = side * (TPB * 128) + it * 128
                            do_job(7 + side,
                                   lambda a, b, qc=qc: qow_h[a:b, qc:qc + 128],
                                   lambda a, b, qc=qc: qow_l[a:b, qc:qc + 128],
                                   (7 + side) * BS, (it + 1) * 128,
                                   (b_qow_h, b_qow_l), True,
                                   oslot, b_oslot, negm, b_negm)
                        for side in range(2 if sub >= 7 else 0):
                            oacc, b_oacc = oacc2[side], b_oacc2[side]
                            ofst_h, b_ofst_h = ofst_h2[side], b_ofst_h2[side]
                            ofst_l, b_ofst_l = ofst_l2[side], b_ofst_l2[side]
                            selx = sela if side == 0 else selb
                            ms = pb.op("vector", "scalar_tensor_tensor",
                                       msel[:, 0:9], negm[:, 0:9], -1.0,
                                       selx[:, 0:9], op0=ALU.mult, op1=ALU.add,
                                       deps=(b_negm.writer, b_oslot.writer),
                                       bufw=(b_msel,),
                                       bufr=(b_negm, b_consts))
                            nmx = pb.op("vector", "reduce_max", msel[:, 9:10],
                                        msel[:, 0:9], axis=AX.X, negate=True,
                                        deps=(ms,), bufw=(b_msel,))
                            cf = pb.op("scalar", "activation", coefs[:, 0:9],
                                       msel[:, 0:9], AF.Exp, bias=msel[:, 9:10],
                                       scale=1.0, deps=(nmx,),
                                       bufw=(b_coef,), bufr=(b_msel,))
                            lv = bass.AP(oslot, VD, [[9 * OST, 128], [OST, 9]])
                            lw2 = pb.op("vector", "tensor_tensor", coefs[:, 9:18],
                                        lv, coefs[:, 0:9], op=ALU.mult,
                                        deps=(cf,), bufw=(b_coef,),
                                        bufr=(b_oslot,))
                            ls = pb.op("vector", "reduce_sum", msel[:, 10:11],
                                       coefs[:, 9:18], axis=AX.X, deps=(lw2,),
                                       bufw=(b_msel,), bufr=(b_coef,))
                            li = pb.op("vector", "reciprocal", msel[:, 11:12],
                                       msel[:, 10:11], deps=(ls,),
                                       bufw=(b_msel,))
                            cn = pb.op("vector", "tensor_scalar", coefs[:, 0:9],
                                       coefs[:, 0:9], msel[:, 11:12], None,
                                       op0=ALU.mult, deps=(li,),
                                       bufw=(b_coef,), bufr=(b_msel,))
                            js = [jj for jj in range(9)
                                  if not (side == 0 and jj == 8)
                                  and not (side == 1 and jj == 7)]
                            a0 = pb.op("vector", "tensor_scalar", oacc[:, 0:VD],
                                       oslot[:, js[0] * OST:js[0] * OST + VD],
                                       coefs[:, js[0]:js[0] + 1], None,
                                       op0=ALU.mult, deps=(cn,),
                                       bufw=(b_oacc,), bufr=(b_oslot, b_coef))
                            for jj in js[1:]:
                                a0 = pb.op("vector", "scalar_tensor_tensor",
                                           oacc[:, 0:VD],
                                           oslot[:, jj * OST:jj * OST + VD],
                                           coefs[:, jj:jj + 1], oacc[:, 0:VD],
                                           op0=ALU.mult, op1=ALU.add,
                                           deps=(a0,), bufw=(b_oacc,),
                                           bufr=(b_oslot, b_coef))
                            pt2 = 4 + side
                            sto = pb.op("tensor", "transpose",
                                        pbank[pt2][:, 0:128], oacc[:, 0:128],
                                        id32[:], deps=(a0,), inc=True,
                                        bufw=(b_pb[pt2],),
                                        bufr=(b_oacc, b_consts))
                            ohh, oll = evac_hilo(pbank[pt2][:, 0:128],
                                                 ofst_h[:], ofst_l[:],
                                                 (b_pb[pt2],), b_ofst_h,
                                                 b_ofst_l, deps=(sto,))
                            towm = side * TPB + it
                            pb.op("scalar", "dma_start",
                                  out=osp_hi[h, towm], in_=ofst_h[:],
                                  deps=(ohh,), inc=True, bufw=(b_osp,),
                                  bufr=(b_ofst_h,))
                            pb.op("scalar", "dma_start",
                                  out=osp_lo[h, towm], in_=ofst_l[:],
                                  deps=(oll,), inc=True, bufw=(b_osp,),
                                  bufr=(b_ofst_l,))
            pb.barrier()

        with ExitStack() as ph4:
            pk4 = [PSx(ph4, f"pk4_{i}", [128, 512]) for i in range(8)]
            b_pk4 = [B_() for _ in range(8)]
            attn4 = SBx(ph4, "attn4", [128, TQG * D])
            b_attn4 = B_()
            wo_f = [SBx(ph4, f"wo_f{i}", [128, DQW]) for i in range(2)]
            wo_h = [SBx(ph4, f"wo_h{i}", [128, DQW], BF16) for i in range(2)]
            wo_l = [SBx(ph4, f"wo_l{i}", [128, DQW], BF16) for i in range(2)]
            b_wo_f = [B_(), B_()]
            b_wo_h = [B_(), B_()]
            b_wo_l = [B_(), B_()]
            ofm_h = [SBx(ph4, f"ofm_h{i}", [128, TQG * 128], BF16) for i in range(2)]
            ofm_l = [SBx(ph4, f"ofm_l{i}", [128, TQG * 128], BF16) for i in range(2)]
            b_ofm_h = [B_(), B_()]
            b_ofm_l = [B_(), B_()]
            x2b = SBx(ph4, "x2b", [128, D])
            xh2 = SBx(ph4, "xh2", [128, D])
            ffnb = SBx(ph4, "ffnb", [128, D])
            b_x2, b_xh2, b_ffn = B_(), B_(), B_()
            xfm2_h = SBx(ph4, "xfm2_h", [128, D], BF16)
            xfm2_l = SBx(ph4, "xfm2_l", [128, D], BF16)
            b_xfm2_h, b_xfm2_l = B_(), B_()
            xb4 = SBx(ph4, "xb4", [128, D])
            b_xb4 = B_()
            eb_h = SBx(ph4, "eb_h", [E, D], BF16)
            eb_l = SBx(ph4, "eb_l", [E, D], BF16)
            b_ebf, b_eb_h, b_eb_l = B_(), B_(), B_()
            rw_h = SBx(ph4, "rw_h", [128, DC * E], BF16)
            rw_l = SBx(ph4, "rw_l", [128, DC * E], BF16)
            b_rwf, b_rwh, b_rwl = B_(), B_(), B_()
            rb_f = SBx(ph4, "rb_f", [1, E])
            rb_h = SBx(ph4, "rb_h", [1, E], BF16)
            rb_l = SBx(ph4, "rb_l", [1, E], BF16)
            b_rbf, b_rbh, b_rbl = B_(), B_(), B_()
            pex = SBx(ph4, "pex", [128, E])
            pnr = SBx(ph4, "pnr", [128, E])
            wdn = SBx(ph4, "wdn", [128, E])
            top8 = SBx(ph4, "top8", [128, 8])
            b_pex, b_pnr, b_wdn, b_top8 = B_(), B_(), B_(), B_()
            wd_h = SBx(ph4, "wd_h", [128, E], BF16)
            wd_l = SBx(ph4, "wd_l", [128, E], BF16)
            b_wdh, b_wdl = B_(), B_()
            wdt_h = SBx(ph4, "wdt_h", [64, 128], BF16)
            wdt_l = SBx(ph4, "wdt_l", [64, 128], BF16)
            b_wdth, b_wdtl = B_(), B_()
            st4 = SBx(ph4, "st4", [128, 8 * 6])
            mv4 = SBx(ph4, "mv4", [128, 8])
            b_st4, b_mv4 = B_(), B_()

            for c0 in range(0, D, DQW):
                le = pb.op("sync", "dma_start", out=wo_f[0][0:E, 0:DQW],
                           in_=EB[:, c0:c0 + DQW], inc=True, bufw=(b_wo_f[0],))
                evac_hilo(wo_f[0][0:E, 0:DQW], eb_h[0:E, c0:c0 + DQW],
                          eb_l[0:E, c0:c0 + DQW], (b_wo_f[0],), b_eb_h, b_eb_l,
                          deps=(le,))
            rwt_v = RWT.ap().rearrange("(c p) w -> p c w", p=128)
            CPG = max(1, DQW // E)
            for cg in range(0, DC, CPG):
                ng = min(CPG, DC - cg)
                lr = pb.op("sync", "dma_start",
                           out=bass.AP(wo_f[0], 0, [[DQW, 128], [E, ng], [1, E]]),
                           in_=rwt_v[:, cg:cg + ng, :],
                           inc=True, bufw=(b_wo_f[0],))
                evac_hilo(wo_f[0][:, 0:ng * E], rw_h[:, cg * E:(cg + ng) * E],
                          rw_l[:, cg * E:(cg + ng) * E], (b_wo_f[0],),
                          b_rwh, b_rwl, deps=(lr,))
            lb = pb.op("sync", "dma_start", out=rb_f[:], in_=RB[:], inc=True,
                       bufw=(b_rbf,))
            evac_hilo(rb_f[:], rb_h[:], rb_l[:], (b_rbf,), b_rbh, b_rbl,
                      deps=(lb,))

            def ln4(src_ap_fn, ncol, src_bufs, eps, deps, mcol):
                nch = (ncol + 511) // 512
                sts = None
                for ci in range(nch):
                    w = min(512, ncol - ci * 512)
                    sts = pb.op("vector", "bn_stats", st4[:, ci * 6:(ci + 1) * 6],
                                src_ap_fn(ci * 512, w), deps=tuple(deps),
                                bufw=(b_st4,), bufr=src_bufs)
                sa = pb.op("vector", "bn_aggr", mv4[:, mcol:mcol + 2],
                           st4[:, 0:6 * nch], deps=(sts,), bufw=(b_mv4,),
                           bufr=(b_st4,))
                s0 = pb.op("vector", "tensor_scalar_add",
                           mv4[:, mcol + 3:mcol + 4],
                           mv4[:, mcol + 1:mcol + 2], float(eps), deps=(sa,),
                           bufw=(b_mv4,), bufr=(b_mv4,))
                s1 = pb.op("scalar", "activation", mv4[:, mcol + 3:mcol + 4],
                           mv4[:, mcol + 3:mcol + 4], AF.Ln,
                           scale=1.0, deps=(s0,), bufw=(b_mv4,), bufr=(b_mv4,))
                s2 = pb.op("scalar", "activation", mv4[:, mcol + 3:mcol + 4],
                           mv4[:, mcol + 3:mcol + 4], AF.Exp,
                           scale=-0.5, deps=(s1,), bufw=(b_mv4,), bufr=(b_mv4,))
                return s2

            NDT = DQW // 512
            for tq in range(OWNT // TQG if phases >= 5 else 0):
                for dq in range(D // DQW):
                    for hv in range(H):
                        wb = hv % 2
                        lw = pb.op("sync", "dma_start", out=wo_f[wb][:],
                                   in_=WOT[hv * 128:(hv + 1) * 128,
                                           dq * DQW:(dq + 1) * DQW],
                                   inc=True, bufw=(b_wo_f[wb],))
                        wh, wl = evac_hilo(wo_f[wb][:], wo_h[wb][:], wo_l[wb][:],
                                           (b_wo_f[wb],), b_wo_h[wb], b_wo_l[wb],
                                           deps=(lw,))
                        lo1 = pb.op("scalar", "dma_start",
                                    out=ofm_h[wb].ap().rearrange(
                                        "p (t w) -> p t w", t=TQG),
                                    in_=osp_hi.ap()[hv, tq * TQG:(tq + 1) * TQG]
                                    .rearrange("t p w -> p t w"),
                                    inc=True, bufw=(b_ofm_h[wb],), bufr=(b_osp,))
                        lo2 = pb.op("scalar", "dma_start",
                                    out=ofm_l[wb].ap().rearrange(
                                        "p (t w) -> p t w", t=TQG),
                                    in_=osp_lo.ap()[hv, tq * TQG:(tq + 1) * TQG]
                                    .rearrange("t p w -> p t w"),
                                    inc=True, bufw=(b_ofm_l[wb],), bufr=(b_osp,))
                        for t4 in range(TQG):
                            for ndt in range(NDT):
                                bk = t4 * NDT + ndt
                                first = hv == 0
                                last = hv == H - 1
                                pb.op("tensor", "matmul", pk4[bk][:, 0:512],
                                      ofm_h[wb][:, t4 * 128:(t4 + 1) * 128],
                                      wo_h[wb][:, ndt * 512:(ndt + 1) * 512],
                                      start=first, stop=False,
                                      deps=(wh, wl, lo1, lo2),
                                      bufwar=(b_pk4[bk],) if first else ())
                                pb.op("tensor", "matmul", pk4[bk][:, 0:512],
                                      ofm_h[wb][:, t4 * 128:(t4 + 1) * 128],
                                      wo_l[wb][:, ndt * 512:(ndt + 1) * 512],
                                      start=False, stop=False)
                                lastb = ndt == NDT - 1 and t4 == TQG - 1
                                pb.op("tensor", "matmul", pk4[bk][:, 0:512],
                                      ofm_l[wb][:, t4 * 128:(t4 + 1) * 128],
                                      wo_h[wb][:, ndt * 512:(ndt + 1) * 512],
                                      start=False, stop=last, inc=lastb or last,
                                      bufw=(b_pk4[bk],) if last else (),
                                      bufr=(b_ofm_h[wb], b_ofm_l[wb],
                                            b_wo_h[wb], b_wo_l[wb])
                                      if lastb else ())
                    for t4 in range(TQG):
                        for ndt in range(NDT):
                            bk = t4 * NDT + ndt
                            pb.op("scalar", "copy",
                                  attn4[:, t4 * D + dq * DQW + ndt * 512:
                                        t4 * D + dq * DQW + (ndt + 1) * 512],
                                  pk4[bk][:, 0:512], deps=(b_pk4[bk].writer,),
                                  bufw=(b_attn4,), bufr=(b_pk4[bk],))
                for t4 in range(TQG):
                    towm = tq * TQG + t4

                    def att(o, w, _t=t4):
                        return attn4[:, _t * D + o:_t * D + o + w]

                    s2 = ln4(att, D, (b_attn4,), EPS_LN, (b_attn4.writer,), 0)
                    lx = pb.op("sync", "dma_start", out=xb4[:],
                               in_=XK[(7 * TPB + towm) * 128:
                                      (7 * TPB + towm + 1) * 128, :],
                               inc=True, bufw=(b_xb4,))
                    t1 = pb.op("vector", "tensor_scalar", x2b[:], att(0, D),
                               mv4[:, 0:1], mv4[:, 3:4], op0=ALU.subtract,
                               op1=ALU.mult, deps=(s2,),
                               bufw=(b_x2,), bufr=(b_attn4, b_mv4))
                    t2 = pb.op("vector", "tensor_add", x2b[:], x2b[:], xb4[:],
                               deps=(t1, lx), bufw=(b_x2,), bufr=(b_xb4,))
                    s3 = ln4(lambda o, w: x2b[:, o:o + w], D, (b_x2,),
                             EPS_LN, (t2,), 4)
                    t3 = pb.op("vector", "tensor_scalar", xh2[:], x2b[:],
                               mv4[:, 4:5], mv4[:, 7:8], op0=ALU.subtract,
                               op1=ALU.mult, deps=(s3,),
                               bufw=(b_xh2,), bufr=(b_x2, b_mv4))
                    for dc in range(DC):
                        pt2 = 4 + dc % 2
                        stx = pb.op("tensor", "transpose", pk4[pt2][:, 0:128],
                                    xh2[:, dc * 128:(dc + 1) * 128], id32[:],
                                    deps=(t3,), inc=True,
                                    bufw=(b_pk4[pt2],),
                                    bufr=(b_xh2, b_consts))
                        fh, fl = evac_hilo(pk4[pt2][:, 0:128],
                                           xfm2_h[:, dc * 128:(dc + 1) * 128],
                                           xfm2_l[:, dc * 128:(dc + 1) * 128],
                                           (b_pk4[pt2],), b_xfm2_h, b_xfm2_l,
                                           deps=(stx,))
                        first = dc == 0
                        pb.op("tensor", "matmul", pk4[0][:, 0:E],
                              xfm2_h[:, dc * 128:(dc + 1) * 128],
                              rw_h[:, dc * E:(dc + 1) * E], start=first,
                              stop=False, deps=(fh, fl, b_rwh.writer,
                                                b_rwl.writer),
                              bufwar=(b_pk4[0],) if first else ())
                        pb.op("tensor", "matmul", pk4[0][:, 0:E],
                              xfm2_h[:, dc * 128:(dc + 1) * 128],
                              rw_l[:, dc * E:(dc + 1) * E], start=False,
                              stop=False)
                        pb.op("tensor", "matmul", pk4[0][:, 0:E],
                              xfm2_l[:, dc * 128:(dc + 1) * 128],
                              rw_h[:, dc * E:(dc + 1) * E], start=False,
                              stop=False)
                    pb.op("tensor", "matmul", pk4[0][:, 0:E], onesb[:],
                          rb_h[:], start=False, stop=False,
                          deps=(b_rbh.writer, b_rbl.writer, b_consts.writer))
                    lmm = pb.op("tensor", "matmul", pk4[0][:, 0:E], onesb[:],
                                rb_l[:], start=False, stop=True, inc=True,
                                bufw=(b_pk4[0],),
                                bufr=(b_xfm2_h, b_xfm2_l, b_rwh, b_rwl,
                                      b_rbh, b_rbl))
                    nm = pb.op("vector", "reduce_max", mv4[:, 0:1],
                               pk4[0][:, 0:E], axis=AX.X, negate=True,
                               deps=(lmm,), bufw=(b_mv4,), bufr=(b_pk4[0],))
                    ex = pb.op("scalar", "activation", pex[:], pk4[0][:, 0:E],
                               AF.Exp, bias=mv4[:, 0:1], scale=1.0, deps=(nm,),
                               bufw=(b_pex,), bufr=(b_pk4[0], b_mv4))
                    sm = pb.op("vector", "reduce_sum", mv4[:, 1:2], pex[:],
                               axis=AX.X, deps=(ex,), bufw=(b_mv4,),
                               bufr=(b_pex,))
                    iv = pb.op("vector", "reciprocal", mv4[:, 2:3], mv4[:, 1:2],
                               deps=(sm,), bufw=(b_mv4,))
                    pn = pb.op("scalar", "mul", pnr[:], pex[:], mv4[:, 2:3],
                               deps=(iv, ex), bufw=(b_pnr,),
                               bufr=(b_pex, b_mv4))
                    t8 = pb.op("vector", "max", top8[:], pnr[:], deps=(pn,),
                               bufw=(b_top8,), bufr=(b_pnr,))
                    msk = pb.op("vector", "tensor_single_scalar", wdn[:], pnr[:],
                                top8[:, 7:8], op=ALU.is_ge, deps=(t8,),
                                bufw=(b_wdn,), bufr=(b_pnr, b_top8))
                    wdm = pb.op("vector", "tensor_mul", wdn[:], wdn[:], pnr[:],
                                deps=(msk,), bufw=(b_wdn,), bufr=(b_pnr,))
                    ws = pb.op("vector", "reduce_sum", mv4[:, 3:4], wdn[:],
                               axis=AX.X, deps=(wdm,), bufw=(b_mv4,),
                               bufr=(b_wdn,))
                    sw = pb.op("tensor", "transpose", pk4[1][0:E, 0:128],
                               wdn[:], id32[:], deps=(wdm,), inc=True,
                               bufw=(b_pk4[1],), bufr=(b_wdn, b_consts))
                    ew1, ew2 = evac_hilo(pk4[1][0:E, 0:128],
                                         wdt_h[0:E, :], wdt_l[0:E, :],
                                         (b_pk4[1],), b_wdth, b_wdtl,
                                         deps=(sw,))
                    for ndt in range(D // 512):
                        bk = 2 + ndt % 2
                        pb.op("tensor", "matmul", pk4[bk][:, 0:512],
                              wdt_h[0:E, :], eb_h[0:E, ndt * 512:(ndt + 1) * 512],
                              start=True, stop=False,
                              deps=(ew1, ew2, b_eb_h.writer, b_eb_l.writer),
                              bufwar=(b_pk4[bk],))
                        pb.op("tensor", "matmul", pk4[bk][:, 0:512],
                              wdt_h[0:E, :], eb_l[0:E, ndt * 512:(ndt + 1) * 512],
                              start=False, stop=False)
                        mmb = pb.op("tensor", "matmul", pk4[bk][:, 0:512],
                                    wdt_l[0:E, :],
                                    eb_h[0:E, ndt * 512:(ndt + 1) * 512],
                                    start=False, stop=True, inc=True,
                                    bufw=(b_pk4[bk],),
                                    bufr=(b_wdth, b_wdtl, b_eb_h, b_eb_l))
                        pb.op("vector", "scalar_tensor_tensor",
                              ffnb[:, ndt * 512:(ndt + 1) * 512],
                              xh2[:, ndt * 512:(ndt + 1) * 512], mv4[:, 3:4],
                              pk4[bk][:, 0:512], op0=ALU.mult, op1=ALU.add,
                              deps=(mmb, ws), bufw=(b_ffn,),
                              bufr=(b_xh2, b_mv4, b_pk4[bk]))
                    s4 = ln4(lambda o, w: ffnb[:, o:o + w], D, (b_ffn,),
                             EPS_LN, (b_ffn.writer,), 4)
                    y1 = pb.op("vector", "tensor_scalar", xh2[:], ffnb[:],
                               mv4[:, 4:5], mv4[:, 7:8], op0=ALU.subtract,
                               op1=ALU.mult, deps=(s4,),
                               bufw=(b_xh2,), bufr=(b_ffn, b_mv4))
                    y2 = pb.op("vector", "tensor_add", xh2[:], xh2[:], x2b[:],
                               deps=(y1,), bufw=(b_xh2,), bufr=(b_x2,))
                    pb.op("scalar", "dma_start",
                          out=Y[towm * 128:(towm + 1) * 128, :],
                          in_=xh2[:], deps=(y2,), inc=True,
                          bufw=(b_y,), bufr=(b_xh2,))
            if phases < 5:
                z = pb.op("vector", "memset", xh2[:], 0.0, bufw=(b_xh2,))
                for towm in range(OWNT):
                    pb.op("scalar", "dma_start",
                          out=Y[towm * 128:(towm + 1) * 128, :], in_=xh2[:],
                          deps=(z,), inc=True, bufw=(b_y,), bufr=(b_xh2,))
            for sk, v in pb.cnt.items():
                if v > 0 and (sk.startswith("scalarq") or sk.startswith("gpsimd")
                              or sk.startswith("sync")):
                    pb._wait("gpsimd", sk, v)

        for e, v in pb.cnt.items():
            assert v < 60000, (e, v)

        blk = stack.enter_context(nc.Block())
        pb.realize(blk)

    nc.finalize()
    return nc



def make_shared_inputs(ins, cfg):
    g = derived(cfg)
    RC, KC, E = g["RC"], g["KC"], cfg["E"]
    cos, sin = yarn_cos_sin(cfg["S"], dim=64)
    tri = np.zeros((128, 128), np.float32)
    for r in range(128):
        tri[r, r + 1:] = NEG
    shared = dict(
        wqat=np.ascontiguousarray(np.asarray(ins["L0_attn_wq_a"], np.float32).T),
        wqbt=np.ascontiguousarray(np.asarray(ins["L0_attn_wq_b"], np.float32).T),
        wkvat=np.ascontiguousarray(np.asarray(ins["L0_attn_wkv_a"], np.float32).T),
        wkvbt=np.ascontiguousarray(np.asarray(ins["L0_attn_wkv_b"], np.float32).T),
        wot=np.ascontiguousarray(np.asarray(ins["L0_attn_wo"], np.float32).T),
        rwt=np.ascontiguousarray(np.asarray(ins["L0_ffn_router_weight"], np.float32).T),
        rb=np.ascontiguousarray(np.asarray(ins["L0_ffn_router_bias"], np.float32)
                                .reshape(1, E)),
        eb=np.ascontiguousarray(np.asarray(ins["L0_ffn_expert_bias"], np.float32)),
        qaw=np.ascontiguousarray(np.asarray(ins["L0_attn_q_a_ln_w"], np.float32)
                                 .reshape(RC, 128)),
        kvw=np.ascontiguousarray(np.asarray(ins["L0_attn_kv_a_ln_w"], np.float32)
                                 .reshape(KC, 128)),
        tri=tri,
        id32=np.eye(128, dtype=np.float32),
        idbf=np.eye(128, dtype=BF16NP),
        onesb=np.ones((1, 128), dtype=BF16NP),
        _cos32=np.ascontiguousarray(cos[:, :32]),
        _sin32=np.ascontiguousarray(sin[:, :32]),
    )
    return shared


def make_core_inputs(c, ins, cfg, shared):
    g = derived(cfg)
    BS = g["BS"]
    x = np.asarray(ins["L0_attn_x"], np.float32)
    blocks = core_blocks(c)
    xk = np.concatenate([x[b, j * BS:(j + 1) * BS] for b, j in blocks], axis=0)
    pos = np.concatenate([np.arange(j * BS, (j + 1) * BS) for b, j in blocks])
    sela = np.zeros((128, 9), np.float32)
    selb = np.zeros((128, 9), np.float32)
    for j in range(7):
        if j < c:
            selb[:, j] = NEG
        else:
            sela[:, j] = NEG
    sela[:, 8] = NEG
    selb[:, 7] = NEG
    H = cfg["H"]
    qsel = np.zeros((128, H * 7), np.int32)
    for h in range(H):
        for j in range(7):
            side = 0 if j < c else 1
            qsel[:, h * 7 + j] = h * 256 + side * 128 + np.arange(128)
    d = {k: v for k, v in shared.items() if not k.startswith("_")}
    d.update(
        xk=np.ascontiguousarray(xk, np.float32),
        cosk=np.ascontiguousarray(shared["_cos32"][pos]),
        sink=np.ascontiguousarray(shared["_sin32"][pos]),
        sela=sela, selb=selb, qsel=qsel,
    )
    return d


_PROG_CACHE = {}


def get_program():
    if "nc" not in _PROG_CACHE:
        _PROG_CACHE["nc"] = build_program(REAL)
    return _PROG_CACHE["nc"]


def kernel(**inputs):
    cfg = REAL
    g = derived(cfg)
    nc = get_program()
    shared = make_shared_inputs(inputs, cfg)
    in_maps = [make_core_inputs(c, inputs, cfg, shared) for c in range(8)]
    res = run_bass_kernel_spmd(nc, in_maps, core_ids=list(range(8)))
    BS = g["BS"]
    y = np.empty((2, cfg["S"], cfg["D"]), np.float32)
    for c in range(8):
        yc = np.asarray(res.results[c]["y"])
        y[0, c * BS:(c + 1) * BS] = yc[0:BS]
        y[1, (7 - c) * BS:(8 - c) * BS] = yc[BS:2 * BS]
    return y


# version 05
# speedup vs baseline: 1.1302x; 1.1302x over previous
import math
import numpy as np
import ml_dtypes
from contextlib import ExitStack

import concourse.bass as bass
import concourse.mybir as mybir
from concourse.bass import IndirectOffsetOnAxis
from concourse.bass_utils import run_bass_kernel_spmd

F32 = mybir.dt.float32
BF16 = mybir.dt.bfloat16
I32 = mybir.dt.int32
AF = mybir.ActivationFunctionType
ALU = mybir.AluOpType
AX = mybir.AxisListType
BF16NP = ml_dtypes.bfloat16

NEG = -1.0e30
EPS_RMS = 1.0e-6
EPS_LN = 1.0e-5

REAL = dict(D=4096, RQ=1024, KVR=256, H=32, S=4096, E=64)


def derived(cfg):
    g = dict(cfg)
    g["VD"] = 128
    g["KVD"] = 192
    g["NB"] = 8
    g["BS"] = cfg["S"] // 8
    assert g["BS"] % 128 == 0 and g["BS"] <= 512
    g["TPB"] = g["BS"] // 128
    g["NKEY"] = 9 * g["BS"]
    g["NKT"] = g["NKEY"] // 128
    g["OWN"] = 2 * g["BS"]
    g["OWNT"] = g["OWN"] // 128
    g["DC"] = cfg["D"] // 128
    g["RC"] = cfg["RQ"] // 128
    g["KC"] = cfg["KVR"] // 128
    g["WKA"] = cfg["KVR"] + 64
    g["MSC"] = 0.1 * math.log(128.0) + 1.0
    g["SCALE"] = (128.0 ** -0.5) * g["MSC"] * g["MSC"]
    g["ETW"] = min(512, cfg["H"] * 128)
    g["NET"] = cfg["H"] * 128 // g["ETW"]
    g["HPE"] = g["ETW"] // 128
    g["G1B"] = min(2, g["OWNT"])
    g["TQG"] = min(4, g["OWNT"])
    g["DQW"] = 1024 if cfg["D"] % 1024 == 0 else cfg["D"]
    assert g["TQG"] * (g["DQW"] // 512) <= 8
    return g


def yarn_cos_sin(seq_len, dim=64, base=10000.0, factor=128.0, beta_fast=32.0,
                 beta_slow=1.0, orig_max=8192, mscale=1.0, mscale_all_dim=1.0):
    def corr_dim(r):
        return dim * math.log(orig_max / (r * 2 * math.pi)) / (2 * math.log(base))
    low = max(math.floor(corr_dim(beta_fast)), 0)
    high = min(math.ceil(corr_dim(beta_slow)), dim - 1)
    hi = high + 0.001 if low == high else float(high)
    ramp = np.clip((np.arange(dim // 2, dtype=np.float32) - low) / (hi - low), 0.0, 1.0)
    inv_freq_mask = 1.0 - ramp
    freq_extra = 1.0 / base ** (np.arange(0, dim, 2, dtype=np.float32) / dim)
    freq_inter = freq_extra / factor
    inv_freq = freq_inter * (1.0 - inv_freq_mask) + freq_extra * inv_freq_mask
    t = np.arange(seq_len, dtype=np.float32)
    freqs = np.outer(t, inv_freq)
    emb = np.concatenate([freqs, freqs], axis=-1)

    def gm(s, m):
        return 1.0 if s <= 1 else 0.1 * m * math.log(s) + 1.0
    ms = gm(factor, mscale) / gm(factor, mscale_all_dim)
    return ((np.cos(emb) * np.float32(ms)).astype(np.float32),
            (np.sin(emb) * np.float32(ms)).astype(np.float32))


def core_blocks(c):
    return ([(0, j) for j in range(c)] + [(1, j) for j in range(7 - c)]
            + [(0, c), (1, 7 - c)])



class Buf:
    __slots__ = ("writers", "readers")

    def __init__(self):
        self.writers = []
        self.readers = []

    @property
    def writer(self):
        return self.writers[-1] if self.writers else None


class PB:
    ENGS = ("sync", "tensor", "vector", "scalar", "gpsimd")
    QPOOL = {"sync": 8, "scalarq": 8, "gpsimd": 4}

    def __init__(self, nc, stack):
        self.nc = nc
        self.q = {e: [] for e in self.ENGS}
        sems = ["tensor", "vector", "scalar"]
        for qn, k in self.QPOOL.items():
            sems += [f"{qn}{i}" for i in range(k)]
        self.sem = {s: stack.enter_context(nc.semaphore("sem_" + s)) for s in sems}
        self.cnt = {s: 0 for s in sems}
        self.qidx = {qn: 0 for qn in self.QPOOL}
        self.opidx = {e: 0 for e in self.ENGS}
        self.last_drain = {e: 0 for e in self.ENGS}
        self.hiw = {e: {} for e in self.ENGS}

    def _wait(self, eng, peng, val):
        if self.hiw[eng].get(peng, 0) >= val:
            return
        self.hiw[eng][peng] = val
        sem = self.sem[peng]
        self.opidx[eng] += 1
        self.q[eng].append(lambda e, sem=sem, val=val: e.wait_ge(sem, val))

    def _drain(self, eng):
        self.opidx[eng] += 1
        self.q[eng].append(lambda e: e.drain())
        self.last_drain[eng] = self.opidx[eng]

    def dep(self, eng, stamp):
        if stamp is None:
            return
        kind, peng, oidx, cnt = stamp
        if kind == "c" and peng == eng:
            if oidx > self.last_drain[eng]:
                self._drain(eng)
        else:
            assert cnt is not None, f"cross-engine dep {peng}->{eng} without inc"
            self._wait(eng, peng, cnt)

    def op(self, eng, fname, *args, deps=(), inc=None, bufw=(), bufr=(),
           bufd=(), bufwar=(), **kw):
        for st in deps:
            self.dep(eng, st)
        for b in tuple(bufw) + tuple(bufwar):
            if b.readers:
                for r in b.readers:
                    self.dep(eng, r)
                for w in b.writers:
                    self.dep(eng, w)
                b.readers = []
                b.writers = []
        for b in bufr:
            for w in b.writers:
                self.dep(eng, w)
        for b in bufd:
            for w in b.writers:
                self.dep(eng, w)
        if inc is None:
            inc = eng in ("vector", "scalar")
        self.opidx[eng] += 1
        oidx = self.opidx[eng]
        cnt = None
        isdma = "dma" in fname
        semkey = eng
        if isdma:
            qn = {"scalar": "scalarq", "sync": "sync",
                  "gpsimd": "gpsimd"}[eng]
            k = self.QPOOL[qn]
            semkey = f"{qn}{self.qidx[qn] % k}"
            self.qidx[qn] += 1
            if self.cnt[semkey] > 0:
                self._wait(eng, semkey, self.cnt[semkey])
            assert inc, "every DMA must inc"
        if inc:
            n = 16 if isdma else 1
            self.cnt[semkey] += n
            cnt = self.cnt[semkey]
            sem = self.sem[semkey]
            self.q[eng].append(
                lambda e, a=args, k=kw, f=fname, sem=sem, n=n:
                getattr(e, f)(*a, **k).then_inc(sem, n))
        else:
            self.q[eng].append(
                lambda e, a=args, k=kw, f=fname: getattr(e, f)(*a, **k))
        st = ("d" if isdma else "c", semkey, oidx, cnt)
        for b in bufw:
            b.writers.append(st)
        for b in bufr:
            b.readers.append(st)
        return st

    def barrier(self):
        for eng in self.ENGS:
            for sk, v in self.cnt.items():
                if sk != eng and v > 0:
                    self._wait(eng, sk, v)
            self._drain(eng)

    def realize(self, blk):
        for name in self.ENGS:
            ops = self.q[name]

            def runner(e, ops=ops):
                for f in ops:
                    f(e)
            getattr(blk, name)(runner)



def build_program(cfg, phases=5, sub=9):
    g = derived(cfg)
    D, RQ, KVR, H, E, S = cfg["D"], cfg["RQ"], cfg["KVR"], cfg["H"], cfg["E"], cfg["S"]
    VD, KVD = g["VD"], g["KVD"]
    BS, TPB, NKEY, NKT = g["BS"], g["TPB"], g["NKEY"], g["NKT"]
    OWN, OWNT = g["OWN"], g["OWNT"]
    DC, RC, KC, WKA = g["DC"], g["RC"], g["KC"], g["WKA"]
    SCALE = g["SCALE"]
    ETW, NET, HPE = g["ETW"], g["NET"], g["HPE"]
    G1B, TQG, DQW = g["G1B"], g["TQG"], g["DQW"]
    VST = 132
    OST = 132

    nc = bass.Bass("TRN2", target_bir_lowering=False, debug=False)

    def din(name, shape, dt=F32):
        return nc.dram_tensor(name, shape, dt, kind="ExternalInput")

    XK = din("xk", [NKEY, D])
    WQAT = din("wqat", [D, RQ])
    WQBT = din("wqbt", [RQ, H * 128])
    WKVAT = din("wkvat", [D, WKA])
    WKVBT = din("wkvbt", [KVR, H * KVD])
    WOT = din("wot", [H * VD, D])
    RWT = din("rwt", [D, E])
    RB = din("rb", [1, E])
    EB = din("eb", [E, D])
    QAW = din("qaw", [RC, 128])
    KVW = din("kvw", [KC, 128])
    COSK = din("cosk", [NKEY, 32])
    SINK = din("sink", [NKEY, 32])
    TRI = din("tri", [128, 128])
    ID32 = din("id32", [128, 128])
    IDBF = din("idbf", [128, 128], BF16)
    SELA = din("sela", [128, 9])
    SELB = din("selb", [128, 9])
    QSEL = din("qsel", [128, H * 7], I32)
    ONESB = din("onesb", [1, 128], BF16)

    Y = nc.dram_tensor("y", [OWN, D], F32, kind="ExternalOutput")

    qsp_hi = nc.dram_tensor("qsp_hi", [H, 256, TPB * 128], BF16, kind="Internal")
    qsp_lo = nc.dram_tensor("qsp_lo", [H, 256, TPB * 128], BF16, kind="Internal")
    osp_hi = nc.dram_tensor("osp_hi", [H, OWNT, 128, 128], BF16, kind="Internal")
    osp_lo = nc.dram_tensor("osp_lo", [H, OWNT, 128, 128], BF16, kind="Internal")
    wkva_hi = nc.dram_tensor("wkva_hi", [D, WKA], BF16, kind="Internal")
    wkva_lo = nc.dram_tensor("wkva_lo", [D, WKA], BF16, kind="Internal")
    wkvb_hi = nc.dram_tensor("wkvb_hi", [KVR, H * KVD], BF16, kind="Internal")
    wkvb_lo = nc.dram_tensor("wkvb_lo", [KVR, H * KVD], BF16, kind="Internal")

    stack = ExitStack()
    with stack:
        pb = PB(nc, stack)
        B_ = Buf

        def SBx(st, name, shape, dt=F32):
            return st.enter_context(nc.sbuf_tensor(name, shape, dt))

        def PSx(st, name, shape, dt=F32):
            return st.enter_context(nc.psum_tensor(name, shape, dt))

        cosk = SBx(stack, "cosk_sb", [128, NKT * 32])
        sink = SBx(stack, "sink_sb", [128, NKT * 32])
        tri = SBx(stack, "tri_sb", [128, 128])
        id32 = SBx(stack, "id32_sb", [128, 128])
        idbf = SBx(stack, "idbf_sb", [128, 128], BF16)
        sela = SBx(stack, "sela_sb", [128, 9])
        selb = SBx(stack, "selb_sb", [128, 9])
        qsel = SBx(stack, "qsel_sb", [128, H * 7], I32)
        onesb = SBx(stack, "onesb_sb", [1, 128], BF16)
        qaw = SBx(stack, "qaw_sb", [128, RC])
        kvw = SBx(stack, "kvw_sb", [128, KC])
        b_consts = B_()
        b_qsp, b_osp = B_(), B_()
        b_wkva, b_wkvb = B_(), B_()
        b_y = B_()

        def ldc(out_ap, in_ap):
            return pb.op("sync", "dma_start", out=out_ap, in_=in_ap,
                         inc=True, bufw=(b_consts,))

        ckv_view = COSK.ap().rearrange("(n p) w -> n p w", p=128)
        skv_view = SINK.ap().rearrange("(n p) w -> n p w", p=128)
        for n in range(NKT):
            ldc(cosk[:, n * 32:(n + 1) * 32], ckv_view[n])
            ldc(sink[:, n * 32:(n + 1) * 32], skv_view[n])
        ldc(tri[:], TRI[:])
        ldc(id32[:], ID32[:])
        ldc(idbf[:], IDBF[:])
        ldc(sela[:], SELA[:])
        ldc(selb[:], SELB[:])
        ldc(qsel[:], QSEL[:])
        ldc(onesb[:], ONESB[:])
        pb.op("sync", "dma_start", out=qaw[:],
              in_=QAW.ap().rearrange("a b -> b a"), inc=True,
              bufw=(b_consts,), allow_slow_non_contiguous=True)
        pb.op("sync", "dma_start", out=kvw[:],
              in_=KVW.ap().rearrange("a b -> b a"), inc=True,
              bufw=(b_consts,), allow_slow_non_contiguous=True)

        def evac_hilo(src_ap, hi_ap, lo_ap, src_bufs, hi_buf, lo_buf, deps=(),
                      scale=1.0):
            plain = isinstance(scale, float) and scale == 1.0
            if plain:
                h = pb.op("scalar", "copy", hi_ap, src_ap, deps=deps,
                          bufw=(hi_buf,), bufr=src_bufs)
            else:
                h = pb.op("scalar", "activation", hi_ap, src_ap, AF.Copy, scale=scale, deps=deps,
                          bufw=(hi_buf,), bufr=src_bufs)
            l = pb.op("vector", "scalar_tensor_tensor", lo_ap, src_ap,
                      1.0 if plain else scale, hi_ap,
                      op0=ALU.mult, op1=ALU.subtract, deps=deps + (h,),
                      bufw=(lo_buf,), bufr=src_bufs + (hi_buf,))
            return h, l

        with ExitStack() as ph0:
            wsf = SBx(ph0, "wsf", [128, 2048])
            wsh = SBx(ph0, "wsh", [128, 2048], BF16)
            wsl = SBx(ph0, "wsl", [128, 2048], BF16)
            b_wsf, b_wsh, b_wsl = B_(), B_(), B_()

            def presplit(src, dst_hi, dst_lo, rows, cols, bdst):
                assert rows % 128 == 0
                for rchunk in range(rows // 128):
                    for c0 in range(0, cols, 2048):
                        w = min(2048, cols - c0)
                        s = pb.op("sync", "dma_start", out=wsf[:, 0:w],
                                  in_=src[rchunk * 128:(rchunk + 1) * 128,
                                          c0:c0 + w],
                                  inc=True, bufw=(b_wsf,))
                        h, l = evac_hilo(wsf[:, 0:w], wsh[:, 0:w], wsl[:, 0:w],
                                         (b_wsf,), b_wsh, b_wsl, deps=(s,))
                        pb.op("scalar", "dma_start",
                              out=dst_hi[rchunk * 128:(rchunk + 1) * 128,
                                         c0:c0 + w],
                              in_=wsh[:, 0:w], deps=(h,), inc=True,
                              bufw=(bdst,), bufr=(b_wsh,))
                        pb.op("scalar", "dma_start",
                              out=dst_lo[rchunk * 128:(rchunk + 1) * 128,
                                         c0:c0 + w],
                              in_=wsl[:, 0:w], deps=(l,), inc=True,
                              bufw=(bdst,), bufr=(b_wsl,))

            if phases >= 1:
                presplit(WKVAT, wkva_hi, wkva_lo, D, WKA, b_wkva)
                presplit(WKVBT, wkvb_hi, wkvb_lo, KVR, H * KVD, b_wkvb)
            pb.barrier()

        with ExitStack() as s123:
            cpk_hi = SBx(s123, "cpk_hi", [128, KC * NKEY], BF16)
            cpk_lo = SBx(s123, "cpk_lo", [128, KC * NKEY], BF16)
            kpeT_hi = SBx(s123, "kpeT_hi", [128, NKEY], BF16)
            kpeT_lo = SBx(s123, "kpeT_lo", [128, NKEY], BF16)
            cq_hi = SBx(s123, "cq_hi", [128, RC * OWN], BF16)
            cq_lo = SBx(s123, "cq_lo", [128, RC * OWN], BF16)
            b_cpk_h, b_cpk_l = B_(), B_()
            b_kpeT_h, b_kpeT_l = B_(), B_()
            b_cq_h, b_cq_l = B_(), B_()
            pbank = [PSx(s123, f"pb{i}", [128, 512]) for i in range(6)]
            b_pb = [B_() for _ in range(6)]
            pbb = [PSx(s123, f"pbb{i}", [128, 512]) for i in range(2)]
            b_pbb = [B_() for _ in range(2)]

            statb2 = [SBx(s123, f"statb{i}", [128, 8 * 6]) for i in range(2)]
            mvv2 = [SBx(s123, f"mvv{i}", [128, 8]) for i in range(2)]
            b_stat2 = [B_(), B_()]
            b_mvv2 = [B_(), B_()]
            def set_stat(par):
                return dict(statb=statb2[par], mvv=mvv2[par],
                            b_stat=b_stat2[par], b_mvv=b_mvv2[par])

            CUR = {}

            def invsigma_ops(meansq_src_col, eps, dst_col, deps):
                mvv, b_mvv = CUR["mvv"], CUR["b_mvv"]
                """mvv[dst] = 1/sqrt(mvv[src]+eps) via exp(-0.5 ln(x+eps))."""
                s0 = pb.op("vector", "tensor_scalar_add",
                           mvv[:, dst_col:dst_col + 1],
                           mvv[:, meansq_src_col:meansq_src_col + 1],
                           float(eps), deps=deps, bufw=(b_mvv,), bufr=(b_mvv,))
                s1 = pb.op("scalar", "activation", mvv[:, dst_col:dst_col + 1],
                           mvv[:, dst_col:dst_col + 1], AF.Ln,
                           scale=1.0, deps=(s0,),
                           bufw=(b_mvv,), bufr=(b_mvv,))
                return pb.op("scalar", "activation", mvv[:, dst_col:dst_col + 1],
                             mvv[:, dst_col:dst_col + 1], AF.Exp,
                             scale=-0.5, deps=(s1,),
                             bufw=(b_mvv,), bufr=(b_mvv,))

            def ln_stats(src_ap_fn, ncol, src_bufs, eps, deps):
                statb, mvv = CUR["statb"], CUR["mvv"]
                b_stat, b_mvv = CUR["b_stat"], CUR["b_mvv"]
                nch = (ncol + 511) // 512
                sts = None
                for ci in range(nch):
                    w = min(512, ncol - ci * 512)
                    sts = pb.op("vector", "bn_stats", statb[:, ci * 6:(ci + 1) * 6],
                                src_ap_fn(ci * 512, w), deps=tuple(deps),
                                bufw=(b_stat,), bufr=src_bufs)
                sa = pb.op("vector", "bn_aggr", mvv[:, 0:2], statb[:, 0:6 * nch],
                           deps=(sts,), bufw=(b_mvv,), bufr=(b_stat,))
                return invsigma_ops(1, eps, 3, (sa,))

            def rms_stats(src_ap_fn, ncol, src_bufs, eps, deps):
                statb, mvv = CUR["statb"], CUR["mvv"]
                b_stat, b_mvv = CUR["b_stat"], CUR["b_mvv"]
                nch = (ncol + 511) // 512
                sts = None
                for ci in range(nch):
                    w = min(512, ncol - ci * 512)
                    sts = pb.op("vector", "bn_stats", statb[:, ci * 6:(ci + 1) * 6],
                                src_ap_fn(ci * 512, w), deps=tuple(deps),
                                bufw=(b_stat,), bufr=src_bufs)
                sa = pb.op("vector", "bn_aggr", mvv[:, 0:2], statb[:, 0:6 * nch],
                           deps=(sts,), bufw=(b_mvv,), bufr=(b_stat,))
                sm = pb.op("vector", "scalar_tensor_tensor", mvv[:, 4:5],
                           mvv[:, 0:1], mvv[:, 0:1], mvv[:, 1:2],
                           op0=ALU.mult, op1=ALU.add, deps=(sa,),
                           bufw=(b_mvv,), bufr=(b_mvv,))
                return invsigma_ops(4, eps, 6, (sm,))

            with ExitStack() as ph1:
                xb = [SBx(ph1, f"xb{i}", [128, D]) for i in range(2)]
                b_xb = [B_(), B_()]
                xhat2 = [SBx(ph1, f"xhat{i}", [128, D]) for i in range(2)]
                b_xhat2 = [B_(), B_()]
                xfm_h = [SBx(ph1, f"xfm_h{i}", [128, 128], BF16) for i in range(4)]
                xfm_l = [SBx(ph1, f"xfm_l{i}", [128, 128], BF16) for i in range(4)]
                b_xfm_h = [B_() for _ in range(4)]
                b_xfm_l = [B_() for _ in range(4)]
                wa_h = [SBx(ph1, f"wa_h{i}", [128, WKA], BF16) for i in range(2)]
                wa_l = [SBx(ph1, f"wa_l{i}", [128, WKA], BF16) for i in range(2)]
                b_wa_h = [B_(), B_()]
                b_wa_l = [B_(), B_()]
                cbuf = SBx(ph1, "cbuf", [128, KVR])
                b_cbuf = B_()
                krop = SBx(ph1, "krop", [128, 64])
                ktmp = SBx(ph1, "ktmp", [128, 64])
                b_krop, b_ktmp = B_(), B_()
                kst_h = SBx(ph1, "kst_h", [128, 128], BF16)
                kst_l = SBx(ph1, "kst_l", [128, 128], BF16)
                b_kst_h, b_kst_l = B_(), B_()

                for kt in range(NKT if phases >= 2 else 0):
                    bx = kt % 2
                    CUR.update(set_stat(bx))
                    statb, mvv = CUR["statb"], CUR["mvv"]
                    b_stat, b_mvv = CUR["b_stat"], CUR["b_mvv"]
                    xhat, b_xhat = xhat2[bx], b_xhat2[bx]
                    lx = pb.op("sync", "dma_start", out=xb[bx][:],
                               in_=XK[kt * 128:(kt + 1) * 128, :],
                               inc=True, bufw=(b_xb[bx],))
                    siv = ln_stats(lambda o, w, _b=xb[bx]: _b[:, o:o + w],
                                   D, (b_xb[bx],), EPS_LN, (lx,))
                    sx = pb.op("vector", "tensor_scalar", xhat[:], xb[bx][:],
                               mvv[:, 0:1], mvv[:, 3:4],
                               op0=ALU.subtract, op1=ALU.mult,
                               deps=(siv,), bufw=(b_xhat,),
                               bufr=(b_xb[bx], b_mvv))
                    pc = kt % 2

                    def mm_ckv(dc, st):
                        pt, eh, el, lwh, lwl = st
                        first = dc == 0
                        last = dc == DC - 1
                        pb.op("tensor", "matmul", pbank[pc][:, 0:WKA],
                              xfm_h[pt][:], wa_h[pt][:], start=first, stop=False,
                              deps=(eh, lwh, lwl),
                              bufwar=(b_pb[pc],) if first else ())
                        pb.op("tensor", "matmul", pbank[pc][:, 0:WKA],
                              xfm_h[pt][:], wa_l[pt][:], start=False, stop=False)
                        pb.op("tensor", "matmul", pbank[pc][:, 0:WKA],
                              xfm_l[pt][:], wa_h[pt][:], start=False, stop=last,
                              deps=(el,), inc=True,
                              bufw=(b_pb[pc],) if last else (),
                              bufr=(b_xfm_h[pt], b_xfm_l[pt], b_wa_h[pt],
                                    b_wa_l[pt]))

                    prev_ckv = None
                    for dc in range(DC):
                        pt = dc % 2
                        lwh = pb.op("sync", "dma_start", out=wa_h[pt][:],
                                    in_=wkva_hi[dc * 128:(dc + 1) * 128, :],
                                    inc=True, bufw=(b_wa_h[pt],), bufr=(b_wkva,))
                        lwl = pb.op("sync", "dma_start", out=wa_l[pt][:],
                                    in_=wkva_lo[dc * 128:(dc + 1) * 128, :],
                                    inc=True, bufw=(b_wa_l[pt],), bufr=(b_wkva,))
                        stp = pb.op("tensor", "transpose", pbank[4 + pt][:, 0:128],
                                    xhat[:, dc * 128:(dc + 1) * 128], id32[:],
                                    deps=(sx,), inc=True,
                                    bufw=(b_pb[4 + pt],), bufr=(b_xhat, b_consts))
                        eh, el = evac_hilo(pbank[4 + pt][:, 0:128],
                                           xfm_h[pt][:], xfm_l[pt][:],
                                           (b_pb[4 + pt],), b_xfm_h[pt],
                                           b_xfm_l[pt], deps=(stp,))
                        if prev_ckv is not None:
                            mm_ckv(dc - 1, prev_ckv)
                        prev_ckv = (pt, eh, el, lwh, lwl)
                    mm_ckv(DC - 1, prev_ckv)
                    mm_done = b_pb[pc].writer
                    sr = rms_stats(lambda o, w, _p=pbank[pc]: _p[:, o:o + w],
                                   KVR, (b_pb[pc],), EPS_RMS, (mm_done,))
                    sc = pb.op("scalar", "activation", cbuf[:], pbank[pc][:, 0:KVR],
                               AF.Copy, scale=mvv[:, 6:7],
                               deps=(sr, mm_done),
                               bufw=(b_cbuf,), bufr=(b_pb[pc], b_mvv))
                    ap_ev = bass.AP(pbank[pc], KVR, [[512, 128], [2, 32]])
                    ap_od = bass.AP(pbank[pc], KVR + 1, [[512, 128], [2, 32]])
                    cos_t = cosk[:, kt * 32:(kt + 1) * 32]
                    sin_t = sink[:, kt * 32:(kt + 1) * 32]
                    r1 = pb.op("vector", "tensor_tensor", krop[:, 0:32], ap_ev,
                               cos_t, op=ALU.mult, deps=(mm_done,),
                               bufw=(b_krop,), bufr=(b_pb[pc], b_consts))
                    r2 = pb.op("vector", "tensor_tensor", krop[:, 32:64], ap_od,
                               cos_t, op=ALU.mult, deps=(mm_done,),
                               bufw=(b_krop,), bufr=(b_pb[pc], b_consts))
                    r3 = pb.op("vector", "tensor_tensor", ktmp[:, 0:32], ap_od,
                               sin_t, op=ALU.mult, deps=(mm_done,),
                               bufw=(b_ktmp,), bufr=(b_pb[pc], b_consts))
                    r4 = pb.op("vector", "tensor_tensor", ktmp[:, 32:64], ap_ev,
                               sin_t, op=ALU.mult, deps=(mm_done,),
                               bufw=(b_ktmp,), bufr=(b_pb[pc], b_consts))
                    r5 = pb.op("vector", "tensor_sub", krop[:, 0:32], krop[:, 0:32],
                               ktmp[:, 0:32], deps=(r1, r3),
                               bufw=(b_krop,), bufr=(b_ktmp,))
                    r6 = pb.op("vector", "tensor_add", krop[:, 32:64], krop[:, 32:64],
                               ktmp[:, 32:64], deps=(r2, r4),
                               bufw=(b_krop,), bufr=(b_ktmp,))
                    for rc in range(KC):
                        pt = rc % 2
                        stc = pb.op("tensor", "transpose", pbank[4 + pt][:, 0:128],
                                    cbuf[:, rc * 128:(rc + 1) * 128], id32[:],
                                    deps=(sc,), inc=True,
                                    bufw=(b_pb[4 + pt],), bufr=(b_cbuf, b_consts))
                        col = rc * NKEY + kt * 128
                        evac_hilo(pbank[4 + pt][:, 0:128],
                                  cpk_hi[:, col:col + 128], cpk_lo[:, col:col + 128],
                                  (b_pb[4 + pt],), b_cpk_h, b_cpk_l, deps=(stc,),
                                  scale=kvw[:, rc:rc + 1])
                    ptk = 4 + kt % 2
                    stk = pb.op("tensor", "transpose", pbank[ptk][0:64, 0:128],
                                krop[:, 0:64], id32[:], deps=(r5, r6), inc=True,
                                bufw=(b_pb[ptk],), bufr=(b_krop, b_consts))
                    kh, kl = evac_hilo(pbank[ptk][0:64, 0:128],
                                       kst_h[0:64, :], kst_l[0:64, :],
                                       (b_pb[ptk],), b_kst_h, b_kst_l,
                                       deps=(stk,))
                    pb.op("scalar", "dma_start",
                          out=kpeT_hi[64:128, kt * 128:(kt + 1) * 128],
                          in_=kst_h[0:64, :], deps=(kh,), inc=True,
                          bufw=(b_kpeT_h,), bufr=(b_kst_h,))
                    pb.op("scalar", "dma_start",
                          out=kpeT_lo[64:128, kt * 128:(kt + 1) * 128],
                          in_=kst_l[0:64, :], deps=(kl,), inc=True,
                          bufw=(b_kpeT_l,), bufr=(b_kst_l,))

                wqa_f = [SBx(ph1, f"wqa_f{i}", [128, RQ]) for i in range(2)]
                wqa_h = [SBx(ph1, f"wqa_h{i}", [128, RQ], BF16) for i in range(2)]
                wqa_l = [SBx(ph1, f"wqa_l{i}", [128, RQ], BF16) for i in range(2)]
                b_wqa_f = [B_(), B_()]
                b_wqa_h = [B_(), B_()]
                b_wqa_l = [B_(), B_()]
                xhg = [xhat2[i % 2] for i in range(G1B)]
                b_xhg = [b_xhat2[i % 2] for i in range(G1B)]
                cqt2 = [SBx(ph1, f"cqt{i}", [128, RQ]) for i in range(2)]
                b_cqt2 = [B_(), B_()]
                NQB = max(1, RQ // 512)
                QW = min(512, RQ)

                for grp in range(OWNT // G1B if phases >= 2 else 0):
                    tiles = [grp * G1B + i for i in range(G1B)]
                    for gi, towm in enumerate(tiles):
                        kt = 7 * TPB + towm
                        bx = towm % 2
                        CUR.update(set_stat(bx))
                        statb, mvv = CUR["statb"], CUR["mvv"]
                        b_stat, b_mvv = CUR["b_stat"], CUR["b_mvv"]
                        lx = pb.op("sync", "dma_start", out=xb[bx][:],
                                   in_=XK[kt * 128:(kt + 1) * 128, :],
                                   inc=True, bufw=(b_xb[bx],))
                        siv = ln_stats(lambda o, w, _b=xb[bx]: _b[:, o:o + w],
                                       D, (b_xb[bx],), EPS_LN, (lx,))
                        pb.op("vector", "tensor_scalar", xhg[gi][:], xb[bx][:],
                              mvv[:, 0:1], mvv[:, 3:4],
                              op0=ALU.subtract, op1=ALU.mult,
                              deps=(siv,), bufw=(b_xhg[gi],),
                              bufr=(b_xb[bx], b_mvv))
                    def mm_cq(dc, st):
                        pt, gi, xi, eh, el, wh, wl = st
                        first = dc == 0
                        last = dc == DC - 1
                        for nq in range(NQB):
                            bk = gi * NQB + nq
                            pb.op("tensor", "matmul", pbank[bk][:, 0:QW],
                                  xfm_h[xi][:],
                                  wqa_h[pt][:, nq * 512:nq * 512 + QW],
                                  start=first, stop=False,
                                  deps=(eh, wh, wl),
                                  bufwar=(b_pb[bk],) if first else ())
                            pb.op("tensor", "matmul", pbank[bk][:, 0:QW],
                                  xfm_h[xi][:],
                                  wqa_l[pt][:, nq * 512:nq * 512 + QW],
                                  start=False, stop=False)
                            lastnq = nq == NQB - 1
                            pb.op("tensor", "matmul", pbank[bk][:, 0:QW],
                                  xfm_l[xi][:],
                                  wqa_h[pt][:, nq * 512:nq * 512 + QW],
                                  start=False, stop=last,
                                  deps=(el,), inc=lastnq or last,
                                  bufw=(b_pb[bk],) if last else (),
                                  bufr=(b_xfm_h[xi], b_xfm_l[xi],
                                        b_wqa_h[pt], b_wqa_l[pt])
                                  if lastnq else ())

                    prev_cq = None
                    for dc in range(DC):
                        pt = dc % 2
                        lw = pb.op("sync", "dma_start", out=wqa_f[pt][:],
                                   in_=WQAT[dc * 128:(dc + 1) * 128, :],
                                   inc=True, bufw=(b_wqa_f[pt],))
                        wh, wl = evac_hilo(wqa_f[pt][:], wqa_h[pt][:], wqa_l[pt][:],
                                           (b_wqa_f[pt],), b_wqa_h[pt],
                                           b_wqa_l[pt], deps=(lw,))
                        for gi in range(G1B):
                            xi = pt * 2 + gi
                            stp = pb.op("tensor", "transpose",
                                        pbank[4 + pt][:, 0:128],
                                        xhg[gi][:, dc * 128:(dc + 1) * 128],
                                        id32[:], deps=(b_xhg[gi].writer,),
                                        inc=True, bufw=(b_pb[4 + pt],),
                                        bufr=(b_xhg[gi], b_consts))
                            eh, el = evac_hilo(pbank[4 + pt][:, 0:128],
                                               xfm_h[xi][:], xfm_l[xi][:],
                                               (b_pb[4 + pt],), b_xfm_h[xi],
                                               b_xfm_l[xi], deps=(stp,))
                            if prev_cq is not None:
                                mm_cq(*prev_cq)
                            prev_cq = (dc, (pt, gi, xi, eh, el, wh, wl))
                    mm_cq(*prev_cq)
                    for gi, towm in enumerate(tiles):
                        CUR.update(set_stat(towm % 2))
                        statb, mvv = CUR["statb"], CUR["mvv"]
                        b_stat, b_mvv = CUR["b_stat"], CUR["b_mvv"]
                        cqt, b_cqt = cqt2[towm % 2], b_cqt2[towm % 2]
                        bks = [gi * NQB + nq for nq in range(NQB)]
                        mmst = tuple(b_pb[bk].writer for bk in bks)
                        sts = None
                        for nq, bk in enumerate(bks):
                            sts = pb.op("vector", "bn_stats",
                                        statb[:, nq * 6:(nq + 1) * 6],
                                        pbank[bk][:, 0:QW], deps=mmst,
                                        bufw=(b_stat,), bufr=(b_pb[bk],))
                        sa = pb.op("vector", "bn_aggr", mvv[:, 0:2],
                                   statb[:, 0:6 * NQB], deps=(sts,),
                                   bufw=(b_mvv,), bufr=(b_stat,))
                        sm = pb.op("vector", "scalar_tensor_tensor", mvv[:, 4:5],
                                   mvv[:, 0:1], mvv[:, 0:1], mvv[:, 1:2],
                                   op0=ALU.mult, op1=ALU.add, deps=(sa,),
                                   bufw=(b_mvv,), bufr=(b_mvv,))
                        s2 = invsigma_ops(4, EPS_RMS, 6, (sm,))
                        scs = None
                        for nq, bk in enumerate(bks):
                            scs = pb.op("scalar", "activation",
                                        cqt[:, nq * 512:nq * 512 + QW],
                                        pbank[bk][:, 0:QW], AF.Copy,
                                        scale=mvv[:, 6:7],
                                        deps=(s2,) + mmst,
                                        bufw=(b_cqt,), bufr=(b_pb[bk], b_mvv))
                        for rc in range(RC):
                            pt = rc % 2
                            stq = pb.op("tensor", "transpose",
                                        pbank[4 + pt][:, 0:128],
                                        cqt[:, rc * 128:(rc + 1) * 128], id32[:],
                                        deps=(scs,), inc=True,
                                        bufw=(b_pb[4 + pt],),
                                        bufr=(b_cqt, b_consts))
                            col = rc * OWN + towm * 128
                            evac_hilo(pbank[4 + pt][:, 0:128],
                                      cq_hi[:, col:col + 128],
                                      cq_lo[:, col:col + 128],
                                      (b_pb[4 + pt],), b_cq_h, b_cq_l,
                                      deps=(stq,), scale=qaw[:, rc:rc + 1])
                pb.barrier()

            with ExitStack() as ph2:
                wqb_f = [SBx(ph2, f"wqb_f{i}", [128, RC * ETW]) for i in range(2)]
                wqb_h = [SBx(ph2, f"wqb_h{i}", [128, RC * ETW], BF16) for i in range(2)]
                wqb_l = [SBx(ph2, f"wqb_l{i}", [128, RC * ETW], BF16) for i in range(2)]
                b_wqb_f = [B_(), B_()]
                b_wqb_h = [B_(), B_()]
                b_wqb_l = [B_(), B_()]
                qtmp2 = [SBx(ph2, f"qtmp{i}", [128, ETW]) for i in range(2)]
                b_qtmp2 = [B_(), B_()]
                qrt2 = [SBx(ph2, f"qrt{i}", [128, HPE * 64]) for i in range(2)]
                b_qrt2 = [B_(), B_()]
                qtile2 = [SBx(ph2, f"qtile{i}", [128, ETW]) for i in range(2)]
                b_qtile2 = [B_(), B_()]
                qst_h2 = [SBx(ph2, f"qst_h{i}", [128, 128], BF16) for i in range(2)]
                qst_l2 = [SBx(ph2, f"qst_l{i}", [128, 128], BF16) for i in range(2)]
                b_qst_h2 = [B_(), B_()]
                b_qst_l2 = [B_(), B_()]
                wqbt_v = WQBT.ap().rearrange("(c p) w -> p c w", p=128)

                for et in range(NET if phases >= 3 else 0):
                    wb = et % 2
                    lw = pb.op("sync", "dma_start",
                               out=wqb_f[wb].ap().rearrange("p (c w) -> p c w", c=RC),
                               in_=wqbt_v[:, :, et * ETW:(et + 1) * ETW],
                               inc=True, bufw=(b_wqb_f[wb],))
                    wh, wl = evac_hilo(wqb_f[wb][:], wqb_h[wb][:], wqb_l[wb][:],
                                       (b_wqb_f[wb],), b_wqb_h[wb], b_wqb_l[wb],
                                       deps=(lw,))
                    def q_back(towm, mmq):
                        pc = towm % 2
                        qtmp, b_qtmp = qtmp2[pc], b_qtmp2[pc]
                        qrt, b_qrt = qrt2[pc], b_qrt2[pc]
                        qtile, b_qtile = qtile2[pc], b_qtile2[pc]
                        qst_h, b_qst_h = qst_h2[pc], b_qst_h2[pc]
                        qst_l, b_qst_l = qst_l2[pc], b_qst_l2[pc]
                        sq = pb.op("scalar", "activation", qtmp[:, 0:ETW],
                                   pbank[pc][:, 0:ETW], AF.Copy,
                                   scale=SCALE, deps=(mmq,),
                                   bufw=(b_qtmp,), bufr=(b_pb[pc],))
                        ap_nope_i = bass.AP(qtmp, 0, [[ETW, 128], [128, HPE], [1, 64]])
                        ap_nope_o = bass.AP(qtile, 0, [[ETW, 128], [128, HPE], [1, 64]])
                        c0 = pb.op("vector", "tensor_copy", out=ap_nope_o,
                                   in_=ap_nope_i, deps=(sq,),
                                   bufw=(b_qtile,), bufr=(b_qtmp,))
                        ap_ev = bass.AP(qtmp, 64, [[ETW, 128], [128, HPE], [2, 32]])
                        ap_od = bass.AP(qtmp, 65, [[ETW, 128], [128, HPE], [2, 32]])
                        kt_own = 7 * TPB + towm
                        cos_t = bass.AP(cosk, kt_own * 32,
                                        [[NKT * 32, 128], [0, HPE], [1, 32]])
                        sin_t = bass.AP(sink, kt_own * 32,
                                        [[NKT * 32, 128], [0, HPE], [1, 32]])
                        o1 = bass.AP(qtile, 64, [[ETW, 128], [128, HPE], [1, 32]])
                        o2 = bass.AP(qtile, 96, [[ETW, 128], [128, HPE], [1, 32]])
                        t3 = bass.AP(qrt, 0, [[HPE * 64, 128], [64, HPE], [1, 32]])
                        t4 = bass.AP(qrt, 32, [[HPE * 64, 128], [64, HPE], [1, 32]])
                        g1 = pb.op("vector", "tensor_tensor", o1, ap_ev, cos_t,
                                   op=ALU.mult, deps=(sq,), bufw=(b_qtile,),
                                   bufr=(b_qtmp, b_consts))
                        g2 = pb.op("vector", "tensor_tensor", o2, ap_od, cos_t,
                                   op=ALU.mult, deps=(sq,), bufw=(b_qtile,),
                                   bufr=(b_qtmp, b_consts))
                        g3 = pb.op("vector", "tensor_tensor", t3, ap_od, sin_t,
                                   op=ALU.mult, deps=(sq,), bufw=(b_qrt,),
                                   bufr=(b_qtmp, b_consts))
                        g4 = pb.op("vector", "tensor_tensor", t4, ap_ev, sin_t,
                                   op=ALU.mult, deps=(sq,), bufw=(b_qrt,),
                                   bufr=(b_qtmp, b_consts))
                        g5 = pb.op("vector", "tensor_sub", o1, o1, t3,
                                   deps=(g1, g3), bufw=(b_qtile,), bufr=(b_qrt,))
                        g6 = pb.op("vector", "tensor_add", o2, o2, t4,
                                   deps=(g2, g4, g5), bufw=(b_qtile,),
                                   bufr=(b_qrt,))
                        side = towm // TPB
                        itile = towm % TPB
                        for hh in range(HPE):
                            h = et * HPE + hh
                            pt2 = hh % 2
                            stq = pb.op("tensor", "transpose",
                                        pbank[4 + pt2][:, 0:128],
                                        qtile[:, hh * 128:(hh + 1) * 128],
                                        id32[:], deps=(g6, c0), inc=True,
                                        bufw=(b_pb[4 + pt2],),
                                        bufr=(b_qtile, b_consts))
                            qh, ql = evac_hilo(pbank[4 + pt2][:, 0:128],
                                               qst_h[:], qst_l[:],
                                               (b_pb[4 + pt2],), b_qst_h,
                                               b_qst_l, deps=(stq,))
                            pb.op("scalar", "dma_start",
                                  out=qsp_hi[h, side * 128:(side + 1) * 128,
                                             itile * 128:(itile + 1) * 128],
                                  in_=qst_h[:], deps=(qh,), inc=True,
                                  bufw=(b_qsp,), bufr=(b_qst_h,))
                            pb.op("scalar", "dma_start",
                                  out=qsp_lo[h, side * 128:(side + 1) * 128,
                                             itile * 128:(itile + 1) * 128],
                                  in_=qst_l[:], deps=(ql,), inc=True,
                                  bufw=(b_qsp,), bufr=(b_qst_l,))

                    prev_q = None
                    for towm in range(OWNT):
                        pc = towm % 2
                        for rc in range(RC):
                            first = rc == 0
                            last = rc == RC - 1
                            cq_sl = slice(rc * OWN + towm * 128,
                                          rc * OWN + (towm + 1) * 128)
                            pb.op("tensor", "matmul", pbank[pc][:, 0:ETW],
                                  cq_hi[:, cq_sl],
                                  wqb_h[wb][:, rc * ETW:(rc + 1) * ETW],
                                  start=first, stop=False,
                                  deps=(wh, wl, b_cq_h.writer, b_cq_l.writer),
                                  bufwar=(b_pb[pc],) if first else ())
                            pb.op("tensor", "matmul", pbank[pc][:, 0:ETW],
                                  cq_hi[:, cq_sl],
                                  wqb_l[wb][:, rc * ETW:(rc + 1) * ETW],
                                  start=False, stop=False)
                            pb.op("tensor", "matmul", pbank[pc][:, 0:ETW],
                                  cq_lo[:, cq_sl],
                                  wqb_h[wb][:, rc * ETW:(rc + 1) * ETW],
                                  start=False, stop=last, inc=last,
                                  bufw=(b_pb[pc],) if last else (),
                                  bufr=(b_cq_h, b_cq_l, b_wqb_h[wb],
                                        b_wqb_l[wb]) if last else ())
                        mmq = b_pb[pc].writer
                        if prev_q is not None:
                            q_back(*prev_q)
                        prev_q = (towm, mmq)
                    q_back(*prev_q)
                pb.barrier()

            with ExitStack() as ph3:
                v_hi = SBx(ph3, "v_hi", [128, NKT * VST], BF16)
                v_lo = SBx(ph3, "v_lo", [128, NKT * VST], BF16)
                vstage = [SBx(ph3, f"vstage{i}", [128, 512]) for i in range(2)]
                b_vst = [B_(), B_()]
                b_v_h, b_v_l = B_(), B_()
                b_kn_h, b_kn_l = b_kpeT_h, b_kpeT_l
                knw_h = SBx(ph3, "knw_h", [128, KC * 64], BF16)
                knw_l = SBx(ph3, "knw_l", [128, KC * 64], BF16)
                vw_h = SBx(ph3, "vw_h", [128, KC * VD], BF16)
                vw_l = SBx(ph3, "vw_l", [128, KC * VD], BF16)
                b_knw_h, b_knw_l, b_vw_h, b_vw_l = B_(), B_(), B_(), B_()
                qsl_h = SBx(ph3, "qsl_h", [128, 7 * 512], BF16)
                qsl_l = SBx(ph3, "qsl_l", [128, 7 * 512], BF16)
                b_qsl_h, b_qsl_l = B_(), B_()
                qow_h = SBx(ph3, "qow_h", [128, OWN], BF16)
                qow_l = SBx(ph3, "qow_l", [128, OWN], BF16)
                b_qow_h, b_qow_l = B_(), B_()
                p32 = [SBx(ph3, f"p32_{i}", [128, 512]) for i in range(3)]
                b_p32 = [B_() for _ in range(3)]
                pt_h = [SBx(ph3, f"pt_h{i}", [128, 512], BF16) for i in range(3)]
                pt_l = [SBx(ph3, f"pt_l{i}", [128, 512], BF16) for i in range(3)]
                b_pth = [B_() for _ in range(3)]
                b_ptl = [B_() for _ in range(3)]
                oslot2 = [SBx(ph3, f"oslot{i}", [128, 9 * OST]) for i in range(2)]
                b_oslot2 = [B_(), B_()]
                negm2 = [SBx(ph3, f"negm{i}", [128, 12]) for i in range(2)]
                b_negm2 = [B_(), B_()]
                msel2 = [SBx(ph3, f"msel{i}", [128, 12]) for i in range(2)]
                coefs2 = [SBx(ph3, f"coefs{i}", [128, 20]) for i in range(2)]
                b_msel2 = [B_(), B_()]
                b_coef2 = [B_(), B_()]
                oacc2 = [SBx(ph3, f"oacc{i}", [128, 128]) for i in range(2)]
                b_oacc2 = [B_(), B_()]
                ofst_h2 = [SBx(ph3, f"ofst_h{i}", [128, 128], BF16) for i in range(2)]
                ofst_l2 = [SBx(ph3, f"ofst_l{i}", [128, 128], BF16) for i in range(2)]
                b_ofst_h2 = [B_(), B_()]
                b_ofst_l2 = [B_(), B_()]

                pb.op("vector", "memset",
                      bass.AP(v_hi, 128, [[NKT * VST, 128], [VST, NKT]]), 1.0,
                      bufw=(b_v_h,))
                pb.op("vector", "memset",
                      bass.AP(v_lo, 128, [[NKT * VST, 128], [VST, NKT]]), 0.0,
                      bufw=(b_v_l,))

                wkvb_hv = wkvb_hi.ap().rearrange("(c p) w -> p c w", p=128)
                wkvb_lv = wkvb_lo.ap().rearrange("(c p) w -> p c w", p=128)

                for h in range(H if phases >= 4 else 0):
                    l1 = pb.op("sync", "dma_start",
                               out=knw_h.ap().rearrange("p (c w) -> p c w", c=KC),
                               in_=wkvb_hv[:, :, h * KVD:h * KVD + 64],
                               inc=True, bufw=(b_knw_h,), bufr=(b_wkvb,))
                    l2 = pb.op("sync", "dma_start",
                               out=knw_l.ap().rearrange("p (c w) -> p c w", c=KC),
                               in_=wkvb_lv[:, :, h * KVD:h * KVD + 64],
                               inc=True, bufw=(b_knw_l,), bufr=(b_wkvb,))
                    l3 = pb.op("sync", "dma_start",
                               out=vw_h.ap().rearrange("p (c w) -> p c w", c=KC),
                               in_=wkvb_hv[:, :, h * KVD + 64:(h + 1) * KVD],
                               inc=True, bufw=(b_vw_h,), bufr=(b_wkvb,))
                    l4 = pb.op("sync", "dma_start",
                               out=vw_l.ap().rearrange("p (c w) -> p c w", c=KC),
                               in_=wkvb_lv[:, :, h * KVD + 64:(h + 1) * KVD],
                               inc=True, bufw=(b_vw_l,), bufr=(b_wkvb,))
                    gq = None
                    qsp_hi_flat = qsp_hi.ap().rearrange("h r w -> (h r) w")
                    qsp_lo_flat = qsp_lo.ap().rearrange("h r w -> (h r) w")
                    for j in range(7):
                        pb.op("gpsimd", "indirect_dma_start",
                              out=qsl_h[:, j * 512:j * 512 + TPB * 128],
                              out_offset=None, in_=qsp_hi_flat,
                              in_offset=IndirectOffsetOnAxis(
                                  ap=qsel[:, h * 7 + j:h * 7 + j + 1], axis=0),
                              inc=True, bufw=(b_qsl_h,),
                              bufr=(b_qsp, b_consts))
                        pb.op("gpsimd", "indirect_dma_start",
                              out=qsl_l[:, j * 512:j * 512 + TPB * 128],
                              out_offset=None, in_=qsp_lo_flat,
                              in_offset=IndirectOffsetOnAxis(
                                  ap=qsel[:, h * 7 + j:h * 7 + j + 1], axis=0),
                              inc=True, bufw=(b_qsl_l,),
                              bufr=(b_qsp, b_consts))
                    for side in range(2):
                        pb.op("scalar", "dma_start",
                              out=qow_h[:, side * (TPB * 128):
                                        (side + 1) * (TPB * 128)],
                              in_=qsp_hi[h, side * 128:(side + 1) * 128, :],
                              inc=True, bufw=(b_qow_h,), bufr=(b_qsp,))
                        gq = pb.op("scalar", "dma_start",
                                   out=qow_l[:, side * (TPB * 128):
                                             (side + 1) * (TPB * 128)],
                                   in_=qsp_lo[h, side * 128:(side + 1) * 128, :],
                                   inc=True, bufw=(b_qow_l,), bufr=(b_qsp,))
                    for t9 in range((NKEY + 511) // 512):
                        w9 = min(512, NKEY - t9 * 512)
                        pc = 4 + t9 % 2
                        for rc in range(KC):
                            first = rc == 0
                            last = rc == KC - 1
                            cp_sl = slice(rc * NKEY + t9 * 512,
                                          rc * NKEY + t9 * 512 + w9)
                            pb.op("tensor", "matmul", pbank[pc][0:64, 0:w9],
                                  knw_h[:, rc * 64:(rc + 1) * 64],
                                  cpk_hi[:, cp_sl], start=first, stop=False,
                                  deps=(l1, l2),
                                  bufd=(b_cpk_h, b_cpk_l),
                                  bufwar=(b_pb[pc],) if first else ())
                            pb.op("tensor", "matmul", pbank[pc][0:64, 0:w9],
                                  knw_h[:, rc * 64:(rc + 1) * 64],
                                  cpk_lo[:, cp_sl], start=False, stop=False)
                            pb.op("tensor", "matmul", pbank[pc][0:64, 0:w9],
                                  knw_l[:, rc * 64:(rc + 1) * 64],
                                  cpk_hi[:, cp_sl], start=False, stop=last,
                                  inc=last,
                                  bufw=(b_pb[pc],) if last else (),
                                  bufr=(b_knw_h, b_knw_l, b_cpk_h, b_cpk_l)
                                  if last else ())
                        evac_hilo(pbank[pc][0:64, 0:w9],
                                  kpeT_hi[0:64, t9 * 512:t9 * 512 + w9],
                                  kpeT_lo[0:64, t9 * 512:t9 * 512 + w9],
                                  (b_pb[pc],), b_kn_h, b_kn_l,
                                  deps=(b_pb[pc].writer,))
                    for t9 in range((NKEY + 511) // 512):
                        w9 = min(512, NKEY - t9 * 512)
                        nch9 = w9 // 128
                        pc = 4 + t9 % 2
                        for rc in range(KC):
                            first = rc == 0
                            last = rc == KC - 1
                            cp_sl = slice(rc * NKEY + t9 * 512,
                                          rc * NKEY + t9 * 512 + w9)
                            pb.op("tensor", "matmul", pbank[pc][:, 0:w9],
                                  vw_h[:, rc * VD:(rc + 1) * VD],
                                  cpk_hi[:, cp_sl],
                                  start=first, stop=False, deps=(l3, l4),
                                  bufd=(b_cpk_h, b_cpk_l),
                                  bufwar=(b_pb[pc],) if first else ())
                            pb.op("tensor", "matmul", pbank[pc][:, 0:w9],
                                  vw_h[:, rc * VD:(rc + 1) * VD],
                                  cpk_lo[:, cp_sl],
                                  start=False, stop=False)
                            pb.op("tensor", "matmul", pbank[pc][:, 0:w9],
                                  vw_l[:, rc * VD:(rc + 1) * VD],
                                  cpk_hi[:, cp_sl],
                                  start=False, stop=last, inc=last,
                                  bufw=(b_pb[pc],) if last else (),
                                  bufr=(b_vw_h, b_vw_l, b_cpk_h, b_cpk_l)
                                  if last else ())
                        sv9 = pb.op("scalar", "copy", vstage[t9 % 2][:, 0:w9],
                                    pbank[pc][:, 0:w9],
                                    deps=(b_pb[pc].writer,),
                                    bufw=(b_vst[t9 % 2],), bufr=(b_pb[pc],))
                        for kc9 in range(nch9):
                            vt = t9 * 4 + kc9
                            tb9 = vt % 2
                            stv = pb.op("tensor", "transpose",
                                        pbb[tb9][:, 0:128],
                                        vstage[t9 % 2][:, kc9 * 128:
                                                       (kc9 + 1) * 128],
                                        id32[:], deps=(sv9,), inc=True,
                                        bufw=(b_pbb[tb9],),
                                        bufr=(b_vst[t9 % 2], b_consts))
                            evac_hilo(pbb[tb9][:, 0:128],
                                      v_hi[:, vt * VST:vt * VST + VD],
                                      v_lo[:, vt * VST:vt * VST + VD],
                                      (b_pbb[tb9],), b_v_h, b_v_l,
                                      deps=(stv,))

                    def job_front(jslot, qh_ap, ql_ap, kcol, klen, qbufs,
                                  with_tri, oslot, b_oslot, negm, b_negm):
                        spc = 2 + jslot % 3
                        opc = jslot % 3
                        nchunk = klen // 128
                        pb.op("tensor", "matmul", pbank[spc][:, 0:klen],
                              qh_ap(0, 128), kpeT_hi[0:128, kcol:kcol + klen],
                              start=True, stop=False,
                              bufd=qbufs + (b_kpeT_h, b_kpeT_l),
                              bufwar=(b_pb[spc],))
                        pb.op("tensor", "matmul", pbank[spc][:, 0:klen],
                              qh_ap(0, 128), kpeT_lo[0:128, kcol:kcol + klen],
                              start=False, stop=False)
                        smm = pb.op("tensor", "matmul", pbank[spc][:, 0:klen],
                                    ql_ap(0, 128), kpeT_hi[0:128, kcol:kcol + klen],
                                    start=False, stop=True,
                                    inc=True, bufw=(b_pb[spc],),
                                    bufr=qbufs + (b_kpeT_h, b_kpeT_l))
                        if sub < 2:
                            return None
                        if with_tri:
                            smm = pb.op("vector", "tensor_add",
                                        pbank[spc][:, klen - 128:klen],
                                        pbank[spc][:, klen - 128:klen], tri[:],
                                        deps=(smm,), bufw=(b_pb[spc],),
                                        bufr=(b_consts,))
                        if sub < 3:
                            return None
                        nm = pb.op("vector", "reduce_max", negm[:, jslot:jslot + 1],
                                   pbank[spc][:, 0:klen], axis=AX.X, negate=True,
                                   deps=(smm,), bufw=(b_negm,),
                                   bufr=(b_pb[spc],))
                        ex = pb.op("scalar", "activation", p32[opc][:, 0:klen],
                                   pbank[spc][:, 0:klen], AF.Exp,
                                   bias=negm[:, jslot:jslot + 1], scale=1.0,
                                   deps=(nm,), bufw=(b_p32[opc],),
                                   bufr=(b_pb[spc], b_negm))
                        return (jslot, kcol, klen, opc, spc, ex,
                                oslot, b_oslot)

                    def job_back(st8):
                        if st8 is None or sub < 4:
                            return
                        jslot, kcol, klen, opc, spc, ex, oslot, b_oslot = st8
                        nchunk = klen // 128
                        tb = jslot % 2
                        tps = None
                        for kc2 in range(nchunk):
                            last2 = kc2 == nchunk - 1
                            tps = pb.op("tensor", "transpose",
                                        pbb[tb][:, kc2 * 128:(kc2 + 1) * 128],
                                        p32[opc][:, kc2 * 128:(kc2 + 1) * 128],
                                        id32[:], deps=(ex,), inc=last2,
                                        bufwar=(b_pbb[tb],) if kc2 == 0 else (),
                                        bufw=(b_pbb[tb],) if last2 else (),
                                        bufr=(b_p32[opc], b_consts)
                                        if last2 else (b_consts,))
                        if sub < 4.5:
                            return
                        e1, e2 = evac_hilo(pbb[tb][:, 0:nchunk * 128],
                                           pt_h[opc][:, 0:nchunk * 128],
                                           pt_l[opc][:, 0:nchunk * 128],
                                           (b_pbb[tb],), b_pth[opc], b_ptl[opc],
                                           deps=(tps,))
                        if sub < 5:
                            return
                        oc = jslot % 2
                        for kc2 in range(nchunk):
                            vt = kcol // 128 + kc2
                            first = kc2 == 0
                            last2 = kc2 == nchunk - 1
                            pb.op("tensor", "matmul", pbank[oc][:, 0:VD + 1],
                                  pt_h[opc][:, kc2 * 128:(kc2 + 1) * 128],
                                  v_hi[:, vt * VST:vt * VST + VD + 1],
                                  start=first, stop=False, deps=(e1, e2),
                                  bufwar=(b_pb[oc],) if first else ())
                            pb.op("tensor", "matmul", pbank[oc][:, 0:VD + 1],
                                  pt_h[opc][:, kc2 * 128:(kc2 + 1) * 128],
                                  v_lo[:, vt * VST:vt * VST + VD + 1],
                                  start=False, stop=False)
                            pb.op("tensor", "matmul", pbank[oc][:, 0:VD + 1],
                                  pt_l[opc][:, kc2 * 128:(kc2 + 1) * 128],
                                  v_hi[:, vt * VST:vt * VST + VD + 1],
                                  start=False, stop=last2, inc=last2,
                                  bufw=(b_pb[oc],) if last2 else (),
                                  bufr=(b_pth[opc], b_ptl[opc], b_v_h, b_v_l)
                                  if last2 else ())
                        if sub < 6:
                            return
                        pb.op("vector", "tensor_copy",
                              oslot[:, jslot * OST:jslot * OST + VD + 1],
                              pbank[oc][:, 0:VD + 1], deps=(b_pb[oc].writer,),
                              bufw=(b_oslot,), bufr=(b_pb[oc],))

                    LOOKAHEAD = 2
                    pend = []

                    def push_job(st8):
                        pend.append(st8)
                        if len(pend) > LOOKAHEAD:
                            job_back(pend.pop(0))

                    def flush_jobs():
                        while pend:
                            job_back(pend.pop(0))

                    for it in range(TPB):
                        itp = it % 2
                        oslot, b_oslot = oslot2[itp], b_oslot2[itp]
                        negm, b_negm = negm2[itp], b_negm2[itp]
                        msel, b_msel = msel2[itp], b_msel2[itp]
                        coefs, b_coef = coefs2[itp], b_coef2[itp]
                        for j in range(7):
                            qc = j * 512 + it * 128
                            push_job(job_front(j,
                                     lambda a, b, qc=qc: qsl_h[a:b, qc:qc + 128],
                                     lambda a, b, qc=qc: qsl_l[a:b, qc:qc + 128],
                                     j * BS, BS, (b_qsl_h, b_qsl_l), False,
                                     oslot, b_oslot, negm, b_negm))
                        for side in range(2):
                            qc = side * (TPB * 128) + it * 128
                            push_job(job_front(7 + side,
                                     lambda a, b, qc=qc: qow_h[a:b, qc:qc + 128],
                                     lambda a, b, qc=qc: qow_l[a:b, qc:qc + 128],
                                     (7 + side) * BS, (it + 1) * 128,
                                     (b_qow_h, b_qow_l), True,
                                     oslot, b_oslot, negm, b_negm))
                        flush_jobs()
                        for side in range(2 if sub >= 7 else 0):
                            oacc, b_oacc = oacc2[side], b_oacc2[side]
                            ofst_h, b_ofst_h = ofst_h2[side], b_ofst_h2[side]
                            ofst_l, b_ofst_l = ofst_l2[side], b_ofst_l2[side]
                            selx = sela if side == 0 else selb
                            ms = pb.op("vector", "scalar_tensor_tensor",
                                       msel[:, 0:9], negm[:, 0:9], -1.0,
                                       selx[:, 0:9], op0=ALU.mult, op1=ALU.add,
                                       deps=(b_negm.writer, b_oslot.writer),
                                       bufw=(b_msel,),
                                       bufr=(b_negm, b_consts))
                            nmx = pb.op("vector", "reduce_max", msel[:, 9:10],
                                        msel[:, 0:9], axis=AX.X, negate=True,
                                        deps=(ms,), bufw=(b_msel,))
                            cf = pb.op("scalar", "activation", coefs[:, 0:9],
                                       msel[:, 0:9], AF.Exp, bias=msel[:, 9:10],
                                       scale=1.0, deps=(nmx,),
                                       bufw=(b_coef,), bufr=(b_msel,))
                            lv = bass.AP(oslot, VD, [[9 * OST, 128], [OST, 9]])
                            lw2 = pb.op("vector", "tensor_tensor", coefs[:, 9:18],
                                        lv, coefs[:, 0:9], op=ALU.mult,
                                        deps=(cf,), bufw=(b_coef,),
                                        bufr=(b_oslot,))
                            ls = pb.op("vector", "reduce_sum", msel[:, 10:11],
                                       coefs[:, 9:18], axis=AX.X, deps=(lw2,),
                                       bufw=(b_msel,), bufr=(b_coef,))
                            li = pb.op("vector", "reciprocal", msel[:, 11:12],
                                       msel[:, 10:11], deps=(ls,),
                                       bufw=(b_msel,))
                            cn = pb.op("vector", "tensor_scalar", coefs[:, 0:9],
                                       coefs[:, 0:9], msel[:, 11:12], None,
                                       op0=ALU.mult, deps=(li,),
                                       bufw=(b_coef,), bufr=(b_msel,))
                            js = [jj for jj in range(9)
                                  if not (side == 0 and jj == 8)
                                  and not (side == 1 and jj == 7)]
                            a0 = pb.op("vector", "tensor_scalar", oacc[:, 0:VD],
                                       oslot[:, js[0] * OST:js[0] * OST + VD],
                                       coefs[:, js[0]:js[0] + 1], None,
                                       op0=ALU.mult, deps=(cn,),
                                       bufw=(b_oacc,), bufr=(b_oslot, b_coef))
                            for jj in js[1:]:
                                a0 = pb.op("vector", "scalar_tensor_tensor",
                                           oacc[:, 0:VD],
                                           oslot[:, jj * OST:jj * OST + VD],
                                           coefs[:, jj:jj + 1], oacc[:, 0:VD],
                                           op0=ALU.mult, op1=ALU.add,
                                           deps=(a0,), bufw=(b_oacc,),
                                           bufr=(b_oslot, b_coef))
                            pt2 = 4 + side
                            sto = pb.op("tensor", "transpose",
                                        pbank[pt2][:, 0:128], oacc[:, 0:128],
                                        id32[:], deps=(a0,), inc=True,
                                        bufw=(b_pb[pt2],),
                                        bufr=(b_oacc, b_consts))
                            ohh, oll = evac_hilo(pbank[pt2][:, 0:128],
                                                 ofst_h[:], ofst_l[:],
                                                 (b_pb[pt2],), b_ofst_h,
                                                 b_ofst_l, deps=(sto,))
                            towm = side * TPB + it
                            pb.op("scalar", "dma_start",
                                  out=osp_hi[h, towm], in_=ofst_h[:],
                                  deps=(ohh,), inc=True, bufw=(b_osp,),
                                  bufr=(b_ofst_h,))
                            pb.op("scalar", "dma_start",
                                  out=osp_lo[h, towm], in_=ofst_l[:],
                                  deps=(oll,), inc=True, bufw=(b_osp,),
                                  bufr=(b_ofst_l,))
            pb.barrier()

        with ExitStack() as ph4:
            pk4 = [PSx(ph4, f"pk4_{i}", [128, 512]) for i in range(8)]
            b_pk4 = [B_() for _ in range(8)]
            attn4 = SBx(ph4, "attn4", [128, TQG * D])
            b_attn4 = B_()
            wo_f = [SBx(ph4, f"wo_f{i}", [128, DQW]) for i in range(2)]
            wo_h = [SBx(ph4, f"wo_h{i}", [128, DQW], BF16) for i in range(2)]
            wo_l = [SBx(ph4, f"wo_l{i}", [128, DQW], BF16) for i in range(2)]
            b_wo_f = [B_(), B_()]
            b_wo_h = [B_(), B_()]
            b_wo_l = [B_(), B_()]
            ofm_h = [SBx(ph4, f"ofm_h{i}", [128, TQG * 128], BF16) for i in range(2)]
            ofm_l = [SBx(ph4, f"ofm_l{i}", [128, TQG * 128], BF16) for i in range(2)]
            b_ofm_h = [B_(), B_()]
            b_ofm_l = [B_(), B_()]
            x2b = SBx(ph4, "x2b", [128, D])
            xh2 = SBx(ph4, "xh2", [128, D])
            ffnb = SBx(ph4, "ffnb", [128, D])
            b_x2, b_xh2, b_ffn = B_(), B_(), B_()
            xfm2_h = SBx(ph4, "xfm2_h", [128, D], BF16)
            xfm2_l = SBx(ph4, "xfm2_l", [128, D], BF16)
            b_xfm2_h, b_xfm2_l = B_(), B_()
            xb4 = SBx(ph4, "xb4", [128, D])
            b_xb4 = B_()
            eb_h = SBx(ph4, "eb_h", [E, D], BF16)
            eb_l = SBx(ph4, "eb_l", [E, D], BF16)
            b_ebf, b_eb_h, b_eb_l = B_(), B_(), B_()
            rw_h = SBx(ph4, "rw_h", [128, DC * E], BF16)
            rw_l = SBx(ph4, "rw_l", [128, DC * E], BF16)
            b_rwf, b_rwh, b_rwl = B_(), B_(), B_()
            rb_f = SBx(ph4, "rb_f", [1, E])
            rb_h = SBx(ph4, "rb_h", [1, E], BF16)
            rb_l = SBx(ph4, "rb_l", [1, E], BF16)
            b_rbf, b_rbh, b_rbl = B_(), B_(), B_()
            pex = SBx(ph4, "pex", [128, E])
            pnr = SBx(ph4, "pnr", [128, E])
            wdn = SBx(ph4, "wdn", [128, E])
            top8 = SBx(ph4, "top8", [128, 8])
            b_pex, b_pnr, b_wdn, b_top8 = B_(), B_(), B_(), B_()
            wd_h = SBx(ph4, "wd_h", [128, E], BF16)
            wd_l = SBx(ph4, "wd_l", [128, E], BF16)
            b_wdh, b_wdl = B_(), B_()
            wdt_h = SBx(ph4, "wdt_h", [64, 128], BF16)
            wdt_l = SBx(ph4, "wdt_l", [64, 128], BF16)
            b_wdth, b_wdtl = B_(), B_()
            st4 = SBx(ph4, "st4", [128, 8 * 6])
            mv4 = SBx(ph4, "mv4", [128, 8])
            b_st4, b_mv4 = B_(), B_()

            for c0 in range(0, D, DQW):
                le = pb.op("sync", "dma_start", out=wo_f[0][0:E, 0:DQW],
                           in_=EB[:, c0:c0 + DQW], inc=True, bufw=(b_wo_f[0],))
                evac_hilo(wo_f[0][0:E, 0:DQW], eb_h[0:E, c0:c0 + DQW],
                          eb_l[0:E, c0:c0 + DQW], (b_wo_f[0],), b_eb_h, b_eb_l,
                          deps=(le,))
            rwt_v = RWT.ap().rearrange("(c p) w -> p c w", p=128)
            CPG = max(1, DQW // E)
            for cg in range(0, DC, CPG):
                ng = min(CPG, DC - cg)
                lr = pb.op("sync", "dma_start",
                           out=bass.AP(wo_f[0], 0, [[DQW, 128], [E, ng], [1, E]]),
                           in_=rwt_v[:, cg:cg + ng, :],
                           inc=True, bufw=(b_wo_f[0],))
                evac_hilo(wo_f[0][:, 0:ng * E], rw_h[:, cg * E:(cg + ng) * E],
                          rw_l[:, cg * E:(cg + ng) * E], (b_wo_f[0],),
                          b_rwh, b_rwl, deps=(lr,))
            lb = pb.op("sync", "dma_start", out=rb_f[:], in_=RB[:], inc=True,
                       bufw=(b_rbf,))
            evac_hilo(rb_f[:], rb_h[:], rb_l[:], (b_rbf,), b_rbh, b_rbl,
                      deps=(lb,))

            def ln4(src_ap_fn, ncol, src_bufs, eps, deps, mcol):
                nch = (ncol + 511) // 512
                sts = None
                for ci in range(nch):
                    w = min(512, ncol - ci * 512)
                    sts = pb.op("vector", "bn_stats", st4[:, ci * 6:(ci + 1) * 6],
                                src_ap_fn(ci * 512, w), deps=tuple(deps),
                                bufw=(b_st4,), bufr=src_bufs)
                sa = pb.op("vector", "bn_aggr", mv4[:, mcol:mcol + 2],
                           st4[:, 0:6 * nch], deps=(sts,), bufw=(b_mv4,),
                           bufr=(b_st4,))
                s0 = pb.op("vector", "tensor_scalar_add",
                           mv4[:, mcol + 3:mcol + 4],
                           mv4[:, mcol + 1:mcol + 2], float(eps), deps=(sa,),
                           bufw=(b_mv4,), bufr=(b_mv4,))
                s1 = pb.op("scalar", "activation", mv4[:, mcol + 3:mcol + 4],
                           mv4[:, mcol + 3:mcol + 4], AF.Ln,
                           scale=1.0, deps=(s0,), bufw=(b_mv4,), bufr=(b_mv4,))
                s2 = pb.op("scalar", "activation", mv4[:, mcol + 3:mcol + 4],
                           mv4[:, mcol + 3:mcol + 4], AF.Exp,
                           scale=-0.5, deps=(s1,), bufw=(b_mv4,), bufr=(b_mv4,))
                return s2

            NDT = DQW // 512
            for tq in range(OWNT // TQG if phases >= 5 else 0):
                for dq in range(D // DQW):
                    for hv in range(H):
                        wb = hv % 2
                        lw = pb.op("sync", "dma_start", out=wo_f[wb][:],
                                   in_=WOT[hv * 128:(hv + 1) * 128,
                                           dq * DQW:(dq + 1) * DQW],
                                   inc=True, bufw=(b_wo_f[wb],))
                        wh, wl = evac_hilo(wo_f[wb][:], wo_h[wb][:], wo_l[wb][:],
                                           (b_wo_f[wb],), b_wo_h[wb], b_wo_l[wb],
                                           deps=(lw,))
                        lo1 = pb.op("scalar", "dma_start",
                                    out=ofm_h[wb].ap().rearrange(
                                        "p (t w) -> p t w", t=TQG),
                                    in_=osp_hi.ap()[hv, tq * TQG:(tq + 1) * TQG]
                                    .rearrange("t p w -> p t w"),
                                    inc=True, bufw=(b_ofm_h[wb],), bufr=(b_osp,))
                        lo2 = pb.op("scalar", "dma_start",
                                    out=ofm_l[wb].ap().rearrange(
                                        "p (t w) -> p t w", t=TQG),
                                    in_=osp_lo.ap()[hv, tq * TQG:(tq + 1) * TQG]
                                    .rearrange("t p w -> p t w"),
                                    inc=True, bufw=(b_ofm_l[wb],), bufr=(b_osp,))
                        for t4 in range(TQG):
                            for ndt in range(NDT):
                                bk = t4 * NDT + ndt
                                first = hv == 0
                                last = hv == H - 1
                                pb.op("tensor", "matmul", pk4[bk][:, 0:512],
                                      ofm_h[wb][:, t4 * 128:(t4 + 1) * 128],
                                      wo_h[wb][:, ndt * 512:(ndt + 1) * 512],
                                      start=first, stop=False,
                                      deps=(wh, wl, lo1, lo2),
                                      bufwar=(b_pk4[bk],) if first else ())
                                pb.op("tensor", "matmul", pk4[bk][:, 0:512],
                                      ofm_h[wb][:, t4 * 128:(t4 + 1) * 128],
                                      wo_l[wb][:, ndt * 512:(ndt + 1) * 512],
                                      start=False, stop=False)
                                lastb = ndt == NDT - 1 and t4 == TQG - 1
                                pb.op("tensor", "matmul", pk4[bk][:, 0:512],
                                      ofm_l[wb][:, t4 * 128:(t4 + 1) * 128],
                                      wo_h[wb][:, ndt * 512:(ndt + 1) * 512],
                                      start=False, stop=last, inc=lastb or last,
                                      bufw=(b_pk4[bk],) if last else (),
                                      bufr=(b_ofm_h[wb], b_ofm_l[wb],
                                            b_wo_h[wb], b_wo_l[wb])
                                      if lastb else ())
                    for t4 in range(TQG):
                        for ndt in range(NDT):
                            bk = t4 * NDT + ndt
                            pb.op("scalar", "copy",
                                  attn4[:, t4 * D + dq * DQW + ndt * 512:
                                        t4 * D + dq * DQW + (ndt + 1) * 512],
                                  pk4[bk][:, 0:512], deps=(b_pk4[bk].writer,),
                                  bufw=(b_attn4,), bufr=(b_pk4[bk],))
                for t4 in range(TQG):
                    towm = tq * TQG + t4

                    def att(o, w, _t=t4):
                        return attn4[:, _t * D + o:_t * D + o + w]

                    s2 = ln4(att, D, (b_attn4,), EPS_LN, (b_attn4.writer,), 0)
                    lx = pb.op("sync", "dma_start", out=xb4[:],
                               in_=XK[(7 * TPB + towm) * 128:
                                      (7 * TPB + towm + 1) * 128, :],
                               inc=True, bufw=(b_xb4,))
                    t1 = pb.op("vector", "tensor_scalar", x2b[:], att(0, D),
                               mv4[:, 0:1], mv4[:, 3:4], op0=ALU.subtract,
                               op1=ALU.mult, deps=(s2,),
                               bufw=(b_x2,), bufr=(b_attn4, b_mv4))
                    t2 = pb.op("vector", "tensor_add", x2b[:], x2b[:], xb4[:],
                               deps=(t1, lx), bufw=(b_x2,), bufr=(b_xb4,))
                    s3 = ln4(lambda o, w: x2b[:, o:o + w], D, (b_x2,),
                             EPS_LN, (t2,), 4)
                    t3 = pb.op("vector", "tensor_scalar", xh2[:], x2b[:],
                               mv4[:, 4:5], mv4[:, 7:8], op0=ALU.subtract,
                               op1=ALU.mult, deps=(s3,),
                               bufw=(b_xh2,), bufr=(b_x2, b_mv4))
                    for dc in range(DC):
                        pt2 = 4 + dc % 2
                        stx = pb.op("tensor", "transpose", pk4[pt2][:, 0:128],
                                    xh2[:, dc * 128:(dc + 1) * 128], id32[:],
                                    deps=(t3,), inc=True,
                                    bufw=(b_pk4[pt2],),
                                    bufr=(b_xh2, b_consts))
                        fh, fl = evac_hilo(pk4[pt2][:, 0:128],
                                           xfm2_h[:, dc * 128:(dc + 1) * 128],
                                           xfm2_l[:, dc * 128:(dc + 1) * 128],
                                           (b_pk4[pt2],), b_xfm2_h, b_xfm2_l,
                                           deps=(stx,))
                        first = dc == 0
                        pb.op("tensor", "matmul", pk4[0][:, 0:E],
                              xfm2_h[:, dc * 128:(dc + 1) * 128],
                              rw_h[:, dc * E:(dc + 1) * E], start=first,
                              stop=False, deps=(fh, fl, b_rwh.writer,
                                                b_rwl.writer),
                              bufwar=(b_pk4[0],) if first else ())
                        pb.op("tensor", "matmul", pk4[0][:, 0:E],
                              xfm2_h[:, dc * 128:(dc + 1) * 128],
                              rw_l[:, dc * E:(dc + 1) * E], start=False,
                              stop=False)
                        pb.op("tensor", "matmul", pk4[0][:, 0:E],
                              xfm2_l[:, dc * 128:(dc + 1) * 128],
                              rw_h[:, dc * E:(dc + 1) * E], start=False,
                              stop=False)
                    pb.op("tensor", "matmul", pk4[0][:, 0:E], onesb[:],
                          rb_h[:], start=False, stop=False,
                          deps=(b_rbh.writer, b_rbl.writer, b_consts.writer))
                    lmm = pb.op("tensor", "matmul", pk4[0][:, 0:E], onesb[:],
                                rb_l[:], start=False, stop=True, inc=True,
                                bufw=(b_pk4[0],),
                                bufr=(b_xfm2_h, b_xfm2_l, b_rwh, b_rwl,
                                      b_rbh, b_rbl))
                    nm = pb.op("vector", "reduce_max", mv4[:, 0:1],
                               pk4[0][:, 0:E], axis=AX.X, negate=True,
                               deps=(lmm,), bufw=(b_mv4,), bufr=(b_pk4[0],))
                    ex = pb.op("scalar", "activation", pex[:], pk4[0][:, 0:E],
                               AF.Exp, bias=mv4[:, 0:1], scale=1.0, deps=(nm,),
                               bufw=(b_pex,), bufr=(b_pk4[0], b_mv4))
                    sm = pb.op("vector", "reduce_sum", mv4[:, 1:2], pex[:],
                               axis=AX.X, deps=(ex,), bufw=(b_mv4,),
                               bufr=(b_pex,))
                    iv = pb.op("vector", "reciprocal", mv4[:, 2:3], mv4[:, 1:2],
                               deps=(sm,), bufw=(b_mv4,))
                    pn = pb.op("scalar", "mul", pnr[:], pex[:], mv4[:, 2:3],
                               deps=(iv, ex), bufw=(b_pnr,),
                               bufr=(b_pex, b_mv4))
                    t8 = pb.op("vector", "max", top8[:], pnr[:], deps=(pn,),
                               bufw=(b_top8,), bufr=(b_pnr,))
                    msk = pb.op("vector", "tensor_single_scalar", wdn[:], pnr[:],
                                top8[:, 7:8], op=ALU.is_ge, deps=(t8,),
                                bufw=(b_wdn,), bufr=(b_pnr, b_top8))
                    wdm = pb.op("vector", "tensor_mul", wdn[:], wdn[:], pnr[:],
                                deps=(msk,), bufw=(b_wdn,), bufr=(b_pnr,))
                    ws = pb.op("vector", "reduce_sum", mv4[:, 3:4], wdn[:],
                               axis=AX.X, deps=(wdm,), bufw=(b_mv4,),
                               bufr=(b_wdn,))
                    sw = pb.op("tensor", "transpose", pk4[1][0:E, 0:128],
                               wdn[:], id32[:], deps=(wdm,), inc=True,
                               bufw=(b_pk4[1],), bufr=(b_wdn, b_consts))
                    ew1, ew2 = evac_hilo(pk4[1][0:E, 0:128],
                                         wdt_h[0:E, :], wdt_l[0:E, :],
                                         (b_pk4[1],), b_wdth, b_wdtl,
                                         deps=(sw,))
                    for ndt in range(D // 512):
                        bk = 2 + ndt % 2
                        pb.op("tensor", "matmul", pk4[bk][:, 0:512],
                              wdt_h[0:E, :], eb_h[0:E, ndt * 512:(ndt + 1) * 512],
                              start=True, stop=False,
                              deps=(ew1, ew2, b_eb_h.writer, b_eb_l.writer),
                              bufwar=(b_pk4[bk],))
                        pb.op("tensor", "matmul", pk4[bk][:, 0:512],
                              wdt_h[0:E, :], eb_l[0:E, ndt * 512:(ndt + 1) * 512],
                              start=False, stop=False)
                        mmb = pb.op("tensor", "matmul", pk4[bk][:, 0:512],
                                    wdt_l[0:E, :],
                                    eb_h[0:E, ndt * 512:(ndt + 1) * 512],
                                    start=False, stop=True, inc=True,
                                    bufw=(b_pk4[bk],),
                                    bufr=(b_wdth, b_wdtl, b_eb_h, b_eb_l))
                        pb.op("vector", "scalar_tensor_tensor",
                              ffnb[:, ndt * 512:(ndt + 1) * 512],
                              xh2[:, ndt * 512:(ndt + 1) * 512], mv4[:, 3:4],
                              pk4[bk][:, 0:512], op0=ALU.mult, op1=ALU.add,
                              deps=(mmb, ws), bufw=(b_ffn,),
                              bufr=(b_xh2, b_mv4, b_pk4[bk]))
                    s4 = ln4(lambda o, w: ffnb[:, o:o + w], D, (b_ffn,),
                             EPS_LN, (b_ffn.writer,), 4)
                    y1 = pb.op("vector", "tensor_scalar", xh2[:], ffnb[:],
                               mv4[:, 4:5], mv4[:, 7:8], op0=ALU.subtract,
                               op1=ALU.mult, deps=(s4,),
                               bufw=(b_xh2,), bufr=(b_ffn, b_mv4))
                    y2 = pb.op("vector", "tensor_add", xh2[:], xh2[:], x2b[:],
                               deps=(y1,), bufw=(b_xh2,), bufr=(b_x2,))
                    pb.op("scalar", "dma_start",
                          out=Y[towm * 128:(towm + 1) * 128, :],
                          in_=xh2[:], deps=(y2,), inc=True,
                          bufw=(b_y,), bufr=(b_xh2,))
            if phases < 5:
                z = pb.op("vector", "memset", xh2[:], 0.0, bufw=(b_xh2,))
                for towm in range(OWNT):
                    pb.op("scalar", "dma_start",
                          out=Y[towm * 128:(towm + 1) * 128, :], in_=xh2[:],
                          deps=(z,), inc=True, bufw=(b_y,), bufr=(b_xh2,))
            for sk, v in pb.cnt.items():
                if v > 0 and (sk.startswith("scalarq") or sk.startswith("gpsimd")
                              or sk.startswith("sync")):
                    pb._wait("gpsimd", sk, v)

        for e, v in pb.cnt.items():
            assert v < 60000, (e, v)

        blk = stack.enter_context(nc.Block())
        pb.realize(blk)

    nc.finalize()
    return nc



def make_shared_inputs(ins, cfg):
    g = derived(cfg)
    RC, KC, E = g["RC"], g["KC"], cfg["E"]
    cos, sin = yarn_cos_sin(cfg["S"], dim=64)
    tri = np.zeros((128, 128), np.float32)
    for r in range(128):
        tri[r, r + 1:] = NEG
    shared = dict(
        wqat=np.ascontiguousarray(np.asarray(ins["L0_attn_wq_a"], np.float32).T),
        wqbt=np.ascontiguousarray(np.asarray(ins["L0_attn_wq_b"], np.float32).T),
        wkvat=np.ascontiguousarray(np.asarray(ins["L0_attn_wkv_a"], np.float32).T),
        wkvbt=np.ascontiguousarray(np.asarray(ins["L0_attn_wkv_b"], np.float32).T),
        wot=np.ascontiguousarray(np.asarray(ins["L0_attn_wo"], np.float32).T),
        rwt=np.ascontiguousarray(np.asarray(ins["L0_ffn_router_weight"], np.float32).T),
        rb=np.ascontiguousarray(np.asarray(ins["L0_ffn_router_bias"], np.float32)
                                .reshape(1, E)),
        eb=np.ascontiguousarray(np.asarray(ins["L0_ffn_expert_bias"], np.float32)),
        qaw=np.ascontiguousarray(np.asarray(ins["L0_attn_q_a_ln_w"], np.float32)
                                 .reshape(RC, 128)),
        kvw=np.ascontiguousarray(np.asarray(ins["L0_attn_kv_a_ln_w"], np.float32)
                                 .reshape(KC, 128)),
        tri=tri,
        id32=np.eye(128, dtype=np.float32),
        idbf=np.eye(128, dtype=BF16NP),
        onesb=np.ones((1, 128), dtype=BF16NP),
        _cos32=np.ascontiguousarray(cos[:, :32]),
        _sin32=np.ascontiguousarray(sin[:, :32]),
    )
    return shared


def make_core_inputs(c, ins, cfg, shared):
    g = derived(cfg)
    BS = g["BS"]
    x = np.asarray(ins["L0_attn_x"], np.float32)
    blocks = core_blocks(c)
    xk = np.concatenate([x[b, j * BS:(j + 1) * BS] for b, j in blocks], axis=0)
    pos = np.concatenate([np.arange(j * BS, (j + 1) * BS) for b, j in blocks])
    sela = np.zeros((128, 9), np.float32)
    selb = np.zeros((128, 9), np.float32)
    for j in range(7):
        if j < c:
            selb[:, j] = NEG
        else:
            sela[:, j] = NEG
    sela[:, 8] = NEG
    selb[:, 7] = NEG
    H = cfg["H"]
    qsel = np.zeros((128, H * 7), np.int32)
    for h in range(H):
        for j in range(7):
            side = 0 if j < c else 1
            qsel[:, h * 7 + j] = h * 256 + side * 128 + np.arange(128)
    d = {k: v for k, v in shared.items() if not k.startswith("_")}
    d.update(
        xk=np.ascontiguousarray(xk, np.float32),
        cosk=np.ascontiguousarray(shared["_cos32"][pos]),
        sink=np.ascontiguousarray(shared["_sin32"][pos]),
        sela=sela, selb=selb, qsel=qsel,
    )
    return d


_PROG_CACHE = {}


def get_program():
    if "nc" not in _PROG_CACHE:
        _PROG_CACHE["nc"] = build_program(REAL)
    return _PROG_CACHE["nc"]


def kernel(**inputs):
    cfg = REAL
    g = derived(cfg)
    nc = get_program()
    shared = make_shared_inputs(inputs, cfg)
    in_maps = [make_core_inputs(c, inputs, cfg, shared) for c in range(8)]
    res = run_bass_kernel_spmd(nc, in_maps, core_ids=list(range(8)))
    BS = g["BS"]
    y = np.empty((2, cfg["S"], cfg["D"]), np.float32)
    for c in range(8):
        yc = np.asarray(res.results[c]["y"])
        y[0, c * BS:(c + 1) * BS] = yc[0:BS]
        y[1, (7 - c) * BS:(8 - c) * BS] = yc[BS:2 * BS]
    return y


# version 06
# speedup vs baseline: 1.3092x; 1.1584x over previous
import math
import numpy as np
import ml_dtypes
from contextlib import ExitStack

import concourse.bass as bass
import concourse.mybir as mybir
from concourse.bass import IndirectOffsetOnAxis
from concourse.bass_utils import run_bass_kernel_spmd

F32 = mybir.dt.float32
BF16 = mybir.dt.bfloat16
I32 = mybir.dt.int32
AF = mybir.ActivationFunctionType
ALU = mybir.AluOpType
AX = mybir.AxisListType
BF16NP = ml_dtypes.bfloat16

NEG = -1.0e30
EPS_RMS = 1.0e-6
EPS_LN = 1.0e-5

REAL = dict(D=4096, RQ=1024, KVR=256, H=32, S=4096, E=64)


def derived(cfg):
    g = dict(cfg)
    g["VD"] = 128
    g["KVD"] = 192
    g["NB"] = 8
    g["BS"] = cfg["S"] // 8
    assert g["BS"] % 128 == 0 and g["BS"] <= 512
    g["TPB"] = g["BS"] // 128
    g["NKEY"] = 9 * g["BS"]
    g["NKT"] = g["NKEY"] // 128
    g["OWN"] = 2 * g["BS"]
    g["OWNT"] = g["OWN"] // 128
    g["DC"] = cfg["D"] // 128
    g["RC"] = cfg["RQ"] // 128
    g["KC"] = cfg["KVR"] // 128
    g["WKA"] = cfg["KVR"] + 64
    g["MSC"] = 0.1 * math.log(128.0) + 1.0
    g["SCALE"] = (128.0 ** -0.5) * g["MSC"] * g["MSC"]
    g["ETW"] = min(512, cfg["H"] * 128)
    g["NET"] = cfg["H"] * 128 // g["ETW"]
    g["HPE"] = g["ETW"] // 128
    g["G1B"] = min(2, g["OWNT"])
    g["TQG"] = min(4, g["OWNT"])
    g["DQW"] = 1024 if cfg["D"] % 1024 == 0 else cfg["D"]
    assert g["TQG"] * (g["DQW"] // 512) <= 8
    return g


def yarn_cos_sin(seq_len, dim=64, base=10000.0, factor=128.0, beta_fast=32.0,
                 beta_slow=1.0, orig_max=8192, mscale=1.0, mscale_all_dim=1.0):
    def corr_dim(r):
        return dim * math.log(orig_max / (r * 2 * math.pi)) / (2 * math.log(base))
    low = max(math.floor(corr_dim(beta_fast)), 0)
    high = min(math.ceil(corr_dim(beta_slow)), dim - 1)
    hi = high + 0.001 if low == high else float(high)
    ramp = np.clip((np.arange(dim // 2, dtype=np.float32) - low) / (hi - low), 0.0, 1.0)
    inv_freq_mask = 1.0 - ramp
    freq_extra = 1.0 / base ** (np.arange(0, dim, 2, dtype=np.float32) / dim)
    freq_inter = freq_extra / factor
    inv_freq = freq_inter * (1.0 - inv_freq_mask) + freq_extra * inv_freq_mask
    t = np.arange(seq_len, dtype=np.float32)
    freqs = np.outer(t, inv_freq)
    emb = np.concatenate([freqs, freqs], axis=-1)

    def gm(s, m):
        return 1.0 if s <= 1 else 0.1 * m * math.log(s) + 1.0
    ms = gm(factor, mscale) / gm(factor, mscale_all_dim)
    return ((np.cos(emb) * np.float32(ms)).astype(np.float32),
            (np.sin(emb) * np.float32(ms)).astype(np.float32))


def core_blocks(c):
    return ([(0, j) for j in range(c)] + [(1, j) for j in range(7 - c)]
            + [(0, c), (1, 7 - c)])



class Buf:
    __slots__ = ("writers", "readers")

    def __init__(self):
        self.writers = []
        self.readers = []

    @property
    def writer(self):
        return self.writers[-1] if self.writers else None


class PB:
    ENGS = ("sync", "tensor", "vector", "scalar", "gpsimd")
    QPOOL = {"sync": 8, "scalarq": 8, "gpsimd": 4}

    def __init__(self, nc, stack):
        self.nc = nc
        self.q = {e: [] for e in self.ENGS}
        sems = ["tensor", "vector", "scalar"]
        for qn, k in self.QPOOL.items():
            sems += [f"{qn}{i}" for i in range(k)]
        self.sem = {s: stack.enter_context(nc.semaphore("sem_" + s)) for s in sems}
        self.cnt = {s: 0 for s in sems}
        self.qidx = {qn: 0 for qn in self.QPOOL}
        self.opidx = {e: 0 for e in self.ENGS}
        self.last_drain = {e: 0 for e in self.ENGS}
        self.hiw = {e: {} for e in self.ENGS}

    def _wait(self, eng, peng, val):
        if self.hiw[eng].get(peng, 0) >= val:
            return
        self.hiw[eng][peng] = val
        sem = self.sem[peng]
        self.opidx[eng] += 1
        self.q[eng].append(lambda e, sem=sem, val=val: e.wait_ge(sem, val))

    def _drain(self, eng):
        self.opidx[eng] += 1
        self.q[eng].append(lambda e: e.drain())
        self.last_drain[eng] = self.opidx[eng]

    def dep(self, eng, stamp):
        if stamp is None:
            return
        kind, peng, oidx, cnt = stamp
        if kind == "c" and peng == eng:
            if oidx > self.last_drain[eng]:
                self._drain(eng)
        else:
            assert cnt is not None, f"cross-engine dep {peng}->{eng} without inc"
            self._wait(eng, peng, cnt)

    def op(self, eng, fname, *args, deps=(), inc=None, bufw=(), bufr=(),
           bufd=(), bufwar=(), **kw):
        for st in deps:
            self.dep(eng, st)
        for b in tuple(bufw) + tuple(bufwar):
            if b.readers:
                for r in b.readers:
                    self.dep(eng, r)
                for w in b.writers:
                    self.dep(eng, w)
                b.readers = []
                b.writers = []
        for b in bufr:
            for w in b.writers:
                self.dep(eng, w)
        for b in bufd:
            for w in b.writers:
                self.dep(eng, w)
        if inc is None:
            inc = eng in ("vector", "scalar")
        self.opidx[eng] += 1
        oidx = self.opidx[eng]
        cnt = None
        isdma = "dma" in fname
        semkey = eng
        if isdma:
            qn = {"scalar": "scalarq", "sync": "sync",
                  "gpsimd": "gpsimd"}[eng]
            k = self.QPOOL[qn]
            semkey = f"{qn}{self.qidx[qn] % k}"
            self.qidx[qn] += 1
            if self.cnt[semkey] > 0:
                self._wait(eng, semkey, self.cnt[semkey])
            assert inc, "every DMA must inc"
        if inc:
            n = 16 if isdma else 1
            self.cnt[semkey] += n
            cnt = self.cnt[semkey]
            sem = self.sem[semkey]
            self.q[eng].append(
                lambda e, a=args, k=kw, f=fname, sem=sem, n=n:
                getattr(e, f)(*a, **k).then_inc(sem, n))
        else:
            self.q[eng].append(
                lambda e, a=args, k=kw, f=fname: getattr(e, f)(*a, **k))
        st = ("d" if isdma else "c", semkey, oidx, cnt)
        for b in bufw:
            b.writers.append(st)
        for b in bufr:
            b.readers.append(st)
        return st

    def barrier(self):
        for eng in self.ENGS:
            for sk, v in self.cnt.items():
                if sk != eng and v > 0:
                    self._wait(eng, sk, v)
            self._drain(eng)

    def realize(self, blk):
        for name in self.ENGS:
            ops = self.q[name]

            def runner(e, ops=ops):
                for f in ops:
                    f(e)
            getattr(blk, name)(runner)



def build_program(cfg, phases=5, sub=9):
    g = derived(cfg)
    D, RQ, KVR, H, E, S = cfg["D"], cfg["RQ"], cfg["KVR"], cfg["H"], cfg["E"], cfg["S"]
    VD, KVD = g["VD"], g["KVD"]
    BS, TPB, NKEY, NKT = g["BS"], g["TPB"], g["NKEY"], g["NKT"]
    OWN, OWNT = g["OWN"], g["OWNT"]
    DC, RC, KC, WKA = g["DC"], g["RC"], g["KC"], g["WKA"]
    SCALE = g["SCALE"]
    ETW, NET, HPE = g["ETW"], g["NET"], g["HPE"]
    G1B, TQG, DQW = g["G1B"], g["TQG"], g["DQW"]
    VST = 132
    OST = 132

    nc = bass.Bass("TRN2", target_bir_lowering=False, debug=False)

    def din(name, shape, dt=F32):
        return nc.dram_tensor(name, shape, dt, kind="ExternalInput")

    XK = din("xk", [NKEY, D])
    WQAT = din("wqat", [D, RQ])
    WQBT = din("wqbt", [RQ, H * 128])
    WKVAT = din("wkvat", [D, WKA])
    WKVBT = din("wkvbt", [KVR, H * KVD])
    WOT = din("wot", [H * VD, D])
    RWT = din("rwt", [D, E])
    RB = din("rb", [1, E])
    EB = din("eb", [E, D])
    QAW = din("qaw", [RC, 128])
    KVW = din("kvw", [KC, 128])
    COSK = din("cosk", [NKEY, 32])
    SINK = din("sink", [NKEY, 32])
    TRI = din("tri", [128, 128])
    ID32 = din("id32", [128, 128])
    IDBF = din("idbf", [128, 128], BF16)
    SELA = din("sela", [128, 9])
    SELB = din("selb", [128, 9])
    QSEL = din("qsel", [128, H * 7], I32)
    ONESB = din("onesb", [1, 128], BF16)

    Y = nc.dram_tensor("y", [OWN, D], F32, kind="ExternalOutput")

    qsp_hi = nc.dram_tensor("qsp_hi", [H, 256, TPB * 128], BF16, kind="Internal")
    qsp_lo = nc.dram_tensor("qsp_lo", [H, 256, TPB * 128], BF16, kind="Internal")
    osp_hi = nc.dram_tensor("osp_hi", [H, OWNT, 128, 128], BF16, kind="Internal")
    osp_lo = nc.dram_tensor("osp_lo", [H, OWNT, 128, 128], BF16, kind="Internal")
    wkva_hi = nc.dram_tensor("wkva_hi", [D, WKA], BF16, kind="Internal")
    wkva_lo = nc.dram_tensor("wkva_lo", [D, WKA], BF16, kind="Internal")
    wkvb_hi = nc.dram_tensor("wkvb_hi", [KVR, H * KVD], BF16, kind="Internal")
    wkvb_lo = nc.dram_tensor("wkvb_lo", [KVR, H * KVD], BF16, kind="Internal")

    stack = ExitStack()
    with stack:
        pb = PB(nc, stack)
        B_ = Buf

        def SBx(st, name, shape, dt=F32):
            return st.enter_context(nc.sbuf_tensor(name, shape, dt))

        def PSx(st, name, shape, dt=F32):
            return st.enter_context(nc.psum_tensor(name, shape, dt))

        cosk = SBx(stack, "cosk_sb", [128, NKT * 32])
        sink = SBx(stack, "sink_sb", [128, NKT * 32])
        tri = SBx(stack, "tri_sb", [128, 128])
        id32 = SBx(stack, "id32_sb", [128, 128])
        idbf = SBx(stack, "idbf_sb", [128, 128], BF16)
        sela = SBx(stack, "sela_sb", [128, 9])
        selb = SBx(stack, "selb_sb", [128, 9])
        qsel = SBx(stack, "qsel_sb", [128, H * 7], I32)
        onesb = SBx(stack, "onesb_sb", [1, 128], BF16)
        qaw = SBx(stack, "qaw_sb", [128, RC])
        kvw = SBx(stack, "kvw_sb", [128, KC])
        b_consts = B_()
        b_qsp, b_osp = B_(), B_()
        b_wkva, b_wkvb = B_(), B_()
        b_y = B_()

        def ldc(out_ap, in_ap):
            return pb.op("sync", "dma_start", out=out_ap, in_=in_ap,
                         inc=True, bufw=(b_consts,))

        ckv_view = COSK.ap().rearrange("(n p) w -> n p w", p=128)
        skv_view = SINK.ap().rearrange("(n p) w -> n p w", p=128)
        for n in range(NKT):
            ldc(cosk[:, n * 32:(n + 1) * 32], ckv_view[n])
            ldc(sink[:, n * 32:(n + 1) * 32], skv_view[n])
        ldc(tri[:], TRI[:])
        ldc(id32[:], ID32[:])
        ldc(idbf[:], IDBF[:])
        ldc(sela[:], SELA[:])
        ldc(selb[:], SELB[:])
        ldc(qsel[:], QSEL[:])
        ldc(onesb[:], ONESB[:])
        pb.op("sync", "dma_start", out=qaw[:],
              in_=QAW.ap().rearrange("a b -> b a"), inc=True,
              bufw=(b_consts,), allow_slow_non_contiguous=True)
        pb.op("sync", "dma_start", out=kvw[:],
              in_=KVW.ap().rearrange("a b -> b a"), inc=True,
              bufw=(b_consts,), allow_slow_non_contiguous=True)

        def evac_hilo(src_ap, hi_ap, lo_ap, src_bufs, hi_buf, lo_buf, deps=(),
                      scale=1.0):
            plain = isinstance(scale, float) and scale == 1.0
            if plain:
                h = pb.op("scalar", "copy", hi_ap, src_ap, deps=deps,
                          bufw=(hi_buf,), bufr=src_bufs)
            else:
                h = pb.op("scalar", "activation", hi_ap, src_ap, AF.Copy, scale=scale, deps=deps,
                          bufw=(hi_buf,), bufr=src_bufs)
            l = pb.op("vector", "scalar_tensor_tensor", lo_ap, src_ap,
                      1.0 if plain else scale, hi_ap,
                      op0=ALU.mult, op1=ALU.subtract, deps=deps + (h,),
                      bufw=(lo_buf,), bufr=src_bufs + (hi_buf,))
            return h, l

        with ExitStack() as ph0:
            wsf = SBx(ph0, "wsf", [128, 2048])
            wsh = SBx(ph0, "wsh", [128, 2048], BF16)
            wsl = SBx(ph0, "wsl", [128, 2048], BF16)
            b_wsf, b_wsh, b_wsl = B_(), B_(), B_()

            def presplit(src, dst_hi, dst_lo, rows, cols, bdst):
                assert rows % 128 == 0
                for rchunk in range(rows // 128):
                    for c0 in range(0, cols, 2048):
                        w = min(2048, cols - c0)
                        s = pb.op("sync", "dma_start", out=wsf[:, 0:w],
                                  in_=src[rchunk * 128:(rchunk + 1) * 128,
                                          c0:c0 + w],
                                  inc=True, bufw=(b_wsf,))
                        h, l = evac_hilo(wsf[:, 0:w], wsh[:, 0:w], wsl[:, 0:w],
                                         (b_wsf,), b_wsh, b_wsl, deps=(s,))
                        pb.op("scalar", "dma_start",
                              out=dst_hi[rchunk * 128:(rchunk + 1) * 128,
                                         c0:c0 + w],
                              in_=wsh[:, 0:w], deps=(h,), inc=True,
                              bufw=(bdst,), bufr=(b_wsh,))
                        pb.op("scalar", "dma_start",
                              out=dst_lo[rchunk * 128:(rchunk + 1) * 128,
                                         c0:c0 + w],
                              in_=wsl[:, 0:w], deps=(l,), inc=True,
                              bufw=(bdst,), bufr=(b_wsl,))

            if phases >= 1:
                presplit(WKVAT, wkva_hi, wkva_lo, D, WKA, b_wkva)
                presplit(WKVBT, wkvb_hi, wkvb_lo, KVR, H * KVD, b_wkvb)
            pb.barrier()

        with ExitStack() as s123:
            cpk_hi = SBx(s123, "cpk_hi", [128, KC * NKEY], BF16)
            cpk_lo = SBx(s123, "cpk_lo", [128, KC * NKEY], BF16)
            kpeT_hi = SBx(s123, "kpeT_hi", [128, NKEY], BF16)
            kpeT_lo = SBx(s123, "kpeT_lo", [128, NKEY], BF16)
            cq_hi = SBx(s123, "cq_hi", [128, RC * OWN], BF16)
            cq_lo = SBx(s123, "cq_lo", [128, RC * OWN], BF16)
            b_cpk_h, b_cpk_l = B_(), B_()
            b_kpeT_h, b_kpeT_l = B_(), B_()
            b_cq_h, b_cq_l = B_(), B_()
            pbank = [PSx(s123, f"pb{i}", [128, 512]) for i in range(6)]
            b_pb = [B_() for _ in range(6)]
            pbb = [PSx(s123, f"pbb{i}", [128, 512]) for i in range(2)]
            b_pbb = [B_() for _ in range(2)]

            statb2 = [SBx(s123, f"statb{i}", [128, 8 * 6]) for i in range(2)]
            mvv2 = [SBx(s123, f"mvv{i}", [128, 8]) for i in range(2)]
            b_stat2 = [B_(), B_()]
            b_mvv2 = [B_(), B_()]
            def set_stat(par):
                return dict(statb=statb2[par], mvv=mvv2[par],
                            b_stat=b_stat2[par], b_mvv=b_mvv2[par])

            CUR = {}

            def invsigma_ops(meansq_src_col, eps, dst_col, deps):
                mvv, b_mvv = CUR["mvv"], CUR["b_mvv"]
                """mvv[dst] = 1/sqrt(mvv[src]+eps) via exp(-0.5 ln(x+eps))."""
                s0 = pb.op("vector", "tensor_scalar_add",
                           mvv[:, dst_col:dst_col + 1],
                           mvv[:, meansq_src_col:meansq_src_col + 1],
                           float(eps), deps=deps, bufw=(b_mvv,), bufr=(b_mvv,))
                s1 = pb.op("scalar", "activation", mvv[:, dst_col:dst_col + 1],
                           mvv[:, dst_col:dst_col + 1], AF.Ln,
                           scale=1.0, deps=(s0,),
                           bufw=(b_mvv,), bufr=(b_mvv,))
                return pb.op("scalar", "activation", mvv[:, dst_col:dst_col + 1],
                             mvv[:, dst_col:dst_col + 1], AF.Exp,
                             scale=-0.5, deps=(s1,),
                             bufw=(b_mvv,), bufr=(b_mvv,))

            def ln_stats(src_ap_fn, ncol, src_bufs, eps, deps):
                statb, mvv = CUR["statb"], CUR["mvv"]
                b_stat, b_mvv = CUR["b_stat"], CUR["b_mvv"]
                nch = (ncol + 511) // 512
                sts = None
                for ci in range(nch):
                    w = min(512, ncol - ci * 512)
                    sts = pb.op("vector", "bn_stats", statb[:, ci * 6:(ci + 1) * 6],
                                src_ap_fn(ci * 512, w), deps=tuple(deps),
                                bufw=(b_stat,), bufr=src_bufs)
                sa = pb.op("vector", "bn_aggr", mvv[:, 0:2], statb[:, 0:6 * nch],
                           deps=(sts,), bufw=(b_mvv,), bufr=(b_stat,))
                return invsigma_ops(1, eps, 3, (sa,))

            def rms_stats(src_ap_fn, ncol, src_bufs, eps, deps):
                statb, mvv = CUR["statb"], CUR["mvv"]
                b_stat, b_mvv = CUR["b_stat"], CUR["b_mvv"]
                nch = (ncol + 511) // 512
                sts = None
                for ci in range(nch):
                    w = min(512, ncol - ci * 512)
                    sts = pb.op("vector", "bn_stats", statb[:, ci * 6:(ci + 1) * 6],
                                src_ap_fn(ci * 512, w), deps=tuple(deps),
                                bufw=(b_stat,), bufr=src_bufs)
                sa = pb.op("vector", "bn_aggr", mvv[:, 0:2], statb[:, 0:6 * nch],
                           deps=(sts,), bufw=(b_mvv,), bufr=(b_stat,))
                sm = pb.op("vector", "scalar_tensor_tensor", mvv[:, 4:5],
                           mvv[:, 0:1], mvv[:, 0:1], mvv[:, 1:2],
                           op0=ALU.mult, op1=ALU.add, deps=(sa,),
                           bufw=(b_mvv,), bufr=(b_mvv,))
                return invsigma_ops(4, eps, 6, (sm,))

            with ExitStack() as ph1:
                xb = [SBx(ph1, f"xb{i}", [128, D]) for i in range(2)]
                b_xb = [B_(), B_()]
                xhat2 = [SBx(ph1, f"xhat{i}", [128, D]) for i in range(2)]
                b_xhat2 = [B_(), B_()]
                xfm_h = [SBx(ph1, f"xfm_h{i}", [128, 128], BF16) for i in range(4)]
                xfm_l = [SBx(ph1, f"xfm_l{i}", [128, 128], BF16) for i in range(4)]
                b_xfm_h = [B_() for _ in range(4)]
                b_xfm_l = [B_() for _ in range(4)]
                wa_h = [SBx(ph1, f"wa_h{i}", [128, WKA], BF16) for i in range(2)]
                wa_l = [SBx(ph1, f"wa_l{i}", [128, WKA], BF16) for i in range(2)]
                b_wa_h = [B_(), B_()]
                b_wa_l = [B_(), B_()]
                cbuf = SBx(ph1, "cbuf", [128, KVR])
                b_cbuf = B_()
                krop = SBx(ph1, "krop", [128, 64])
                ktmp = SBx(ph1, "ktmp", [128, 64])
                b_krop, b_ktmp = B_(), B_()
                kst_h = SBx(ph1, "kst_h", [128, 128], BF16)
                kst_l = SBx(ph1, "kst_l", [128, 128], BF16)
                b_kst_h, b_kst_l = B_(), B_()

                for kt in range(NKT if phases >= 2 else 0):
                    bx = kt % 2
                    CUR.update(set_stat(bx))
                    statb, mvv = CUR["statb"], CUR["mvv"]
                    b_stat, b_mvv = CUR["b_stat"], CUR["b_mvv"]
                    xhat, b_xhat = xhat2[bx], b_xhat2[bx]
                    lx = pb.op("sync", "dma_start", out=xb[bx][:],
                               in_=XK[kt * 128:(kt + 1) * 128, :],
                               inc=True, bufw=(b_xb[bx],))
                    siv = ln_stats(lambda o, w, _b=xb[bx]: _b[:, o:o + w],
                                   D, (b_xb[bx],), EPS_LN, (lx,))
                    sx = pb.op("vector", "tensor_scalar", xhat[:], xb[bx][:],
                               mvv[:, 0:1], mvv[:, 3:4],
                               op0=ALU.subtract, op1=ALU.mult,
                               deps=(siv,), bufw=(b_xhat,),
                               bufr=(b_xb[bx], b_mvv))
                    pc = kt % 2

                    def mm_ckv(dc, st):
                        pt, eh, el, lwh, lwl = st
                        first = dc == 0
                        last = dc == DC - 1
                        pb.op("tensor", "matmul", pbank[pc][:, 0:WKA],
                              xfm_h[pt][:], wa_h[pt][:], start=first, stop=False,
                              deps=(eh, lwh, lwl),
                              bufwar=(b_pb[pc],) if first else ())
                        pb.op("tensor", "matmul", pbank[pc][:, 0:WKA],
                              xfm_h[pt][:], wa_l[pt][:], start=False, stop=False)
                        pb.op("tensor", "matmul", pbank[pc][:, 0:WKA],
                              xfm_l[pt][:], wa_h[pt][:], start=False, stop=last,
                              deps=(el,), inc=True,
                              bufw=(b_pb[pc],) if last else (),
                              bufr=(b_xfm_h[pt], b_xfm_l[pt], b_wa_h[pt],
                                    b_wa_l[pt]))

                    prev_ckv = None
                    for dc in range(DC):
                        pt = dc % 2
                        lwh = pb.op("sync", "dma_start", out=wa_h[pt][:],
                                    in_=wkva_hi[dc * 128:(dc + 1) * 128, :],
                                    inc=True, bufw=(b_wa_h[pt],), bufr=(b_wkva,))
                        lwl = pb.op("sync", "dma_start", out=wa_l[pt][:],
                                    in_=wkva_lo[dc * 128:(dc + 1) * 128, :],
                                    inc=True, bufw=(b_wa_l[pt],), bufr=(b_wkva,))
                        stp = pb.op("tensor", "transpose", pbank[4 + pt][:, 0:128],
                                    xhat[:, dc * 128:(dc + 1) * 128], id32[:],
                                    deps=(sx,), inc=True,
                                    bufw=(b_pb[4 + pt],), bufr=(b_xhat, b_consts))
                        eh, el = evac_hilo(pbank[4 + pt][:, 0:128],
                                           xfm_h[pt][:], xfm_l[pt][:],
                                           (b_pb[4 + pt],), b_xfm_h[pt],
                                           b_xfm_l[pt], deps=(stp,))
                        if prev_ckv is not None:
                            mm_ckv(dc - 1, prev_ckv)
                        prev_ckv = (pt, eh, el, lwh, lwl)
                    mm_ckv(DC - 1, prev_ckv)
                    mm_done = b_pb[pc].writer
                    sr = rms_stats(lambda o, w, _p=pbank[pc]: _p[:, o:o + w],
                                   KVR, (b_pb[pc],), EPS_RMS, (mm_done,))
                    sc = pb.op("scalar", "activation", cbuf[:], pbank[pc][:, 0:KVR],
                               AF.Copy, scale=mvv[:, 6:7],
                               deps=(sr, mm_done),
                               bufw=(b_cbuf,), bufr=(b_pb[pc], b_mvv))
                    ap_ev = bass.AP(pbank[pc], KVR, [[512, 128], [2, 32]])
                    ap_od = bass.AP(pbank[pc], KVR + 1, [[512, 128], [2, 32]])
                    cos_t = cosk[:, kt * 32:(kt + 1) * 32]
                    sin_t = sink[:, kt * 32:(kt + 1) * 32]
                    r1 = pb.op("vector", "tensor_tensor", krop[:, 0:32], ap_ev,
                               cos_t, op=ALU.mult, deps=(mm_done,),
                               bufw=(b_krop,), bufr=(b_pb[pc], b_consts))
                    r2 = pb.op("vector", "tensor_tensor", krop[:, 32:64], ap_od,
                               cos_t, op=ALU.mult, deps=(mm_done,),
                               bufw=(b_krop,), bufr=(b_pb[pc], b_consts))
                    r3 = pb.op("vector", "tensor_tensor", ktmp[:, 0:32], ap_od,
                               sin_t, op=ALU.mult, deps=(mm_done,),
                               bufw=(b_ktmp,), bufr=(b_pb[pc], b_consts))
                    r4 = pb.op("vector", "tensor_tensor", ktmp[:, 32:64], ap_ev,
                               sin_t, op=ALU.mult, deps=(mm_done,),
                               bufw=(b_ktmp,), bufr=(b_pb[pc], b_consts))
                    r5 = pb.op("vector", "tensor_sub", krop[:, 0:32], krop[:, 0:32],
                               ktmp[:, 0:32], deps=(r1, r3),
                               bufw=(b_krop,), bufr=(b_ktmp,))
                    r6 = pb.op("vector", "tensor_add", krop[:, 32:64], krop[:, 32:64],
                               ktmp[:, 32:64], deps=(r2, r4),
                               bufw=(b_krop,), bufr=(b_ktmp,))
                    for rc in range(KC):
                        pt = rc % 2
                        stc = pb.op("tensor", "transpose", pbank[4 + pt][:, 0:128],
                                    cbuf[:, rc * 128:(rc + 1) * 128], id32[:],
                                    deps=(sc,), inc=True,
                                    bufw=(b_pb[4 + pt],), bufr=(b_cbuf, b_consts))
                        col = rc * NKEY + kt * 128
                        evac_hilo(pbank[4 + pt][:, 0:128],
                                  cpk_hi[:, col:col + 128], cpk_lo[:, col:col + 128],
                                  (b_pb[4 + pt],), b_cpk_h, b_cpk_l, deps=(stc,),
                                  scale=kvw[:, rc:rc + 1])
                    ptk = 4 + kt % 2
                    stk = pb.op("tensor", "transpose", pbank[ptk][0:64, 0:128],
                                krop[:, 0:64], id32[:], deps=(r5, r6), inc=True,
                                bufw=(b_pb[ptk],), bufr=(b_krop, b_consts))
                    kh, kl = evac_hilo(pbank[ptk][0:64, 0:128],
                                       kst_h[0:64, :], kst_l[0:64, :],
                                       (b_pb[ptk],), b_kst_h, b_kst_l,
                                       deps=(stk,))
                    pb.op("scalar", "dma_start",
                          out=kpeT_hi[64:128, kt * 128:(kt + 1) * 128],
                          in_=kst_h[0:64, :], deps=(kh,), inc=True,
                          bufw=(b_kpeT_h,), bufr=(b_kst_h,))
                    pb.op("scalar", "dma_start",
                          out=kpeT_lo[64:128, kt * 128:(kt + 1) * 128],
                          in_=kst_l[0:64, :], deps=(kl,), inc=True,
                          bufw=(b_kpeT_l,), bufr=(b_kst_l,))

                wqa_f = [SBx(ph1, f"wqa_f{i}", [128, RQ]) for i in range(2)]
                wqa_h = [SBx(ph1, f"wqa_h{i}", [128, RQ], BF16) for i in range(2)]
                wqa_l = [SBx(ph1, f"wqa_l{i}", [128, RQ], BF16) for i in range(2)]
                b_wqa_f = [B_(), B_()]
                b_wqa_h = [B_(), B_()]
                b_wqa_l = [B_(), B_()]
                xhg = [xhat2[i % 2] for i in range(G1B)]
                b_xhg = [b_xhat2[i % 2] for i in range(G1B)]
                cqt2 = [SBx(ph1, f"cqt{i}", [128, RQ]) for i in range(2)]
                b_cqt2 = [B_(), B_()]
                NQB = max(1, RQ // 512)
                QW = min(512, RQ)

                for grp in range(OWNT // G1B if phases >= 2 else 0):
                    tiles = [grp * G1B + i for i in range(G1B)]
                    for gi, towm in enumerate(tiles):
                        kt = 7 * TPB + towm
                        bx = towm % 2
                        CUR.update(set_stat(bx))
                        statb, mvv = CUR["statb"], CUR["mvv"]
                        b_stat, b_mvv = CUR["b_stat"], CUR["b_mvv"]
                        lx = pb.op("sync", "dma_start", out=xb[bx][:],
                                   in_=XK[kt * 128:(kt + 1) * 128, :],
                                   inc=True, bufw=(b_xb[bx],))
                        siv = ln_stats(lambda o, w, _b=xb[bx]: _b[:, o:o + w],
                                       D, (b_xb[bx],), EPS_LN, (lx,))
                        pb.op("vector", "tensor_scalar", xhg[gi][:], xb[bx][:],
                              mvv[:, 0:1], mvv[:, 3:4],
                              op0=ALU.subtract, op1=ALU.mult,
                              deps=(siv,), bufw=(b_xhg[gi],),
                              bufr=(b_xb[bx], b_mvv))
                    def mm_cq(dc, st):
                        pt, gi, xi, eh, el, wh, wl = st
                        first = dc == 0
                        last = dc == DC - 1
                        for nq in range(NQB):
                            bk = gi * NQB + nq
                            pb.op("tensor", "matmul", pbank[bk][:, 0:QW],
                                  xfm_h[xi][:],
                                  wqa_h[pt][:, nq * 512:nq * 512 + QW],
                                  start=first, stop=False,
                                  deps=(eh, wh, wl),
                                  bufwar=(b_pb[bk],) if first else ())
                            pb.op("tensor", "matmul", pbank[bk][:, 0:QW],
                                  xfm_h[xi][:],
                                  wqa_l[pt][:, nq * 512:nq * 512 + QW],
                                  start=False, stop=False)
                            lastnq = nq == NQB - 1
                            pb.op("tensor", "matmul", pbank[bk][:, 0:QW],
                                  xfm_l[xi][:],
                                  wqa_h[pt][:, nq * 512:nq * 512 + QW],
                                  start=False, stop=last,
                                  deps=(el,), inc=lastnq or last,
                                  bufw=(b_pb[bk],) if last else (),
                                  bufr=(b_xfm_h[xi], b_xfm_l[xi],
                                        b_wqa_h[pt], b_wqa_l[pt])
                                  if lastnq else ())

                    prev_cq = None
                    for dc in range(DC):
                        pt = dc % 2
                        lw = pb.op("sync", "dma_start", out=wqa_f[pt][:],
                                   in_=WQAT[dc * 128:(dc + 1) * 128, :],
                                   inc=True, bufw=(b_wqa_f[pt],))
                        wh, wl = evac_hilo(wqa_f[pt][:], wqa_h[pt][:], wqa_l[pt][:],
                                           (b_wqa_f[pt],), b_wqa_h[pt],
                                           b_wqa_l[pt], deps=(lw,))
                        for gi in range(G1B):
                            xi = pt * 2 + gi
                            stp = pb.op("tensor", "transpose",
                                        pbank[4 + pt][:, 0:128],
                                        xhg[gi][:, dc * 128:(dc + 1) * 128],
                                        id32[:], deps=(b_xhg[gi].writer,),
                                        inc=True, bufw=(b_pb[4 + pt],),
                                        bufr=(b_xhg[gi], b_consts))
                            eh, el = evac_hilo(pbank[4 + pt][:, 0:128],
                                               xfm_h[xi][:], xfm_l[xi][:],
                                               (b_pb[4 + pt],), b_xfm_h[xi],
                                               b_xfm_l[xi], deps=(stp,))
                            if prev_cq is not None:
                                mm_cq(*prev_cq)
                            prev_cq = (dc, (pt, gi, xi, eh, el, wh, wl))
                    mm_cq(*prev_cq)
                    for gi, towm in enumerate(tiles):
                        CUR.update(set_stat(towm % 2))
                        statb, mvv = CUR["statb"], CUR["mvv"]
                        b_stat, b_mvv = CUR["b_stat"], CUR["b_mvv"]
                        cqt, b_cqt = cqt2[towm % 2], b_cqt2[towm % 2]
                        bks = [gi * NQB + nq for nq in range(NQB)]
                        mmst = tuple(b_pb[bk].writer for bk in bks)
                        sts = None
                        for nq, bk in enumerate(bks):
                            sts = pb.op("vector", "bn_stats",
                                        statb[:, nq * 6:(nq + 1) * 6],
                                        pbank[bk][:, 0:QW], deps=mmst,
                                        bufw=(b_stat,), bufr=(b_pb[bk],))
                        sa = pb.op("vector", "bn_aggr", mvv[:, 0:2],
                                   statb[:, 0:6 * NQB], deps=(sts,),
                                   bufw=(b_mvv,), bufr=(b_stat,))
                        sm = pb.op("vector", "scalar_tensor_tensor", mvv[:, 4:5],
                                   mvv[:, 0:1], mvv[:, 0:1], mvv[:, 1:2],
                                   op0=ALU.mult, op1=ALU.add, deps=(sa,),
                                   bufw=(b_mvv,), bufr=(b_mvv,))
                        s2 = invsigma_ops(4, EPS_RMS, 6, (sm,))
                        scs = None
                        for nq, bk in enumerate(bks):
                            scs = pb.op("scalar", "activation",
                                        cqt[:, nq * 512:nq * 512 + QW],
                                        pbank[bk][:, 0:QW], AF.Copy,
                                        scale=mvv[:, 6:7],
                                        deps=(s2,) + mmst,
                                        bufw=(b_cqt,), bufr=(b_pb[bk], b_mvv))
                        for rc in range(RC):
                            pt = rc % 2
                            stq = pb.op("tensor", "transpose",
                                        pbank[4 + pt][:, 0:128],
                                        cqt[:, rc * 128:(rc + 1) * 128], id32[:],
                                        deps=(scs,), inc=True,
                                        bufw=(b_pb[4 + pt],),
                                        bufr=(b_cqt, b_consts))
                            col = rc * OWN + towm * 128
                            evac_hilo(pbank[4 + pt][:, 0:128],
                                      cq_hi[:, col:col + 128],
                                      cq_lo[:, col:col + 128],
                                      (b_pb[4 + pt],), b_cq_h, b_cq_l,
                                      deps=(stq,), scale=qaw[:, rc:rc + 1])
                pb.barrier()

            with ExitStack() as ph2:
                wqb_f = [SBx(ph2, f"wqb_f{i}", [128, RC * ETW]) for i in range(2)]
                wqb_h = [SBx(ph2, f"wqb_h{i}", [128, RC * ETW], BF16) for i in range(2)]
                wqb_l = [SBx(ph2, f"wqb_l{i}", [128, RC * ETW], BF16) for i in range(2)]
                b_wqb_f = [B_(), B_()]
                b_wqb_h = [B_(), B_()]
                b_wqb_l = [B_(), B_()]
                qtmp2 = [SBx(ph2, f"qtmp{i}", [128, ETW]) for i in range(2)]
                b_qtmp2 = [B_(), B_()]
                qrt2 = [SBx(ph2, f"qrt{i}", [128, HPE * 64]) for i in range(2)]
                b_qrt2 = [B_(), B_()]
                qtile2 = [SBx(ph2, f"qtile{i}", [128, ETW]) for i in range(2)]
                b_qtile2 = [B_(), B_()]
                qst_h2 = [SBx(ph2, f"qst_h{i}", [128, 128], BF16) for i in range(2)]
                qst_l2 = [SBx(ph2, f"qst_l{i}", [128, 128], BF16) for i in range(2)]
                b_qst_h2 = [B_(), B_()]
                b_qst_l2 = [B_(), B_()]
                wqbt_v = WQBT.ap().rearrange("(c p) w -> p c w", p=128)

                for et in range(NET if phases >= 3 else 0):
                    wb = et % 2
                    lw = pb.op("sync", "dma_start",
                               out=wqb_f[wb].ap().rearrange("p (c w) -> p c w", c=RC),
                               in_=wqbt_v[:, :, et * ETW:(et + 1) * ETW],
                               inc=True, bufw=(b_wqb_f[wb],))
                    wh, wl = evac_hilo(wqb_f[wb][:], wqb_h[wb][:], wqb_l[wb][:],
                                       (b_wqb_f[wb],), b_wqb_h[wb], b_wqb_l[wb],
                                       deps=(lw,))
                    def q_back(towm, mmq):
                        pc = towm % 2
                        qtmp, b_qtmp = qtmp2[pc], b_qtmp2[pc]
                        qrt, b_qrt = qrt2[pc], b_qrt2[pc]
                        qtile, b_qtile = qtile2[pc], b_qtile2[pc]
                        qst_h, b_qst_h = qst_h2[pc], b_qst_h2[pc]
                        qst_l, b_qst_l = qst_l2[pc], b_qst_l2[pc]
                        sq = pb.op("scalar", "activation", qtmp[:, 0:ETW],
                                   pbank[pc][:, 0:ETW], AF.Copy,
                                   scale=SCALE, deps=(mmq,),
                                   bufw=(b_qtmp,), bufr=(b_pb[pc],))
                        ap_nope_i = bass.AP(qtmp, 0, [[ETW, 128], [128, HPE], [1, 64]])
                        ap_nope_o = bass.AP(qtile, 0, [[ETW, 128], [128, HPE], [1, 64]])
                        c0 = pb.op("vector", "tensor_copy", out=ap_nope_o,
                                   in_=ap_nope_i, deps=(sq,),
                                   bufw=(b_qtile,), bufr=(b_qtmp,))
                        ap_ev = bass.AP(qtmp, 64, [[ETW, 128], [128, HPE], [2, 32]])
                        ap_od = bass.AP(qtmp, 65, [[ETW, 128], [128, HPE], [2, 32]])
                        kt_own = 7 * TPB + towm
                        cos_t = bass.AP(cosk, kt_own * 32,
                                        [[NKT * 32, 128], [0, HPE], [1, 32]])
                        sin_t = bass.AP(sink, kt_own * 32,
                                        [[NKT * 32, 128], [0, HPE], [1, 32]])
                        o1 = bass.AP(qtile, 64, [[ETW, 128], [128, HPE], [1, 32]])
                        o2 = bass.AP(qtile, 96, [[ETW, 128], [128, HPE], [1, 32]])
                        t3 = bass.AP(qrt, 0, [[HPE * 64, 128], [64, HPE], [1, 32]])
                        t4 = bass.AP(qrt, 32, [[HPE * 64, 128], [64, HPE], [1, 32]])
                        g1 = pb.op("vector", "tensor_tensor", o1, ap_ev, cos_t,
                                   op=ALU.mult, deps=(sq,), bufw=(b_qtile,),
                                   bufr=(b_qtmp, b_consts))
                        g2 = pb.op("vector", "tensor_tensor", o2, ap_od, cos_t,
                                   op=ALU.mult, deps=(sq,), bufw=(b_qtile,),
                                   bufr=(b_qtmp, b_consts))
                        g3 = pb.op("vector", "tensor_tensor", t3, ap_od, sin_t,
                                   op=ALU.mult, deps=(sq,), bufw=(b_qrt,),
                                   bufr=(b_qtmp, b_consts))
                        g4 = pb.op("vector", "tensor_tensor", t4, ap_ev, sin_t,
                                   op=ALU.mult, deps=(sq,), bufw=(b_qrt,),
                                   bufr=(b_qtmp, b_consts))
                        g5 = pb.op("vector", "tensor_sub", o1, o1, t3,
                                   deps=(g1, g3), bufw=(b_qtile,), bufr=(b_qrt,))
                        g6 = pb.op("vector", "tensor_add", o2, o2, t4,
                                   deps=(g2, g4, g5), bufw=(b_qtile,),
                                   bufr=(b_qrt,))
                        side = towm // TPB
                        itile = towm % TPB
                        for hh in range(HPE):
                            h = et * HPE + hh
                            pt2 = hh % 2
                            stq = pb.op("tensor", "transpose",
                                        pbank[4 + pt2][:, 0:128],
                                        qtile[:, hh * 128:(hh + 1) * 128],
                                        id32[:], deps=(g6, c0), inc=True,
                                        bufw=(b_pb[4 + pt2],),
                                        bufr=(b_qtile, b_consts))
                            qh, ql = evac_hilo(pbank[4 + pt2][:, 0:128],
                                               qst_h[:], qst_l[:],
                                               (b_pb[4 + pt2],), b_qst_h,
                                               b_qst_l, deps=(stq,))
                            pb.op("scalar", "dma_start",
                                  out=qsp_hi[h, side * 128:(side + 1) * 128,
                                             itile * 128:(itile + 1) * 128],
                                  in_=qst_h[:], deps=(qh,), inc=True,
                                  bufw=(b_qsp,), bufr=(b_qst_h,))
                            pb.op("scalar", "dma_start",
                                  out=qsp_lo[h, side * 128:(side + 1) * 128,
                                             itile * 128:(itile + 1) * 128],
                                  in_=qst_l[:], deps=(ql,), inc=True,
                                  bufw=(b_qsp,), bufr=(b_qst_l,))

                    prev_q = None
                    for towm in range(OWNT):
                        pc = towm % 2
                        for rc in range(RC):
                            first = rc == 0
                            last = rc == RC - 1
                            cq_sl = slice(rc * OWN + towm * 128,
                                          rc * OWN + (towm + 1) * 128)
                            pb.op("tensor", "matmul", pbank[pc][:, 0:ETW],
                                  cq_hi[:, cq_sl],
                                  wqb_h[wb][:, rc * ETW:(rc + 1) * ETW],
                                  start=first, stop=False,
                                  deps=(wh, wl, b_cq_h.writer, b_cq_l.writer),
                                  bufwar=(b_pb[pc],) if first else ())
                            pb.op("tensor", "matmul", pbank[pc][:, 0:ETW],
                                  cq_hi[:, cq_sl],
                                  wqb_l[wb][:, rc * ETW:(rc + 1) * ETW],
                                  start=False, stop=False)
                            pb.op("tensor", "matmul", pbank[pc][:, 0:ETW],
                                  cq_lo[:, cq_sl],
                                  wqb_h[wb][:, rc * ETW:(rc + 1) * ETW],
                                  start=False, stop=last, inc=last,
                                  bufw=(b_pb[pc],) if last else (),
                                  bufr=(b_cq_h, b_cq_l, b_wqb_h[wb],
                                        b_wqb_l[wb]) if last else ())
                        mmq = b_pb[pc].writer
                        if prev_q is not None:
                            q_back(*prev_q)
                        prev_q = (towm, mmq)
                    q_back(*prev_q)
                pb.barrier()

            with ExitStack() as ph3:
                v_hi = SBx(ph3, "v_hi", [128, NKT * VST], BF16)
                v_lo = SBx(ph3, "v_lo", [128, NKT * VST], BF16)
                vstage = [SBx(ph3, f"vstage{i}", [128, 512]) for i in range(2)]
                b_vst = [B_(), B_()]
                b_v_h, b_v_l = B_(), B_()
                b_kn_h, b_kn_l = b_kpeT_h, b_kpeT_l
                knw_h = SBx(ph3, "knw_h", [128, KC * 64], BF16)
                knw_l = SBx(ph3, "knw_l", [128, KC * 64], BF16)
                vw_h = SBx(ph3, "vw_h", [128, KC * VD], BF16)
                vw_l = SBx(ph3, "vw_l", [128, KC * VD], BF16)
                b_knw_h, b_knw_l, b_vw_h, b_vw_l = B_(), B_(), B_(), B_()
                qsl_h = SBx(ph3, "qsl_h", [128, 7 * 512], BF16)
                qsl_l = SBx(ph3, "qsl_l", [128, 7 * 512], BF16)
                b_qsl_h, b_qsl_l = B_(), B_()
                qow_h = SBx(ph3, "qow_h", [128, OWN], BF16)
                qow_l = SBx(ph3, "qow_l", [128, OWN], BF16)
                b_qow_h, b_qow_l = B_(), B_()
                p32 = [SBx(ph3, f"p32_{i}", [128, 512]) for i in range(3)]
                b_p32 = [B_() for _ in range(3)]
                pt_h = [SBx(ph3, f"pt_h{i}", [128, 512], BF16) for i in range(3)]
                pt_l = [SBx(ph3, f"pt_l{i}", [128, 512], BF16) for i in range(3)]
                b_pth = [B_() for _ in range(3)]
                b_ptl = [B_() for _ in range(3)]
                oslot2 = [SBx(ph3, f"oslot{i}", [128, 9 * OST]) for i in range(2)]
                b_oslot2 = [B_(), B_()]
                negm2 = [SBx(ph3, f"negm{i}", [128, 12]) for i in range(2)]
                b_negm2 = [B_(), B_()]
                msel2 = [SBx(ph3, f"msel{i}", [128, 12]) for i in range(2)]
                coefs2 = [SBx(ph3, f"coefs{i}", [128, 20]) for i in range(2)]
                b_msel2 = [B_(), B_()]
                b_coef2 = [B_(), B_()]
                oacc2 = [SBx(ph3, f"oacc{i}", [128, 128]) for i in range(2)]
                b_oacc2 = [B_(), B_()]
                ofst_h2 = [SBx(ph3, f"ofst_h{i}", [128, 128], BF16) for i in range(2)]
                ofst_l2 = [SBx(ph3, f"ofst_l{i}", [128, 128], BF16) for i in range(2)]
                b_ofst_h2 = [B_(), B_()]
                b_ofst_l2 = [B_(), B_()]

                pb.op("vector", "memset",
                      bass.AP(v_hi, 128, [[NKT * VST, 128], [VST, NKT]]), 1.0,
                      bufw=(b_v_h,))
                pb.op("vector", "memset",
                      bass.AP(v_lo, 128, [[NKT * VST, 128], [VST, NKT]]), 0.0,
                      bufw=(b_v_l,))

                wkvb_hv = wkvb_hi.ap().rearrange("(c p) w -> p c w", p=128)
                wkvb_lv = wkvb_lo.ap().rearrange("(c p) w -> p c w", p=128)

                for h in range(H if phases >= 4 else 0):
                    l1 = pb.op("sync", "dma_start",
                               out=knw_h.ap().rearrange("p (c w) -> p c w", c=KC),
                               in_=wkvb_hv[:, :, h * KVD:h * KVD + 64],
                               inc=True, bufw=(b_knw_h,), bufr=(b_wkvb,))
                    l2 = pb.op("sync", "dma_start",
                               out=knw_l.ap().rearrange("p (c w) -> p c w", c=KC),
                               in_=wkvb_lv[:, :, h * KVD:h * KVD + 64],
                               inc=True, bufw=(b_knw_l,), bufr=(b_wkvb,))
                    l3 = pb.op("sync", "dma_start",
                               out=vw_h.ap().rearrange("p (c w) -> p c w", c=KC),
                               in_=wkvb_hv[:, :, h * KVD + 64:(h + 1) * KVD],
                               inc=True, bufw=(b_vw_h,), bufr=(b_wkvb,))
                    l4 = pb.op("sync", "dma_start",
                               out=vw_l.ap().rearrange("p (c w) -> p c w", c=KC),
                               in_=wkvb_lv[:, :, h * KVD + 64:(h + 1) * KVD],
                               inc=True, bufw=(b_vw_l,), bufr=(b_wkvb,))
                    gq = None
                    qsp_hi_flat = qsp_hi.ap().rearrange("h r w -> (h r) w")
                    qsp_lo_flat = qsp_lo.ap().rearrange("h r w -> (h r) w")
                    for j in range(7):
                        pb.op("gpsimd", "indirect_dma_start",
                              out=qsl_h[:, j * 512:j * 512 + TPB * 128],
                              out_offset=None, in_=qsp_hi_flat,
                              in_offset=IndirectOffsetOnAxis(
                                  ap=qsel[:, h * 7 + j:h * 7 + j + 1], axis=0),
                              inc=True, bufw=(b_qsl_h,),
                              bufr=(b_qsp, b_consts))
                        pb.op("gpsimd", "indirect_dma_start",
                              out=qsl_l[:, j * 512:j * 512 + TPB * 128],
                              out_offset=None, in_=qsp_lo_flat,
                              in_offset=IndirectOffsetOnAxis(
                                  ap=qsel[:, h * 7 + j:h * 7 + j + 1], axis=0),
                              inc=True, bufw=(b_qsl_l,),
                              bufr=(b_qsp, b_consts))
                    for side in range(2):
                        pb.op("scalar", "dma_start",
                              out=qow_h[:, side * (TPB * 128):
                                        (side + 1) * (TPB * 128)],
                              in_=qsp_hi[h, side * 128:(side + 1) * 128, :],
                              inc=True, bufw=(b_qow_h,), bufr=(b_qsp,))
                        gq = pb.op("scalar", "dma_start",
                                   out=qow_l[:, side * (TPB * 128):
                                             (side + 1) * (TPB * 128)],
                                   in_=qsp_lo[h, side * 128:(side + 1) * 128, :],
                                   inc=True, bufw=(b_qow_l,), bufr=(b_qsp,))
                    for t9 in range((NKEY + 511) // 512):
                        w9 = min(512, NKEY - t9 * 512)
                        pc = 4 + t9 % 2
                        for rc in range(KC):
                            first = rc == 0
                            last = rc == KC - 1
                            cp_sl = slice(rc * NKEY + t9 * 512,
                                          rc * NKEY + t9 * 512 + w9)
                            pb.op("tensor", "matmul", pbank[pc][0:64, 0:w9],
                                  knw_h[:, rc * 64:(rc + 1) * 64],
                                  cpk_hi[:, cp_sl], start=first, stop=False,
                                  deps=(l1, l2),
                                  bufd=(b_cpk_h, b_cpk_l),
                                  bufwar=(b_pb[pc],) if first else ())
                            pb.op("tensor", "matmul", pbank[pc][0:64, 0:w9],
                                  knw_h[:, rc * 64:(rc + 1) * 64],
                                  cpk_lo[:, cp_sl], start=False, stop=False)
                            pb.op("tensor", "matmul", pbank[pc][0:64, 0:w9],
                                  knw_l[:, rc * 64:(rc + 1) * 64],
                                  cpk_hi[:, cp_sl], start=False, stop=last,
                                  inc=last,
                                  bufw=(b_pb[pc],) if last else (),
                                  bufr=(b_knw_h, b_knw_l, b_cpk_h, b_cpk_l)
                                  if last else ())
                        evac_hilo(pbank[pc][0:64, 0:w9],
                                  kpeT_hi[0:64, t9 * 512:t9 * 512 + w9],
                                  kpeT_lo[0:64, t9 * 512:t9 * 512 + w9],
                                  (b_pb[pc],), b_kn_h, b_kn_l,
                                  deps=(b_pb[pc].writer,))
                    for t9 in range((NKEY + 511) // 512):
                        w9 = min(512, NKEY - t9 * 512)
                        nch9 = w9 // 128
                        pc = 4 + t9 % 2
                        for rc in range(KC):
                            first = rc == 0
                            last = rc == KC - 1
                            cp_sl = slice(rc * NKEY + t9 * 512,
                                          rc * NKEY + t9 * 512 + w9)
                            pb.op("tensor", "matmul", pbank[pc][:, 0:w9],
                                  vw_h[:, rc * VD:(rc + 1) * VD],
                                  cpk_hi[:, cp_sl],
                                  start=first, stop=False, deps=(l3, l4),
                                  bufd=(b_cpk_h, b_cpk_l),
                                  bufwar=(b_pb[pc],) if first else ())
                            pb.op("tensor", "matmul", pbank[pc][:, 0:w9],
                                  vw_h[:, rc * VD:(rc + 1) * VD],
                                  cpk_lo[:, cp_sl],
                                  start=False, stop=False)
                            pb.op("tensor", "matmul", pbank[pc][:, 0:w9],
                                  vw_l[:, rc * VD:(rc + 1) * VD],
                                  cpk_hi[:, cp_sl],
                                  start=False, stop=last, inc=last,
                                  bufw=(b_pb[pc],) if last else (),
                                  bufr=(b_vw_h, b_vw_l, b_cpk_h, b_cpk_l)
                                  if last else ())
                        sv9 = pb.op("scalar", "copy", vstage[t9 % 2][:, 0:w9],
                                    pbank[pc][:, 0:w9],
                                    deps=(b_pb[pc].writer,),
                                    bufw=(b_vst[t9 % 2],), bufr=(b_pb[pc],))
                        for kc9 in range(nch9):
                            vt = t9 * 4 + kc9
                            tb9 = vt % 2
                            stv = pb.op("tensor", "transpose",
                                        pbb[tb9][:, 0:128],
                                        vstage[t9 % 2][:, kc9 * 128:
                                                       (kc9 + 1) * 128],
                                        id32[:], deps=(sv9,), inc=True,
                                        bufw=(b_pbb[tb9],),
                                        bufr=(b_vst[t9 % 2], b_consts))
                            evac_hilo(pbb[tb9][:, 0:128],
                                      v_hi[:, vt * VST:vt * VST + VD],
                                      v_lo[:, vt * VST:vt * VST + VD],
                                      (b_pbb[tb9],), b_v_h, b_v_l,
                                      deps=(stv,))

                    def job_front(jslot, qh_ap, ql_ap, kcol, klen, qbufs,
                                  with_tri, oslot, b_oslot, negm, b_negm):
                        spc = 2 + jslot % 3
                        opc = jslot % 3
                        nchunk = klen // 128
                        pb.op("tensor", "matmul", pbank[spc][:, 0:klen],
                              qh_ap(0, 128), kpeT_hi[0:128, kcol:kcol + klen],
                              start=True, stop=False,
                              bufd=qbufs + (b_kpeT_h, b_kpeT_l),
                              bufwar=(b_pb[spc],))
                        pb.op("tensor", "matmul", pbank[spc][:, 0:klen],
                              qh_ap(0, 128), kpeT_lo[0:128, kcol:kcol + klen],
                              start=False, stop=False)
                        smm = pb.op("tensor", "matmul", pbank[spc][:, 0:klen],
                                    ql_ap(0, 128), kpeT_hi[0:128, kcol:kcol + klen],
                                    start=False, stop=True,
                                    inc=True, bufw=(b_pb[spc],),
                                    bufr=qbufs + (b_kpeT_h, b_kpeT_l))
                        if sub < 2:
                            return None
                        if with_tri:
                            smm = pb.op("vector", "tensor_add",
                                        pbank[spc][:, klen - 128:klen],
                                        pbank[spc][:, klen - 128:klen], tri[:],
                                        deps=(smm,), bufw=(b_pb[spc],),
                                        bufr=(b_consts,))
                        if sub < 3:
                            return None
                        nm = pb.op("vector", "reduce_max", negm[:, jslot:jslot + 1],
                                   pbank[spc][:, 0:klen], axis=AX.X, negate=True,
                                   deps=(smm,), bufw=(b_negm,),
                                   bufr=(b_pb[spc],))
                        ex = pb.op("scalar", "activation", p32[opc][:, 0:klen],
                                   pbank[spc][:, 0:klen], AF.Exp,
                                   bias=negm[:, jslot:jslot + 1], scale=1.0,
                                   deps=(nm,), bufw=(b_p32[opc],),
                                   bufr=(b_pb[spc], b_negm))
                        return (jslot, kcol, klen, opc, spc, ex,
                                oslot, b_oslot)

                    def job_mid(st8):
                        if st8 is None or sub < 4:
                            return None
                        jslot, kcol, klen, opc, spc, ex, oslot, b_oslot = st8
                        nchunk = klen // 128
                        tb = jslot % 2
                        tps = None
                        for kc2 in range(nchunk):
                            last2 = kc2 == nchunk - 1
                            tps = pb.op("tensor", "transpose",
                                        pbb[tb][:, kc2 * 128:(kc2 + 1) * 128],
                                        p32[opc][:, kc2 * 128:(kc2 + 1) * 128],
                                        id32[:], deps=(ex,), inc=last2,
                                        bufwar=(b_pbb[tb],) if kc2 == 0 else (),
                                        bufw=(b_pbb[tb],) if last2 else (),
                                        bufr=(b_p32[opc], b_consts)
                                        if last2 else (b_consts,))
                        if sub < 4.5:
                            return None
                        e1, e2 = evac_hilo(pbb[tb][:, 0:nchunk * 128],
                                           pt_h[opc][:, 0:nchunk * 128],
                                           pt_l[opc][:, 0:nchunk * 128],
                                           (b_pbb[tb],), b_pth[opc], b_ptl[opc],
                                           deps=(tps,))
                        return (jslot, kcol, klen, opc, e1, e2, oslot, b_oslot)

                    def job_tail(st8):
                        if st8 is None or sub < 5:
                            return
                        jslot, kcol, klen, opc, e1, e2, oslot, b_oslot = st8
                        nchunk = klen // 128
                        oc = jslot % 2
                        for kc2 in range(nchunk):
                            vt = kcol // 128 + kc2
                            first = kc2 == 0
                            last2 = kc2 == nchunk - 1
                            pb.op("tensor", "matmul", pbank[oc][:, 0:VD + 1],
                                  pt_h[opc][:, kc2 * 128:(kc2 + 1) * 128],
                                  v_hi[:, vt * VST:vt * VST + VD + 1],
                                  start=first, stop=False, deps=(e1, e2),
                                  bufwar=(b_pb[oc],) if first else ())
                            pb.op("tensor", "matmul", pbank[oc][:, 0:VD + 1],
                                  pt_h[opc][:, kc2 * 128:(kc2 + 1) * 128],
                                  v_lo[:, vt * VST:vt * VST + VD + 1],
                                  start=False, stop=False)
                            pb.op("tensor", "matmul", pbank[oc][:, 0:VD + 1],
                                  pt_l[opc][:, kc2 * 128:(kc2 + 1) * 128],
                                  v_hi[:, vt * VST:vt * VST + VD + 1],
                                  start=False, stop=last2, inc=last2,
                                  bufw=(b_pb[oc],) if last2 else (),
                                  bufr=(b_pth[opc], b_ptl[opc], b_v_h, b_v_l)
                                  if last2 else ())
                        if sub < 6:
                            return
                        pb.op("vector", "tensor_copy",
                              oslot[:, jslot * OST:jslot * OST + VD + 1],
                              pbank[oc][:, 0:VD + 1], deps=(b_pb[oc].writer,),
                              bufw=(b_oslot,), bufr=(b_pb[oc],))

                    pend_f = []
                    pend_m = []

                    def push_job(st8):
                        pend_f.append(st8)
                        if len(pend_f) > 1:
                            pend_m.append(job_mid(pend_f.pop(0)))
                        if len(pend_m) > 1:
                            job_tail(pend_m.pop(0))

                    def flush_jobs():
                        while pend_f:
                            pend_m.append(job_mid(pend_f.pop(0)))
                        while pend_m:
                            job_tail(pend_m.pop(0))

                    for it in range(TPB):
                        itp = it % 2
                        oslot, b_oslot = oslot2[itp], b_oslot2[itp]
                        negm, b_negm = negm2[itp], b_negm2[itp]
                        msel, b_msel = msel2[itp], b_msel2[itp]
                        coefs, b_coef = coefs2[itp], b_coef2[itp]
                        for j in range(7):
                            qc = j * 512 + it * 128
                            push_job(job_front(j,
                                     lambda a, b, qc=qc: qsl_h[a:b, qc:qc + 128],
                                     lambda a, b, qc=qc: qsl_l[a:b, qc:qc + 128],
                                     j * BS, BS, (b_qsl_h, b_qsl_l), False,
                                     oslot, b_oslot, negm, b_negm))
                        for side in range(2):
                            qc = side * (TPB * 128) + it * 128
                            push_job(job_front(7 + side,
                                     lambda a, b, qc=qc: qow_h[a:b, qc:qc + 128],
                                     lambda a, b, qc=qc: qow_l[a:b, qc:qc + 128],
                                     (7 + side) * BS, (it + 1) * 128,
                                     (b_qow_h, b_qow_l), True,
                                     oslot, b_oslot, negm, b_negm))
                        flush_jobs()
                        for side in range(2 if sub >= 7 else 0):
                            oacc, b_oacc = oacc2[side], b_oacc2[side]
                            ofst_h, b_ofst_h = ofst_h2[side], b_ofst_h2[side]
                            ofst_l, b_ofst_l = ofst_l2[side], b_ofst_l2[side]
                            selx = sela if side == 0 else selb
                            ms = pb.op("vector", "scalar_tensor_tensor",
                                       msel[:, 0:9], negm[:, 0:9], -1.0,
                                       selx[:, 0:9], op0=ALU.mult, op1=ALU.add,
                                       deps=(b_negm.writer, b_oslot.writer),
                                       bufw=(b_msel,),
                                       bufr=(b_negm, b_consts))
                            nmx = pb.op("vector", "reduce_max", msel[:, 9:10],
                                        msel[:, 0:9], axis=AX.X, negate=True,
                                        deps=(ms,), bufw=(b_msel,))
                            cf = pb.op("scalar", "activation", coefs[:, 0:9],
                                       msel[:, 0:9], AF.Exp, bias=msel[:, 9:10],
                                       scale=1.0, deps=(nmx,),
                                       bufw=(b_coef,), bufr=(b_msel,))
                            lv = bass.AP(oslot, VD, [[9 * OST, 128], [OST, 9]])
                            lw2 = pb.op("vector", "tensor_tensor", coefs[:, 9:18],
                                        lv, coefs[:, 0:9], op=ALU.mult,
                                        deps=(cf,), bufw=(b_coef,),
                                        bufr=(b_oslot,))
                            ls = pb.op("vector", "reduce_sum", msel[:, 10:11],
                                       coefs[:, 9:18], axis=AX.X, deps=(lw2,),
                                       bufw=(b_msel,), bufr=(b_coef,))
                            li = pb.op("vector", "reciprocal", msel[:, 11:12],
                                       msel[:, 10:11], deps=(ls,),
                                       bufw=(b_msel,))
                            cn = pb.op("vector", "tensor_scalar", coefs[:, 0:9],
                                       coefs[:, 0:9], msel[:, 11:12], None,
                                       op0=ALU.mult, deps=(li,),
                                       bufw=(b_coef,), bufr=(b_msel,))
                            js = [jj for jj in range(9)
                                  if not (side == 0 and jj == 8)
                                  and not (side == 1 and jj == 7)]
                            a0 = pb.op("vector", "tensor_scalar", oacc[:, 0:VD],
                                       oslot[:, js[0] * OST:js[0] * OST + VD],
                                       coefs[:, js[0]:js[0] + 1], None,
                                       op0=ALU.mult, deps=(cn,),
                                       bufw=(b_oacc,), bufr=(b_oslot, b_coef))
                            for jj in js[1:]:
                                a0 = pb.op("vector", "scalar_tensor_tensor",
                                           oacc[:, 0:VD],
                                           oslot[:, jj * OST:jj * OST + VD],
                                           coefs[:, jj:jj + 1], oacc[:, 0:VD],
                                           op0=ALU.mult, op1=ALU.add,
                                           deps=(a0,), bufw=(b_oacc,),
                                           bufr=(b_oslot, b_coef))
                            pt2 = 4 + side
                            sto = pb.op("tensor", "transpose",
                                        pbank[pt2][:, 0:128], oacc[:, 0:128],
                                        id32[:], deps=(a0,), inc=True,
                                        bufw=(b_pb[pt2],),
                                        bufr=(b_oacc, b_consts))
                            ohh, oll = evac_hilo(pbank[pt2][:, 0:128],
                                                 ofst_h[:], ofst_l[:],
                                                 (b_pb[pt2],), b_ofst_h,
                                                 b_ofst_l, deps=(sto,))
                            towm = side * TPB + it
                            pb.op("scalar", "dma_start",
                                  out=osp_hi[h, towm], in_=ofst_h[:],
                                  deps=(ohh,), inc=True, bufw=(b_osp,),
                                  bufr=(b_ofst_h,))
                            pb.op("scalar", "dma_start",
                                  out=osp_lo[h, towm], in_=ofst_l[:],
                                  deps=(oll,), inc=True, bufw=(b_osp,),
                                  bufr=(b_ofst_l,))
            pb.barrier()

        with ExitStack() as ph4:
            pk4 = [PSx(ph4, f"pk4_{i}", [128, 512]) for i in range(8)]
            b_pk4 = [B_() for _ in range(8)]
            attn4 = SBx(ph4, "attn4", [128, TQG * D])
            b_attn4 = B_()
            wo_f = [SBx(ph4, f"wo_f{i}", [128, DQW]) for i in range(2)]
            wo_h = [SBx(ph4, f"wo_h{i}", [128, DQW], BF16) for i in range(2)]
            wo_l = [SBx(ph4, f"wo_l{i}", [128, DQW], BF16) for i in range(2)]
            b_wo_f = [B_(), B_()]
            b_wo_h = [B_(), B_()]
            b_wo_l = [B_(), B_()]
            ofm_h = [SBx(ph4, f"ofm_h{i}", [128, TQG * 128], BF16) for i in range(2)]
            ofm_l = [SBx(ph4, f"ofm_l{i}", [128, TQG * 128], BF16) for i in range(2)]
            b_ofm_h = [B_(), B_()]
            b_ofm_l = [B_(), B_()]
            x2b = SBx(ph4, "x2b", [128, D])
            xh2 = SBx(ph4, "xh2", [128, D])
            ffnb = SBx(ph4, "ffnb", [128, D])
            b_x2, b_xh2, b_ffn = B_(), B_(), B_()
            xfm2_h = SBx(ph4, "xfm2_h", [128, D], BF16)
            xfm2_l = SBx(ph4, "xfm2_l", [128, D], BF16)
            b_xfm2_h, b_xfm2_l = B_(), B_()
            xb4 = SBx(ph4, "xb4", [128, D])
            b_xb4 = B_()
            eb_h = SBx(ph4, "eb_h", [E, D], BF16)
            eb_l = SBx(ph4, "eb_l", [E, D], BF16)
            b_ebf, b_eb_h, b_eb_l = B_(), B_(), B_()
            rw_h = SBx(ph4, "rw_h", [128, DC * E], BF16)
            rw_l = SBx(ph4, "rw_l", [128, DC * E], BF16)
            b_rwf, b_rwh, b_rwl = B_(), B_(), B_()
            rb_f = SBx(ph4, "rb_f", [1, E])
            rb_h = SBx(ph4, "rb_h", [1, E], BF16)
            rb_l = SBx(ph4, "rb_l", [1, E], BF16)
            b_rbf, b_rbh, b_rbl = B_(), B_(), B_()
            pex = SBx(ph4, "pex", [128, E])
            pnr = SBx(ph4, "pnr", [128, E])
            wdn = SBx(ph4, "wdn", [128, E])
            top8 = SBx(ph4, "top8", [128, 8])
            b_pex, b_pnr, b_wdn, b_top8 = B_(), B_(), B_(), B_()
            wd_h = SBx(ph4, "wd_h", [128, E], BF16)
            wd_l = SBx(ph4, "wd_l", [128, E], BF16)
            b_wdh, b_wdl = B_(), B_()
            wdt_h = SBx(ph4, "wdt_h", [64, 128], BF16)
            wdt_l = SBx(ph4, "wdt_l", [64, 128], BF16)
            b_wdth, b_wdtl = B_(), B_()
            st4 = SBx(ph4, "st4", [128, 8 * 6])
            mv4 = SBx(ph4, "mv4", [128, 8])
            b_st4, b_mv4 = B_(), B_()

            for c0 in range(0, D, DQW):
                le = pb.op("sync", "dma_start", out=wo_f[0][0:E, 0:DQW],
                           in_=EB[:, c0:c0 + DQW], inc=True, bufw=(b_wo_f[0],))
                evac_hilo(wo_f[0][0:E, 0:DQW], eb_h[0:E, c0:c0 + DQW],
                          eb_l[0:E, c0:c0 + DQW], (b_wo_f[0],), b_eb_h, b_eb_l,
                          deps=(le,))
            rwt_v = RWT.ap().rearrange("(c p) w -> p c w", p=128)
            CPG = max(1, DQW // E)
            for cg in range(0, DC, CPG):
                ng = min(CPG, DC - cg)
                lr = pb.op("sync", "dma_start",
                           out=bass.AP(wo_f[0], 0, [[DQW, 128], [E, ng], [1, E]]),
                           in_=rwt_v[:, cg:cg + ng, :],
                           inc=True, bufw=(b_wo_f[0],))
                evac_hilo(wo_f[0][:, 0:ng * E], rw_h[:, cg * E:(cg + ng) * E],
                          rw_l[:, cg * E:(cg + ng) * E], (b_wo_f[0],),
                          b_rwh, b_rwl, deps=(lr,))
            lb = pb.op("sync", "dma_start", out=rb_f[:], in_=RB[:], inc=True,
                       bufw=(b_rbf,))
            evac_hilo(rb_f[:], rb_h[:], rb_l[:], (b_rbf,), b_rbh, b_rbl,
                      deps=(lb,))

            def ln4(src_ap_fn, ncol, src_bufs, eps, deps, mcol):
                nch = (ncol + 511) // 512
                sts = None
                for ci in range(nch):
                    w = min(512, ncol - ci * 512)
                    sts = pb.op("vector", "bn_stats", st4[:, ci * 6:(ci + 1) * 6],
                                src_ap_fn(ci * 512, w), deps=tuple(deps),
                                bufw=(b_st4,), bufr=src_bufs)
                sa = pb.op("vector", "bn_aggr", mv4[:, mcol:mcol + 2],
                           st4[:, 0:6 * nch], deps=(sts,), bufw=(b_mv4,),
                           bufr=(b_st4,))
                s0 = pb.op("vector", "tensor_scalar_add",
                           mv4[:, mcol + 3:mcol + 4],
                           mv4[:, mcol + 1:mcol + 2], float(eps), deps=(sa,),
                           bufw=(b_mv4,), bufr=(b_mv4,))
                s1 = pb.op("scalar", "activation", mv4[:, mcol + 3:mcol + 4],
                           mv4[:, mcol + 3:mcol + 4], AF.Ln,
                           scale=1.0, deps=(s0,), bufw=(b_mv4,), bufr=(b_mv4,))
                s2 = pb.op("scalar", "activation", mv4[:, mcol + 3:mcol + 4],
                           mv4[:, mcol + 3:mcol + 4], AF.Exp,
                           scale=-0.5, deps=(s1,), bufw=(b_mv4,), bufr=(b_mv4,))
                return s2

            NDT = DQW // 512
            for tq in range(OWNT // TQG if phases >= 5 else 0):
                for dq in range(D // DQW):
                    for hv in range(H):
                        wb = hv % 2
                        lw = pb.op("sync", "dma_start", out=wo_f[wb][:],
                                   in_=WOT[hv * 128:(hv + 1) * 128,
                                           dq * DQW:(dq + 1) * DQW],
                                   inc=True, bufw=(b_wo_f[wb],))
                        wh, wl = evac_hilo(wo_f[wb][:], wo_h[wb][:], wo_l[wb][:],
                                           (b_wo_f[wb],), b_wo_h[wb], b_wo_l[wb],
                                           deps=(lw,))
                        lo1 = pb.op("scalar", "dma_start",
                                    out=ofm_h[wb].ap().rearrange(
                                        "p (t w) -> p t w", t=TQG),
                                    in_=osp_hi.ap()[hv, tq * TQG:(tq + 1) * TQG]
                                    .rearrange("t p w -> p t w"),
                                    inc=True, bufw=(b_ofm_h[wb],), bufr=(b_osp,))
                        lo2 = pb.op("scalar", "dma_start",
                                    out=ofm_l[wb].ap().rearrange(
                                        "p (t w) -> p t w", t=TQG),
                                    in_=osp_lo.ap()[hv, tq * TQG:(tq + 1) * TQG]
                                    .rearrange("t p w -> p t w"),
                                    inc=True, bufw=(b_ofm_l[wb],), bufr=(b_osp,))
                        for t4 in range(TQG):
                            for ndt in range(NDT):
                                bk = t4 * NDT + ndt
                                first = hv == 0
                                last = hv == H - 1
                                pb.op("tensor", "matmul", pk4[bk][:, 0:512],
                                      ofm_h[wb][:, t4 * 128:(t4 + 1) * 128],
                                      wo_h[wb][:, ndt * 512:(ndt + 1) * 512],
                                      start=first, stop=False,
                                      deps=(wh, wl, lo1, lo2),
                                      bufwar=(b_pk4[bk],) if first else ())
                                pb.op("tensor", "matmul", pk4[bk][:, 0:512],
                                      ofm_h[wb][:, t4 * 128:(t4 + 1) * 128],
                                      wo_l[wb][:, ndt * 512:(ndt + 1) * 512],
                                      start=False, stop=False)
                                lastb = ndt == NDT - 1 and t4 == TQG - 1
                                pb.op("tensor", "matmul", pk4[bk][:, 0:512],
                                      ofm_l[wb][:, t4 * 128:(t4 + 1) * 128],
                                      wo_h[wb][:, ndt * 512:(ndt + 1) * 512],
                                      start=False, stop=last, inc=lastb or last,
                                      bufw=(b_pk4[bk],) if last else (),
                                      bufr=(b_ofm_h[wb], b_ofm_l[wb],
                                            b_wo_h[wb], b_wo_l[wb])
                                      if lastb else ())
                    for t4 in range(TQG):
                        for ndt in range(NDT):
                            bk = t4 * NDT + ndt
                            pb.op("scalar", "copy",
                                  attn4[:, t4 * D + dq * DQW + ndt * 512:
                                        t4 * D + dq * DQW + (ndt + 1) * 512],
                                  pk4[bk][:, 0:512], deps=(b_pk4[bk].writer,),
                                  bufw=(b_attn4,), bufr=(b_pk4[bk],))
                for t4 in range(TQG):
                    towm = tq * TQG + t4

                    def att(o, w, _t=t4):
                        return attn4[:, _t * D + o:_t * D + o + w]

                    s2 = ln4(att, D, (b_attn4,), EPS_LN, (b_attn4.writer,), 0)
                    lx = pb.op("sync", "dma_start", out=xb4[:],
                               in_=XK[(7 * TPB + towm) * 128:
                                      (7 * TPB + towm + 1) * 128, :],
                               inc=True, bufw=(b_xb4,))
                    t1 = pb.op("vector", "tensor_scalar", x2b[:], att(0, D),
                               mv4[:, 0:1], mv4[:, 3:4], op0=ALU.subtract,
                               op1=ALU.mult, deps=(s2,),
                               bufw=(b_x2,), bufr=(b_attn4, b_mv4))
                    t2 = pb.op("vector", "tensor_add", x2b[:], x2b[:], xb4[:],
                               deps=(t1, lx), bufw=(b_x2,), bufr=(b_xb4,))
                    s3 = ln4(lambda o, w: x2b[:, o:o + w], D, (b_x2,),
                             EPS_LN, (t2,), 4)
                    t3 = pb.op("vector", "tensor_scalar", xh2[:], x2b[:],
                               mv4[:, 4:5], mv4[:, 7:8], op0=ALU.subtract,
                               op1=ALU.mult, deps=(s3,),
                               bufw=(b_xh2,), bufr=(b_x2, b_mv4))
                    for dc in range(DC):
                        pt2 = 4 + dc % 2
                        stx = pb.op("tensor", "transpose", pk4[pt2][:, 0:128],
                                    xh2[:, dc * 128:(dc + 1) * 128], id32[:],
                                    deps=(t3,), inc=True,
                                    bufw=(b_pk4[pt2],),
                                    bufr=(b_xh2, b_consts))
                        fh, fl = evac_hilo(pk4[pt2][:, 0:128],
                                           xfm2_h[:, dc * 128:(dc + 1) * 128],
                                           xfm2_l[:, dc * 128:(dc + 1) * 128],
                                           (b_pk4[pt2],), b_xfm2_h, b_xfm2_l,
                                           deps=(stx,))
                        first = dc == 0
                        pb.op("tensor", "matmul", pk4[0][:, 0:E],
                              xfm2_h[:, dc * 128:(dc + 1) * 128],
                              rw_h[:, dc * E:(dc + 1) * E], start=first,
                              stop=False, deps=(fh, fl, b_rwh.writer,
                                                b_rwl.writer),
                              bufwar=(b_pk4[0],) if first else ())
                        pb.op("tensor", "matmul", pk4[0][:, 0:E],
                              xfm2_h[:, dc * 128:(dc + 1) * 128],
                              rw_l[:, dc * E:(dc + 1) * E], start=False,
                              stop=False)
                        pb.op("tensor", "matmul", pk4[0][:, 0:E],
                              xfm2_l[:, dc * 128:(dc + 1) * 128],
                              rw_h[:, dc * E:(dc + 1) * E], start=False,
                              stop=False)
                    pb.op("tensor", "matmul", pk4[0][:, 0:E], onesb[:],
                          rb_h[:], start=False, stop=False,
                          deps=(b_rbh.writer, b_rbl.writer, b_consts.writer))
                    lmm = pb.op("tensor", "matmul", pk4[0][:, 0:E], onesb[:],
                                rb_l[:], start=False, stop=True, inc=True,
                                bufw=(b_pk4[0],),
                                bufr=(b_xfm2_h, b_xfm2_l, b_rwh, b_rwl,
                                      b_rbh, b_rbl))
                    nm = pb.op("vector", "reduce_max", mv4[:, 0:1],
                               pk4[0][:, 0:E], axis=AX.X, negate=True,
                               deps=(lmm,), bufw=(b_mv4,), bufr=(b_pk4[0],))
                    ex = pb.op("scalar", "activation", pex[:], pk4[0][:, 0:E],
                               AF.Exp, bias=mv4[:, 0:1], scale=1.0, deps=(nm,),
                               bufw=(b_pex,), bufr=(b_pk4[0], b_mv4))
                    sm = pb.op("vector", "reduce_sum", mv4[:, 1:2], pex[:],
                               axis=AX.X, deps=(ex,), bufw=(b_mv4,),
                               bufr=(b_pex,))
                    iv = pb.op("vector", "reciprocal", mv4[:, 2:3], mv4[:, 1:2],
                               deps=(sm,), bufw=(b_mv4,))
                    pn = pb.op("scalar", "mul", pnr[:], pex[:], mv4[:, 2:3],
                               deps=(iv, ex), bufw=(b_pnr,),
                               bufr=(b_pex, b_mv4))
                    t8 = pb.op("vector", "max", top8[:], pnr[:], deps=(pn,),
                               bufw=(b_top8,), bufr=(b_pnr,))
                    msk = pb.op("vector", "tensor_single_scalar", wdn[:], pnr[:],
                                top8[:, 7:8], op=ALU.is_ge, deps=(t8,),
                                bufw=(b_wdn,), bufr=(b_pnr, b_top8))
                    wdm = pb.op("vector", "tensor_mul", wdn[:], wdn[:], pnr[:],
                                deps=(msk,), bufw=(b_wdn,), bufr=(b_pnr,))
                    ws = pb.op("vector", "reduce_sum", mv4[:, 3:4], wdn[:],
                               axis=AX.X, deps=(wdm,), bufw=(b_mv4,),
                               bufr=(b_wdn,))
                    sw = pb.op("tensor", "transpose", pk4[1][0:E, 0:128],
                               wdn[:], id32[:], deps=(wdm,), inc=True,
                               bufw=(b_pk4[1],), bufr=(b_wdn, b_consts))
                    ew1, ew2 = evac_hilo(pk4[1][0:E, 0:128],
                                         wdt_h[0:E, :], wdt_l[0:E, :],
                                         (b_pk4[1],), b_wdth, b_wdtl,
                                         deps=(sw,))
                    for ndt in range(D // 512):
                        bk = 2 + ndt % 2
                        pb.op("tensor", "matmul", pk4[bk][:, 0:512],
                              wdt_h[0:E, :], eb_h[0:E, ndt * 512:(ndt + 1) * 512],
                              start=True, stop=False,
                              deps=(ew1, ew2, b_eb_h.writer, b_eb_l.writer),
                              bufwar=(b_pk4[bk],))
                        pb.op("tensor", "matmul", pk4[bk][:, 0:512],
                              wdt_h[0:E, :], eb_l[0:E, ndt * 512:(ndt + 1) * 512],
                              start=False, stop=False)
                        mmb = pb.op("tensor", "matmul", pk4[bk][:, 0:512],
                                    wdt_l[0:E, :],
                                    eb_h[0:E, ndt * 512:(ndt + 1) * 512],
                                    start=False, stop=True, inc=True,
                                    bufw=(b_pk4[bk],),
                                    bufr=(b_wdth, b_wdtl, b_eb_h, b_eb_l))
                        pb.op("vector", "scalar_tensor_tensor",
                              ffnb[:, ndt * 512:(ndt + 1) * 512],
                              xh2[:, ndt * 512:(ndt + 1) * 512], mv4[:, 3:4],
                              pk4[bk][:, 0:512], op0=ALU.mult, op1=ALU.add,
                              deps=(mmb, ws), bufw=(b_ffn,),
                              bufr=(b_xh2, b_mv4, b_pk4[bk]))
                    s4 = ln4(lambda o, w: ffnb[:, o:o + w], D, (b_ffn,),
                             EPS_LN, (b_ffn.writer,), 4)
                    y1 = pb.op("vector", "tensor_scalar", xh2[:], ffnb[:],
                               mv4[:, 4:5], mv4[:, 7:8], op0=ALU.subtract,
                               op1=ALU.mult, deps=(s4,),
                               bufw=(b_xh2,), bufr=(b_ffn, b_mv4))
                    y2 = pb.op("vector", "tensor_add", xh2[:], xh2[:], x2b[:],
                               deps=(y1,), bufw=(b_xh2,), bufr=(b_x2,))
                    pb.op("scalar", "dma_start",
                          out=Y[towm * 128:(towm + 1) * 128, :],
                          in_=xh2[:], deps=(y2,), inc=True,
                          bufw=(b_y,), bufr=(b_xh2,))
            if phases < 5:
                z = pb.op("vector", "memset", xh2[:], 0.0, bufw=(b_xh2,))
                for towm in range(OWNT):
                    pb.op("scalar", "dma_start",
                          out=Y[towm * 128:(towm + 1) * 128, :], in_=xh2[:],
                          deps=(z,), inc=True, bufw=(b_y,), bufr=(b_xh2,))
            for sk, v in pb.cnt.items():
                if v > 0 and (sk.startswith("scalarq") or sk.startswith("gpsimd")
                              or sk.startswith("sync")):
                    pb._wait("gpsimd", sk, v)

        for e, v in pb.cnt.items():
            assert v < 60000, (e, v)

        blk = stack.enter_context(nc.Block())
        pb.realize(blk)

    nc.finalize()
    return nc



def make_shared_inputs(ins, cfg):
    g = derived(cfg)
    RC, KC, E = g["RC"], g["KC"], cfg["E"]
    cos, sin = yarn_cos_sin(cfg["S"], dim=64)
    tri = np.zeros((128, 128), np.float32)
    for r in range(128):
        tri[r, r + 1:] = NEG
    shared = dict(
        wqat=np.ascontiguousarray(np.asarray(ins["L0_attn_wq_a"], np.float32).T),
        wqbt=np.ascontiguousarray(np.asarray(ins["L0_attn_wq_b"], np.float32).T),
        wkvat=np.ascontiguousarray(np.asarray(ins["L0_attn_wkv_a"], np.float32).T),
        wkvbt=np.ascontiguousarray(np.asarray(ins["L0_attn_wkv_b"], np.float32).T),
        wot=np.ascontiguousarray(np.asarray(ins["L0_attn_wo"], np.float32).T),
        rwt=np.ascontiguousarray(np.asarray(ins["L0_ffn_router_weight"], np.float32).T),
        rb=np.ascontiguousarray(np.asarray(ins["L0_ffn_router_bias"], np.float32)
                                .reshape(1, E)),
        eb=np.ascontiguousarray(np.asarray(ins["L0_ffn_expert_bias"], np.float32)),
        qaw=np.ascontiguousarray(np.asarray(ins["L0_attn_q_a_ln_w"], np.float32)
                                 .reshape(RC, 128)),
        kvw=np.ascontiguousarray(np.asarray(ins["L0_attn_kv_a_ln_w"], np.float32)
                                 .reshape(KC, 128)),
        tri=tri,
        id32=np.eye(128, dtype=np.float32),
        idbf=np.eye(128, dtype=BF16NP),
        onesb=np.ones((1, 128), dtype=BF16NP),
        _cos32=np.ascontiguousarray(cos[:, :32]),
        _sin32=np.ascontiguousarray(sin[:, :32]),
    )
    return shared


def make_core_inputs(c, ins, cfg, shared):
    g = derived(cfg)
    BS = g["BS"]
    x = np.asarray(ins["L0_attn_x"], np.float32)
    blocks = core_blocks(c)
    xk = np.concatenate([x[b, j * BS:(j + 1) * BS] for b, j in blocks], axis=0)
    pos = np.concatenate([np.arange(j * BS, (j + 1) * BS) for b, j in blocks])
    sela = np.zeros((128, 9), np.float32)
    selb = np.zeros((128, 9), np.float32)
    for j in range(7):
        if j < c:
            selb[:, j] = NEG
        else:
            sela[:, j] = NEG
    sela[:, 8] = NEG
    selb[:, 7] = NEG
    H = cfg["H"]
    qsel = np.zeros((128, H * 7), np.int32)
    for h in range(H):
        for j in range(7):
            side = 0 if j < c else 1
            qsel[:, h * 7 + j] = h * 256 + side * 128 + np.arange(128)
    d = {k: v for k, v in shared.items() if not k.startswith("_")}
    d.update(
        xk=np.ascontiguousarray(xk, np.float32),
        cosk=np.ascontiguousarray(shared["_cos32"][pos]),
        sink=np.ascontiguousarray(shared["_sin32"][pos]),
        sela=sela, selb=selb, qsel=qsel,
    )
    return d


_PROG_CACHE = {}


def get_program():
    if "nc" not in _PROG_CACHE:
        _PROG_CACHE["nc"] = build_program(REAL)
    return _PROG_CACHE["nc"]


def kernel(**inputs):
    cfg = REAL
    g = derived(cfg)
    nc = get_program()
    shared = make_shared_inputs(inputs, cfg)
    in_maps = [make_core_inputs(c, inputs, cfg, shared) for c in range(8)]
    res = run_bass_kernel_spmd(nc, in_maps, core_ids=list(range(8)))
    BS = g["BS"]
    y = np.empty((2, cfg["S"], cfg["D"]), np.float32)
    for c in range(8):
        yc = np.asarray(res.results[c]["y"])
        y[0, c * BS:(c + 1) * BS] = yc[0:BS]
        y[1, (7 - c) * BS:(8 - c) * BS] = yc[BS:2 * BS]
    return y


# version 07
# speedup vs baseline: 1.4847x; 1.1341x over previous
import math
import numpy as np
import ml_dtypes
from contextlib import ExitStack

import concourse.bass as bass
import concourse.mybir as mybir
from concourse.bass import IndirectOffsetOnAxis
from concourse.bass_utils import run_bass_kernel_spmd

F32 = mybir.dt.float32
BF16 = mybir.dt.bfloat16
I32 = mybir.dt.int32
AF = mybir.ActivationFunctionType
ALU = mybir.AluOpType
AX = mybir.AxisListType
BF16NP = ml_dtypes.bfloat16

NEG = -1.0e30
EPS_RMS = 1.0e-6
EPS_LN = 1.0e-5

REAL = dict(D=4096, RQ=1024, KVR=256, H=32, S=4096, E=64)


def derived(cfg):
    g = dict(cfg)
    g["VD"] = 128
    g["KVD"] = 192
    g["NB"] = 8
    g["BS"] = cfg["S"] // 8
    assert g["BS"] % 128 == 0 and g["BS"] <= 512
    g["TPB"] = g["BS"] // 128
    g["NKEY"] = 9 * g["BS"]
    g["NKT"] = g["NKEY"] // 128
    g["OWN"] = 2 * g["BS"]
    g["OWNT"] = g["OWN"] // 128
    g["DC"] = cfg["D"] // 128
    g["RC"] = cfg["RQ"] // 128
    g["KC"] = cfg["KVR"] // 128
    g["WKA"] = cfg["KVR"] + 64
    g["MSC"] = 0.1 * math.log(128.0) + 1.0
    g["SCALE"] = (128.0 ** -0.5) * g["MSC"] * g["MSC"]
    g["ETW"] = min(512, cfg["H"] * 128)
    g["NET"] = cfg["H"] * 128 // g["ETW"]
    g["HPE"] = g["ETW"] // 128
    g["G1B"] = min(2, g["OWNT"])
    g["TQG"] = min(4, g["OWNT"])
    g["DQW"] = 1024 if cfg["D"] % 1024 == 0 else cfg["D"]
    assert g["TQG"] * (g["DQW"] // 512) <= 8
    return g


def yarn_cos_sin(seq_len, dim=64, base=10000.0, factor=128.0, beta_fast=32.0,
                 beta_slow=1.0, orig_max=8192, mscale=1.0, mscale_all_dim=1.0):
    def corr_dim(r):
        return dim * math.log(orig_max / (r * 2 * math.pi)) / (2 * math.log(base))
    low = max(math.floor(corr_dim(beta_fast)), 0)
    high = min(math.ceil(corr_dim(beta_slow)), dim - 1)
    hi = high + 0.001 if low == high else float(high)
    ramp = np.clip((np.arange(dim // 2, dtype=np.float32) - low) / (hi - low), 0.0, 1.0)
    inv_freq_mask = 1.0 - ramp
    freq_extra = 1.0 / base ** (np.arange(0, dim, 2, dtype=np.float32) / dim)
    freq_inter = freq_extra / factor
    inv_freq = freq_inter * (1.0 - inv_freq_mask) + freq_extra * inv_freq_mask
    t = np.arange(seq_len, dtype=np.float32)
    freqs = np.outer(t, inv_freq)
    emb = np.concatenate([freqs, freqs], axis=-1)

    def gm(s, m):
        return 1.0 if s <= 1 else 0.1 * m * math.log(s) + 1.0
    ms = gm(factor, mscale) / gm(factor, mscale_all_dim)
    return ((np.cos(emb) * np.float32(ms)).astype(np.float32),
            (np.sin(emb) * np.float32(ms)).astype(np.float32))


def core_blocks(c):
    return ([(0, j) for j in range(c)] + [(1, j) for j in range(7 - c)]
            + [(0, c), (1, 7 - c)])



class Buf:
    __slots__ = ("writers", "readers")

    def __init__(self):
        self.writers = []
        self.readers = []

    @property
    def writer(self):
        return self.writers[-1] if self.writers else None


class PB:
    ENGS = ("sync", "tensor", "vector", "scalar", "gpsimd")
    QPOOL = {"sync": 8, "scalarq": 8, "gpsimd": 4}

    def __init__(self, nc, stack):
        self.nc = nc
        self.q = {e: [] for e in self.ENGS}
        sems = ["tensor", "vector", "scalar"]
        for qn, k in self.QPOOL.items():
            sems += [f"{qn}{i}" for i in range(k)]
        self.sem = {s: stack.enter_context(nc.semaphore("sem_" + s)) for s in sems}
        self.cnt = {s: 0 for s in sems}
        self.qidx = {qn: 0 for qn in self.QPOOL}
        self.opidx = {e: 0 for e in self.ENGS}
        self.last_drain = {e: 0 for e in self.ENGS}
        self.hiw = {e: {} for e in self.ENGS}

    def _wait(self, eng, peng, val):
        if self.hiw[eng].get(peng, 0) >= val:
            return
        self.hiw[eng][peng] = val
        sem = self.sem[peng]
        self.opidx[eng] += 1
        self.q[eng].append(lambda e, sem=sem, val=val: e.wait_ge(sem, val))

    def _drain(self, eng):
        self.opidx[eng] += 1
        self.q[eng].append(lambda e: e.drain())
        self.last_drain[eng] = self.opidx[eng]

    def dep(self, eng, stamp):
        if stamp is None:
            return
        kind, peng, oidx, cnt = stamp
        if kind == "c" and peng == eng:
            if oidx > self.last_drain[eng]:
                self._drain(eng)
        else:
            assert cnt is not None, f"cross-engine dep {peng}->{eng} without inc"
            self._wait(eng, peng, cnt)

    def op(self, eng, fname, *args, deps=(), inc=None, bufw=(), bufr=(),
           bufd=(), bufwar=(), **kw):
        for st in deps:
            self.dep(eng, st)
        for b in tuple(bufw) + tuple(bufwar):
            if b.readers:
                for r in b.readers:
                    self.dep(eng, r)
                for w in b.writers:
                    self.dep(eng, w)
                b.readers = []
                b.writers = []
        for b in bufr:
            for w in b.writers:
                self.dep(eng, w)
        for b in bufd:
            for w in b.writers:
                self.dep(eng, w)
        if inc is None:
            inc = eng in ("vector", "scalar")
        self.opidx[eng] += 1
        oidx = self.opidx[eng]
        cnt = None
        isdma = "dma" in fname
        semkey = eng
        if isdma:
            qn = {"scalar": "scalarq", "sync": "sync",
                  "gpsimd": "gpsimd"}[eng]
            k = self.QPOOL[qn]
            semkey = f"{qn}{self.qidx[qn] % k}"
            self.qidx[qn] += 1
            if self.cnt[semkey] > 0:
                self._wait(eng, semkey, self.cnt[semkey])
            assert inc, "every DMA must inc"
        if inc:
            n = 16 if isdma else 1
            self.cnt[semkey] += n
            cnt = self.cnt[semkey]
            sem = self.sem[semkey]
            self.q[eng].append(
                lambda e, a=args, k=kw, f=fname, sem=sem, n=n:
                getattr(e, f)(*a, **k).then_inc(sem, n))
        else:
            self.q[eng].append(
                lambda e, a=args, k=kw, f=fname: getattr(e, f)(*a, **k))
        st = ("d" if isdma else "c", semkey, oidx, cnt)
        for b in bufw:
            b.writers.append(st)
        for b in bufr:
            b.readers.append(st)
        return st

    def barrier(self):
        for eng in self.ENGS:
            for sk, v in self.cnt.items():
                if sk != eng and v > 0:
                    self._wait(eng, sk, v)
            self._drain(eng)

    def realize(self, blk):
        for name in self.ENGS:
            ops = self.q[name]

            def runner(e, ops=ops):
                for f in ops:
                    f(e)
            getattr(blk, name)(runner)



def build_program(cfg, phases=5, sub=9):
    g = derived(cfg)
    D, RQ, KVR, H, E, S = cfg["D"], cfg["RQ"], cfg["KVR"], cfg["H"], cfg["E"], cfg["S"]
    VD, KVD = g["VD"], g["KVD"]
    BS, TPB, NKEY, NKT = g["BS"], g["TPB"], g["NKEY"], g["NKT"]
    OWN, OWNT = g["OWN"], g["OWNT"]
    DC, RC, KC, WKA = g["DC"], g["RC"], g["KC"], g["WKA"]
    SCALE = g["SCALE"]
    ETW, NET, HPE = g["ETW"], g["NET"], g["HPE"]
    G1B, TQG, DQW = g["G1B"], g["TQG"], g["DQW"]
    VST = 132
    OST = 132

    nc = bass.Bass("TRN2", target_bir_lowering=False, debug=False)

    def din(name, shape, dt=F32):
        return nc.dram_tensor(name, shape, dt, kind="ExternalInput")

    XK = din("xk", [NKEY, D])
    WQAT = din("wqat", [D, RQ])
    WQBT = din("wqbt", [RQ, H * 128])
    WKVAT = din("wkvat", [D, WKA])
    WKVBT = din("wkvbt", [KVR, H * KVD])
    WOT = din("wot", [H * VD, D])
    RWT = din("rwt", [D, E])
    RB = din("rb", [1, E])
    EB = din("eb", [E, D])
    QAW = din("qaw", [RC, 128])
    KVW = din("kvw", [KC, 128])
    COSK = din("cosk", [NKEY, 32])
    SINK = din("sink", [NKEY, 32])
    TRI = din("tri", [128, 128])
    ID32 = din("id32", [128, 128])
    IDBF = din("idbf", [128, 128], BF16)
    SELA = din("sela", [128, 9])
    SELB = din("selb", [128, 9])
    QSEL = din("qsel", [128, H * 7], I32)
    ONESB = din("onesb", [1, 128], BF16)

    Y = nc.dram_tensor("y", [OWN, D], F32, kind="ExternalOutput")

    qsp_hi = nc.dram_tensor("qsp_hi", [H, 256, TPB * 128], BF16, kind="Internal")
    qsp_lo = nc.dram_tensor("qsp_lo", [H, 256, TPB * 128], BF16, kind="Internal")
    osp_hi = nc.dram_tensor("osp_hi", [H, OWNT, 128, 128], BF16, kind="Internal")
    osp_lo = nc.dram_tensor("osp_lo", [H, OWNT, 128, 128], BF16, kind="Internal")
    wkva_hi = nc.dram_tensor("wkva_hi", [D, WKA], BF16, kind="Internal")
    wkva_lo = nc.dram_tensor("wkva_lo", [D, WKA], BF16, kind="Internal")
    wkvb_hi = nc.dram_tensor("wkvb_hi", [KVR, H * KVD], BF16, kind="Internal")
    wkvb_lo = nc.dram_tensor("wkvb_lo", [KVR, H * KVD], BF16, kind="Internal")

    stack = ExitStack()
    with stack:
        pb = PB(nc, stack)
        B_ = Buf

        def SBx(st, name, shape, dt=F32):
            return st.enter_context(nc.sbuf_tensor(name, shape, dt))

        def PSx(st, name, shape, dt=F32):
            return st.enter_context(nc.psum_tensor(name, shape, dt))

        cosk = SBx(stack, "cosk_sb", [128, NKT * 32])
        sink = SBx(stack, "sink_sb", [128, NKT * 32])
        tri = SBx(stack, "tri_sb", [128, 128])
        id32 = SBx(stack, "id32_sb", [128, 128])
        idbf = SBx(stack, "idbf_sb", [128, 128], BF16)
        sela = SBx(stack, "sela_sb", [128, 9])
        selb = SBx(stack, "selb_sb", [128, 9])
        qsel = SBx(stack, "qsel_sb", [128, H * 7], I32)
        onesb = SBx(stack, "onesb_sb", [1, 128], BF16)
        qaw = SBx(stack, "qaw_sb", [128, RC])
        kvw = SBx(stack, "kvw_sb", [128, KC])
        negb = SBx(stack, "negb_sb", [128, 1])
        b_consts = B_()
        b_qsp, b_osp = B_(), B_()
        b_wkva, b_wkvb = B_(), B_()
        b_y = B_()

        def ldc(out_ap, in_ap):
            return pb.op("sync", "dma_start", out=out_ap, in_=in_ap,
                         inc=True, bufw=(b_consts,))

        ckv_view = COSK.ap().rearrange("(n p) w -> n p w", p=128)
        skv_view = SINK.ap().rearrange("(n p) w -> n p w", p=128)
        for n in range(NKT):
            ldc(cosk[:, n * 32:(n + 1) * 32], ckv_view[n])
            ldc(sink[:, n * 32:(n + 1) * 32], skv_view[n])
        ldc(tri[:], TRI[:])
        ldc(id32[:], ID32[:])
        ldc(idbf[:], IDBF[:])
        ldc(sela[:], SELA[:])
        ldc(selb[:], SELB[:])
        ldc(qsel[:], QSEL[:])
        ldc(onesb[:], ONESB[:])
        pb.op("sync", "dma_start", out=qaw[:],
              in_=QAW.ap().rearrange("a b -> b a"), inc=True,
              bufw=(b_consts,), allow_slow_non_contiguous=True)
        pb.op("sync", "dma_start", out=kvw[:],
              in_=KVW.ap().rearrange("a b -> b a"), inc=True,
              bufw=(b_consts,), allow_slow_non_contiguous=True)
        pb.op("vector", "memset", negb[:], -24.0, bufw=(b_consts,))

        def evac_hilo(src_ap, hi_ap, lo_ap, src_bufs, hi_buf, lo_buf, deps=(),
                      scale=1.0):
            plain = isinstance(scale, float) and scale == 1.0
            if plain:
                h = pb.op("scalar", "copy", hi_ap, src_ap, deps=deps,
                          bufw=(hi_buf,), bufr=src_bufs)
            else:
                h = pb.op("scalar", "activation", hi_ap, src_ap, AF.Copy, scale=scale, deps=deps,
                          bufw=(hi_buf,), bufr=src_bufs)
            l = pb.op("vector", "scalar_tensor_tensor", lo_ap, src_ap,
                      1.0 if plain else scale, hi_ap,
                      op0=ALU.mult, op1=ALU.subtract, deps=deps + (h,),
                      bufw=(lo_buf,), bufr=src_bufs + (hi_buf,))
            return h, l

        with ExitStack() as ph0:
            wsf = SBx(ph0, "wsf", [128, 2048])
            wsh = SBx(ph0, "wsh", [128, 2048], BF16)
            wsl = SBx(ph0, "wsl", [128, 2048], BF16)
            b_wsf, b_wsh, b_wsl = B_(), B_(), B_()

            def presplit(src, dst_hi, dst_lo, rows, cols, bdst):
                assert rows % 128 == 0
                for rchunk in range(rows // 128):
                    for c0 in range(0, cols, 2048):
                        w = min(2048, cols - c0)
                        s = pb.op("sync", "dma_start", out=wsf[:, 0:w],
                                  in_=src[rchunk * 128:(rchunk + 1) * 128,
                                          c0:c0 + w],
                                  inc=True, bufw=(b_wsf,))
                        h, l = evac_hilo(wsf[:, 0:w], wsh[:, 0:w], wsl[:, 0:w],
                                         (b_wsf,), b_wsh, b_wsl, deps=(s,))
                        pb.op("scalar", "dma_start",
                              out=dst_hi[rchunk * 128:(rchunk + 1) * 128,
                                         c0:c0 + w],
                              in_=wsh[:, 0:w], deps=(h,), inc=True,
                              bufw=(bdst,), bufr=(b_wsh,))
                        pb.op("scalar", "dma_start",
                              out=dst_lo[rchunk * 128:(rchunk + 1) * 128,
                                         c0:c0 + w],
                              in_=wsl[:, 0:w], deps=(l,), inc=True,
                              bufw=(bdst,), bufr=(b_wsl,))

            if phases >= 1:
                presplit(WKVAT, wkva_hi, wkva_lo, D, WKA, b_wkva)
                presplit(WKVBT, wkvb_hi, wkvb_lo, KVR, H * KVD, b_wkvb)
            pb.barrier()

        with ExitStack() as s123:
            cpk_hi = SBx(s123, "cpk_hi", [128, KC * NKEY], BF16)
            cpk_lo = SBx(s123, "cpk_lo", [128, KC * NKEY], BF16)
            kpeT_hi = SBx(s123, "kpeT_hi", [128, NKEY], BF16)
            kpeT_lo = SBx(s123, "kpeT_lo", [128, NKEY], BF16)
            cq_hi = SBx(s123, "cq_hi", [128, RC * OWN], BF16)
            cq_lo = SBx(s123, "cq_lo", [128, RC * OWN], BF16)
            b_cpk_h, b_cpk_l = B_(), B_()
            b_kpeT_h, b_kpeT_l = B_(), B_()
            b_cq_h, b_cq_l = B_(), B_()
            pbank = [PSx(s123, f"pb{i}", [128, 512]) for i in range(6)]
            b_pb = [B_() for _ in range(6)]
            pbb = [PSx(s123, f"pbb{i}", [128, 512]) for i in range(2)]
            b_pbb = [B_() for _ in range(2)]

            statb2 = [SBx(s123, f"statb{i}", [128, 8 * 6]) for i in range(2)]
            mvv2 = [SBx(s123, f"mvv{i}", [128, 8]) for i in range(2)]
            b_stat2 = [B_(), B_()]
            b_mvv2 = [B_(), B_()]
            def set_stat(par):
                return dict(statb=statb2[par], mvv=mvv2[par],
                            b_stat=b_stat2[par], b_mvv=b_mvv2[par])

            CUR = {}

            def invsigma_ops(meansq_src_col, eps, dst_col, deps):
                mvv, b_mvv = CUR["mvv"], CUR["b_mvv"]
                """mvv[dst] = 1/sqrt(mvv[src]+eps) via exp(-0.5 ln(x+eps))."""
                s0 = pb.op("vector", "tensor_scalar_add",
                           mvv[:, dst_col:dst_col + 1],
                           mvv[:, meansq_src_col:meansq_src_col + 1],
                           float(eps), deps=deps, bufw=(b_mvv,), bufr=(b_mvv,))
                s1 = pb.op("scalar", "activation", mvv[:, dst_col:dst_col + 1],
                           mvv[:, dst_col:dst_col + 1], AF.Ln,
                           scale=1.0, deps=(s0,),
                           bufw=(b_mvv,), bufr=(b_mvv,))
                return pb.op("scalar", "activation", mvv[:, dst_col:dst_col + 1],
                             mvv[:, dst_col:dst_col + 1], AF.Exp,
                             scale=-0.5, deps=(s1,),
                             bufw=(b_mvv,), bufr=(b_mvv,))

            def ln_stats(src_ap_fn, ncol, src_bufs, eps, deps):
                statb, mvv = CUR["statb"], CUR["mvv"]
                b_stat, b_mvv = CUR["b_stat"], CUR["b_mvv"]
                nch = (ncol + 511) // 512
                sts = None
                for ci in range(nch):
                    w = min(512, ncol - ci * 512)
                    sts = pb.op("vector", "bn_stats", statb[:, ci * 6:(ci + 1) * 6],
                                src_ap_fn(ci * 512, w), deps=tuple(deps),
                                bufw=(b_stat,), bufr=src_bufs)
                sa = pb.op("vector", "bn_aggr", mvv[:, 0:2], statb[:, 0:6 * nch],
                           deps=(sts,), bufw=(b_mvv,), bufr=(b_stat,))
                return invsigma_ops(1, eps, 3, (sa,))

            def rms_stats(src_ap_fn, ncol, src_bufs, eps, deps):
                statb, mvv = CUR["statb"], CUR["mvv"]
                b_stat, b_mvv = CUR["b_stat"], CUR["b_mvv"]
                nch = (ncol + 511) // 512
                sts = None
                for ci in range(nch):
                    w = min(512, ncol - ci * 512)
                    sts = pb.op("vector", "bn_stats", statb[:, ci * 6:(ci + 1) * 6],
                                src_ap_fn(ci * 512, w), deps=tuple(deps),
                                bufw=(b_stat,), bufr=src_bufs)
                sa = pb.op("vector", "bn_aggr", mvv[:, 0:2], statb[:, 0:6 * nch],
                           deps=(sts,), bufw=(b_mvv,), bufr=(b_stat,))
                sm = pb.op("vector", "scalar_tensor_tensor", mvv[:, 4:5],
                           mvv[:, 0:1], mvv[:, 0:1], mvv[:, 1:2],
                           op0=ALU.mult, op1=ALU.add, deps=(sa,),
                           bufw=(b_mvv,), bufr=(b_mvv,))
                return invsigma_ops(4, eps, 6, (sm,))

            with ExitStack() as ph1:
                xb = [SBx(ph1, f"xb{i}", [128, D]) for i in range(2)]
                b_xb = [B_(), B_()]
                xhat2 = [SBx(ph1, f"xhat{i}", [128, D]) for i in range(2)]
                b_xhat2 = [B_(), B_()]
                xfm_h = [SBx(ph1, f"xfm_h{i}", [128, 128], BF16) for i in range(4)]
                xfm_l = [SBx(ph1, f"xfm_l{i}", [128, 128], BF16) for i in range(4)]
                b_xfm_h = [B_() for _ in range(4)]
                b_xfm_l = [B_() for _ in range(4)]
                wa_h = [SBx(ph1, f"wa_h{i}", [128, WKA], BF16) for i in range(2)]
                wa_l = [SBx(ph1, f"wa_l{i}", [128, WKA], BF16) for i in range(2)]
                b_wa_h = [B_(), B_()]
                b_wa_l = [B_(), B_()]
                cbuf = SBx(ph1, "cbuf", [128, KVR])
                b_cbuf = B_()
                krop = SBx(ph1, "krop", [128, 64])
                ktmp = SBx(ph1, "ktmp", [128, 64])
                b_krop, b_ktmp = B_(), B_()
                kst_h = SBx(ph1, "kst_h", [128, 128], BF16)
                kst_l = SBx(ph1, "kst_l", [128, 128], BF16)
                b_kst_h, b_kst_l = B_(), B_()

                for kt in range(NKT if phases >= 2 else 0):
                    bx = kt % 2
                    CUR.update(set_stat(bx))
                    statb, mvv = CUR["statb"], CUR["mvv"]
                    b_stat, b_mvv = CUR["b_stat"], CUR["b_mvv"]
                    xhat, b_xhat = xhat2[bx], b_xhat2[bx]
                    lx = pb.op("sync", "dma_start", out=xb[bx][:],
                               in_=XK[kt * 128:(kt + 1) * 128, :],
                               inc=True, bufw=(b_xb[bx],))
                    siv = ln_stats(lambda o, w, _b=xb[bx]: _b[:, o:o + w],
                                   D, (b_xb[bx],), EPS_LN, (lx,))
                    sx = pb.op("vector", "tensor_scalar", xhat[:], xb[bx][:],
                               mvv[:, 0:1], mvv[:, 3:4],
                               op0=ALU.subtract, op1=ALU.mult,
                               deps=(siv,), bufw=(b_xhat,),
                               bufr=(b_xb[bx], b_mvv))
                    pc = kt % 2

                    def mm_ckv(dc, st):
                        pt, eh, el, lwh, lwl = st
                        first = dc == 0
                        last = dc == DC - 1
                        pb.op("tensor", "matmul", pbank[pc][:, 0:WKA],
                              xfm_h[pt][:], wa_h[pt][:], start=first, stop=False,
                              deps=(eh, lwh, lwl),
                              bufwar=(b_pb[pc],) if first else ())
                        pb.op("tensor", "matmul", pbank[pc][:, 0:WKA],
                              xfm_h[pt][:], wa_l[pt][:], start=False, stop=False)
                        pb.op("tensor", "matmul", pbank[pc][:, 0:WKA],
                              xfm_l[pt][:], wa_h[pt][:], start=False, stop=last,
                              deps=(el,), inc=True,
                              bufw=(b_pb[pc],) if last else (),
                              bufr=(b_xfm_h[pt], b_xfm_l[pt], b_wa_h[pt],
                                    b_wa_l[pt]))

                    prev_ckv = None
                    for dc in range(DC):
                        pt = dc % 2
                        lwh = pb.op("sync", "dma_start", out=wa_h[pt][:],
                                    in_=wkva_hi[dc * 128:(dc + 1) * 128, :],
                                    inc=True, bufw=(b_wa_h[pt],), bufr=(b_wkva,))
                        lwl = pb.op("sync", "dma_start", out=wa_l[pt][:],
                                    in_=wkva_lo[dc * 128:(dc + 1) * 128, :],
                                    inc=True, bufw=(b_wa_l[pt],), bufr=(b_wkva,))
                        stp = pb.op("tensor", "transpose", pbank[4 + pt][:, 0:128],
                                    xhat[:, dc * 128:(dc + 1) * 128], id32[:],
                                    deps=(sx,), inc=True,
                                    bufw=(b_pb[4 + pt],), bufr=(b_xhat, b_consts))
                        eh, el = evac_hilo(pbank[4 + pt][:, 0:128],
                                           xfm_h[pt][:], xfm_l[pt][:],
                                           (b_pb[4 + pt],), b_xfm_h[pt],
                                           b_xfm_l[pt], deps=(stp,))
                        if prev_ckv is not None:
                            mm_ckv(dc - 1, prev_ckv)
                        prev_ckv = (pt, eh, el, lwh, lwl)
                    mm_ckv(DC - 1, prev_ckv)
                    mm_done = b_pb[pc].writer
                    sr = rms_stats(lambda o, w, _p=pbank[pc]: _p[:, o:o + w],
                                   KVR, (b_pb[pc],), EPS_RMS, (mm_done,))
                    sc = pb.op("scalar", "activation", cbuf[:], pbank[pc][:, 0:KVR],
                               AF.Copy, scale=mvv[:, 6:7],
                               deps=(sr, mm_done),
                               bufw=(b_cbuf,), bufr=(b_pb[pc], b_mvv))
                    ap_ev = bass.AP(pbank[pc], KVR, [[512, 128], [2, 32]])
                    ap_od = bass.AP(pbank[pc], KVR + 1, [[512, 128], [2, 32]])
                    cos_t = cosk[:, kt * 32:(kt + 1) * 32]
                    sin_t = sink[:, kt * 32:(kt + 1) * 32]
                    r1 = pb.op("vector", "tensor_tensor", krop[:, 0:32], ap_ev,
                               cos_t, op=ALU.mult, deps=(mm_done,),
                               bufw=(b_krop,), bufr=(b_pb[pc], b_consts))
                    r2 = pb.op("vector", "tensor_tensor", krop[:, 32:64], ap_od,
                               cos_t, op=ALU.mult, deps=(mm_done,),
                               bufw=(b_krop,), bufr=(b_pb[pc], b_consts))
                    r3 = pb.op("vector", "tensor_tensor", ktmp[:, 0:32], ap_od,
                               sin_t, op=ALU.mult, deps=(mm_done,),
                               bufw=(b_ktmp,), bufr=(b_pb[pc], b_consts))
                    r4 = pb.op("vector", "tensor_tensor", ktmp[:, 32:64], ap_ev,
                               sin_t, op=ALU.mult, deps=(mm_done,),
                               bufw=(b_ktmp,), bufr=(b_pb[pc], b_consts))
                    r5 = pb.op("vector", "tensor_sub", krop[:, 0:32], krop[:, 0:32],
                               ktmp[:, 0:32], deps=(r1, r3),
                               bufw=(b_krop,), bufr=(b_ktmp,))
                    r6 = pb.op("vector", "tensor_add", krop[:, 32:64], krop[:, 32:64],
                               ktmp[:, 32:64], deps=(r2, r4),
                               bufw=(b_krop,), bufr=(b_ktmp,))
                    for rc in range(KC):
                        pt = rc % 2
                        stc = pb.op("tensor", "transpose", pbank[4 + pt][:, 0:128],
                                    cbuf[:, rc * 128:(rc + 1) * 128], id32[:],
                                    deps=(sc,), inc=True,
                                    bufw=(b_pb[4 + pt],), bufr=(b_cbuf, b_consts))
                        col = rc * NKEY + kt * 128
                        evac_hilo(pbank[4 + pt][:, 0:128],
                                  cpk_hi[:, col:col + 128], cpk_lo[:, col:col + 128],
                                  (b_pb[4 + pt],), b_cpk_h, b_cpk_l, deps=(stc,),
                                  scale=kvw[:, rc:rc + 1])
                    ptk = 4 + kt % 2
                    stk = pb.op("tensor", "transpose", pbank[ptk][0:64, 0:128],
                                krop[:, 0:64], id32[:], deps=(r5, r6), inc=True,
                                bufw=(b_pb[ptk],), bufr=(b_krop, b_consts))
                    kh, kl = evac_hilo(pbank[ptk][0:64, 0:128],
                                       kst_h[0:64, :], kst_l[0:64, :],
                                       (b_pb[ptk],), b_kst_h, b_kst_l,
                                       deps=(stk,))
                    pb.op("scalar", "dma_start",
                          out=kpeT_hi[64:128, kt * 128:(kt + 1) * 128],
                          in_=kst_h[0:64, :], deps=(kh,), inc=True,
                          bufw=(b_kpeT_h,), bufr=(b_kst_h,))
                    pb.op("scalar", "dma_start",
                          out=kpeT_lo[64:128, kt * 128:(kt + 1) * 128],
                          in_=kst_l[0:64, :], deps=(kl,), inc=True,
                          bufw=(b_kpeT_l,), bufr=(b_kst_l,))

                wqa_f = [SBx(ph1, f"wqa_f{i}", [128, RQ]) for i in range(2)]
                wqa_h = [SBx(ph1, f"wqa_h{i}", [128, RQ], BF16) for i in range(2)]
                wqa_l = [SBx(ph1, f"wqa_l{i}", [128, RQ], BF16) for i in range(2)]
                b_wqa_f = [B_(), B_()]
                b_wqa_h = [B_(), B_()]
                b_wqa_l = [B_(), B_()]
                xhg = [xhat2[i % 2] for i in range(G1B)]
                b_xhg = [b_xhat2[i % 2] for i in range(G1B)]
                cqt2 = [SBx(ph1, f"cqt{i}", [128, RQ]) for i in range(2)]
                b_cqt2 = [B_(), B_()]
                NQB = max(1, RQ // 512)
                QW = min(512, RQ)

                for grp in range(OWNT // G1B if phases >= 2 else 0):
                    tiles = [grp * G1B + i for i in range(G1B)]
                    for gi, towm in enumerate(tiles):
                        kt = 7 * TPB + towm
                        bx = towm % 2
                        CUR.update(set_stat(bx))
                        statb, mvv = CUR["statb"], CUR["mvv"]
                        b_stat, b_mvv = CUR["b_stat"], CUR["b_mvv"]
                        lx = pb.op("sync", "dma_start", out=xb[bx][:],
                                   in_=XK[kt * 128:(kt + 1) * 128, :],
                                   inc=True, bufw=(b_xb[bx],))
                        siv = ln_stats(lambda o, w, _b=xb[bx]: _b[:, o:o + w],
                                       D, (b_xb[bx],), EPS_LN, (lx,))
                        pb.op("vector", "tensor_scalar", xhg[gi][:], xb[bx][:],
                              mvv[:, 0:1], mvv[:, 3:4],
                              op0=ALU.subtract, op1=ALU.mult,
                              deps=(siv,), bufw=(b_xhg[gi],),
                              bufr=(b_xb[bx], b_mvv))
                    def mm_cq(dc, st):
                        pt, gi, xi, eh, el, wh, wl = st
                        first = dc == 0
                        last = dc == DC - 1
                        for nq in range(NQB):
                            bk = gi * NQB + nq
                            pb.op("tensor", "matmul", pbank[bk][:, 0:QW],
                                  xfm_h[xi][:],
                                  wqa_h[pt][:, nq * 512:nq * 512 + QW],
                                  start=first, stop=False,
                                  deps=(eh, wh, wl),
                                  bufwar=(b_pb[bk],) if first else ())
                            pb.op("tensor", "matmul", pbank[bk][:, 0:QW],
                                  xfm_h[xi][:],
                                  wqa_l[pt][:, nq * 512:nq * 512 + QW],
                                  start=False, stop=False)
                            lastnq = nq == NQB - 1
                            pb.op("tensor", "matmul", pbank[bk][:, 0:QW],
                                  xfm_l[xi][:],
                                  wqa_h[pt][:, nq * 512:nq * 512 + QW],
                                  start=False, stop=last,
                                  deps=(el,), inc=lastnq or last,
                                  bufw=(b_pb[bk],) if last else (),
                                  bufr=(b_xfm_h[xi], b_xfm_l[xi],
                                        b_wqa_h[pt], b_wqa_l[pt])
                                  if lastnq else ())

                    prev_cq = None
                    for dc in range(DC):
                        pt = dc % 2
                        lw = pb.op("sync", "dma_start", out=wqa_f[pt][:],
                                   in_=WQAT[dc * 128:(dc + 1) * 128, :],
                                   inc=True, bufw=(b_wqa_f[pt],))
                        wh, wl = evac_hilo(wqa_f[pt][:], wqa_h[pt][:], wqa_l[pt][:],
                                           (b_wqa_f[pt],), b_wqa_h[pt],
                                           b_wqa_l[pt], deps=(lw,))
                        for gi in range(G1B):
                            xi = pt * 2 + gi
                            stp = pb.op("tensor", "transpose",
                                        pbank[4 + pt][:, 0:128],
                                        xhg[gi][:, dc * 128:(dc + 1) * 128],
                                        id32[:], deps=(b_xhg[gi].writer,),
                                        inc=True, bufw=(b_pb[4 + pt],),
                                        bufr=(b_xhg[gi], b_consts))
                            eh, el = evac_hilo(pbank[4 + pt][:, 0:128],
                                               xfm_h[xi][:], xfm_l[xi][:],
                                               (b_pb[4 + pt],), b_xfm_h[xi],
                                               b_xfm_l[xi], deps=(stp,))
                            if prev_cq is not None:
                                mm_cq(*prev_cq)
                            prev_cq = (dc, (pt, gi, xi, eh, el, wh, wl))
                    mm_cq(*prev_cq)
                    for gi, towm in enumerate(tiles):
                        CUR.update(set_stat(towm % 2))
                        statb, mvv = CUR["statb"], CUR["mvv"]
                        b_stat, b_mvv = CUR["b_stat"], CUR["b_mvv"]
                        cqt, b_cqt = cqt2[towm % 2], b_cqt2[towm % 2]
                        bks = [gi * NQB + nq for nq in range(NQB)]
                        mmst = tuple(b_pb[bk].writer for bk in bks)
                        sts = None
                        for nq, bk in enumerate(bks):
                            sts = pb.op("vector", "bn_stats",
                                        statb[:, nq * 6:(nq + 1) * 6],
                                        pbank[bk][:, 0:QW], deps=mmst,
                                        bufw=(b_stat,), bufr=(b_pb[bk],))
                        sa = pb.op("vector", "bn_aggr", mvv[:, 0:2],
                                   statb[:, 0:6 * NQB], deps=(sts,),
                                   bufw=(b_mvv,), bufr=(b_stat,))
                        sm = pb.op("vector", "scalar_tensor_tensor", mvv[:, 4:5],
                                   mvv[:, 0:1], mvv[:, 0:1], mvv[:, 1:2],
                                   op0=ALU.mult, op1=ALU.add, deps=(sa,),
                                   bufw=(b_mvv,), bufr=(b_mvv,))
                        s2 = invsigma_ops(4, EPS_RMS, 6, (sm,))
                        scs = None
                        for nq, bk in enumerate(bks):
                            scs = pb.op("scalar", "activation",
                                        cqt[:, nq * 512:nq * 512 + QW],
                                        pbank[bk][:, 0:QW], AF.Copy,
                                        scale=mvv[:, 6:7],
                                        deps=(s2,) + mmst,
                                        bufw=(b_cqt,), bufr=(b_pb[bk], b_mvv))
                        for rc in range(RC):
                            pt = rc % 2
                            stq = pb.op("tensor", "transpose",
                                        pbank[4 + pt][:, 0:128],
                                        cqt[:, rc * 128:(rc + 1) * 128], id32[:],
                                        deps=(scs,), inc=True,
                                        bufw=(b_pb[4 + pt],),
                                        bufr=(b_cqt, b_consts))
                            col = rc * OWN + towm * 128
                            evac_hilo(pbank[4 + pt][:, 0:128],
                                      cq_hi[:, col:col + 128],
                                      cq_lo[:, col:col + 128],
                                      (b_pb[4 + pt],), b_cq_h, b_cq_l,
                                      deps=(stq,), scale=qaw[:, rc:rc + 1])
                pb.barrier()

            with ExitStack() as ph2:
                wqb_f = [SBx(ph2, f"wqb_f{i}", [128, RC * ETW]) for i in range(2)]
                wqb_h = [SBx(ph2, f"wqb_h{i}", [128, RC * ETW], BF16) for i in range(2)]
                wqb_l = [SBx(ph2, f"wqb_l{i}", [128, RC * ETW], BF16) for i in range(2)]
                b_wqb_f = [B_(), B_()]
                b_wqb_h = [B_(), B_()]
                b_wqb_l = [B_(), B_()]
                qtmp2 = [SBx(ph2, f"qtmp{i}", [128, ETW]) for i in range(2)]
                b_qtmp2 = [B_(), B_()]
                qrt2 = [SBx(ph2, f"qrt{i}", [128, HPE * 64]) for i in range(2)]
                b_qrt2 = [B_(), B_()]
                qtile2 = [SBx(ph2, f"qtile{i}", [128, ETW]) for i in range(2)]
                b_qtile2 = [B_(), B_()]
                qst_h2 = [SBx(ph2, f"qst_h{i}", [128, 128], BF16) for i in range(2)]
                qst_l2 = [SBx(ph2, f"qst_l{i}", [128, 128], BF16) for i in range(2)]
                b_qst_h2 = [B_(), B_()]
                b_qst_l2 = [B_(), B_()]
                wqbt_v = WQBT.ap().rearrange("(c p) w -> p c w", p=128)

                for et in range(NET if phases >= 3 else 0):
                    wb = et % 2
                    lw = pb.op("sync", "dma_start",
                               out=wqb_f[wb].ap().rearrange("p (c w) -> p c w", c=RC),
                               in_=wqbt_v[:, :, et * ETW:(et + 1) * ETW],
                               inc=True, bufw=(b_wqb_f[wb],))
                    wh, wl = evac_hilo(wqb_f[wb][:], wqb_h[wb][:], wqb_l[wb][:],
                                       (b_wqb_f[wb],), b_wqb_h[wb], b_wqb_l[wb],
                                       deps=(lw,))
                    def q_back(towm, mmq):
                        pc = towm % 2
                        qtmp, b_qtmp = qtmp2[pc], b_qtmp2[pc]
                        qrt, b_qrt = qrt2[pc], b_qrt2[pc]
                        qtile, b_qtile = qtile2[pc], b_qtile2[pc]
                        qst_h, b_qst_h = qst_h2[pc], b_qst_h2[pc]
                        qst_l, b_qst_l = qst_l2[pc], b_qst_l2[pc]
                        sq = pb.op("scalar", "activation", qtmp[:, 0:ETW],
                                   pbank[pc][:, 0:ETW], AF.Copy,
                                   scale=SCALE, deps=(mmq,),
                                   bufw=(b_qtmp,), bufr=(b_pb[pc],))
                        ap_nope_i = bass.AP(qtmp, 0, [[ETW, 128], [128, HPE], [1, 64]])
                        ap_nope_o = bass.AP(qtile, 0, [[ETW, 128], [128, HPE], [1, 64]])
                        c0 = pb.op("vector", "tensor_copy", out=ap_nope_o,
                                   in_=ap_nope_i, deps=(sq,),
                                   bufw=(b_qtile,), bufr=(b_qtmp,))
                        ap_ev = bass.AP(qtmp, 64, [[ETW, 128], [128, HPE], [2, 32]])
                        ap_od = bass.AP(qtmp, 65, [[ETW, 128], [128, HPE], [2, 32]])
                        kt_own = 7 * TPB + towm
                        cos_t = bass.AP(cosk, kt_own * 32,
                                        [[NKT * 32, 128], [0, HPE], [1, 32]])
                        sin_t = bass.AP(sink, kt_own * 32,
                                        [[NKT * 32, 128], [0, HPE], [1, 32]])
                        o1 = bass.AP(qtile, 64, [[ETW, 128], [128, HPE], [1, 32]])
                        o2 = bass.AP(qtile, 96, [[ETW, 128], [128, HPE], [1, 32]])
                        t3 = bass.AP(qrt, 0, [[HPE * 64, 128], [64, HPE], [1, 32]])
                        t4 = bass.AP(qrt, 32, [[HPE * 64, 128], [64, HPE], [1, 32]])
                        g1 = pb.op("vector", "tensor_tensor", o1, ap_ev, cos_t,
                                   op=ALU.mult, deps=(sq,), bufw=(b_qtile,),
                                   bufr=(b_qtmp, b_consts))
                        g2 = pb.op("vector", "tensor_tensor", o2, ap_od, cos_t,
                                   op=ALU.mult, deps=(sq,), bufw=(b_qtile,),
                                   bufr=(b_qtmp, b_consts))
                        g3 = pb.op("vector", "tensor_tensor", t3, ap_od, sin_t,
                                   op=ALU.mult, deps=(sq,), bufw=(b_qrt,),
                                   bufr=(b_qtmp, b_consts))
                        g4 = pb.op("vector", "tensor_tensor", t4, ap_ev, sin_t,
                                   op=ALU.mult, deps=(sq,), bufw=(b_qrt,),
                                   bufr=(b_qtmp, b_consts))
                        g5 = pb.op("vector", "tensor_sub", o1, o1, t3,
                                   deps=(g1, g3), bufw=(b_qtile,), bufr=(b_qrt,))
                        g6 = pb.op("vector", "tensor_add", o2, o2, t4,
                                   deps=(g2, g4, g5), bufw=(b_qtile,),
                                   bufr=(b_qrt,))
                        side = towm // TPB
                        itile = towm % TPB
                        for hh in range(HPE):
                            h = et * HPE + hh
                            pt2 = hh % 2
                            stq = pb.op("tensor", "transpose",
                                        pbank[4 + pt2][:, 0:128],
                                        qtile[:, hh * 128:(hh + 1) * 128],
                                        id32[:], deps=(g6, c0), inc=True,
                                        bufw=(b_pb[4 + pt2],),
                                        bufr=(b_qtile, b_consts))
                            qh, ql = evac_hilo(pbank[4 + pt2][:, 0:128],
                                               qst_h[:], qst_l[:],
                                               (b_pb[4 + pt2],), b_qst_h,
                                               b_qst_l, deps=(stq,))
                            pb.op("scalar", "dma_start",
                                  out=qsp_hi[h, side * 128:(side + 1) * 128,
                                             itile * 128:(itile + 1) * 128],
                                  in_=qst_h[:], deps=(qh,), inc=True,
                                  bufw=(b_qsp,), bufr=(b_qst_h,))
                            pb.op("scalar", "dma_start",
                                  out=qsp_lo[h, side * 128:(side + 1) * 128,
                                             itile * 128:(itile + 1) * 128],
                                  in_=qst_l[:], deps=(ql,), inc=True,
                                  bufw=(b_qsp,), bufr=(b_qst_l,))

                    prev_q = None
                    for towm in range(OWNT):
                        pc = towm % 2
                        for rc in range(RC):
                            first = rc == 0
                            last = rc == RC - 1
                            cq_sl = slice(rc * OWN + towm * 128,
                                          rc * OWN + (towm + 1) * 128)
                            pb.op("tensor", "matmul", pbank[pc][:, 0:ETW],
                                  cq_hi[:, cq_sl],
                                  wqb_h[wb][:, rc * ETW:(rc + 1) * ETW],
                                  start=first, stop=False,
                                  deps=(wh, wl, b_cq_h.writer, b_cq_l.writer),
                                  bufwar=(b_pb[pc],) if first else ())
                            pb.op("tensor", "matmul", pbank[pc][:, 0:ETW],
                                  cq_hi[:, cq_sl],
                                  wqb_l[wb][:, rc * ETW:(rc + 1) * ETW],
                                  start=False, stop=False)
                            pb.op("tensor", "matmul", pbank[pc][:, 0:ETW],
                                  cq_lo[:, cq_sl],
                                  wqb_h[wb][:, rc * ETW:(rc + 1) * ETW],
                                  start=False, stop=last, inc=last,
                                  bufw=(b_pb[pc],) if last else (),
                                  bufr=(b_cq_h, b_cq_l, b_wqb_h[wb],
                                        b_wqb_l[wb]) if last else ())
                        mmq = b_pb[pc].writer
                        if prev_q is not None:
                            q_back(*prev_q)
                        prev_q = (towm, mmq)
                    q_back(*prev_q)
                pb.barrier()

            with ExitStack() as ph3:
                v_hi = SBx(ph3, "v_hi", [128, NKT * VST], BF16)
                v_lo = SBx(ph3, "v_lo", [128, NKT * VST], BF16)
                vstage = [SBx(ph3, f"vstage{i}", [128, 512]) for i in range(2)]
                b_vst = [B_(), B_()]
                b_v_h, b_v_l = B_(), B_()
                b_kn_h, b_kn_l = b_kpeT_h, b_kpeT_l
                knw_h = [SBx(ph3, f"knw_h{i}", [128, KC * 64], BF16)
                         for i in range(2)]
                knw_l = [SBx(ph3, f"knw_l{i}", [128, KC * 64], BF16)
                         for i in range(2)]
                vw_h = [SBx(ph3, f"vw_h{i}", [128, KC * VD], BF16)
                        for i in range(2)]
                vw_l = [SBx(ph3, f"vw_l{i}", [128, KC * VD], BF16)
                        for i in range(2)]
                b_knw_h = [B_(), B_()]
                b_knw_l = [B_(), B_()]
                b_vw_h = [B_(), B_()]
                b_vw_l = [B_(), B_()]
                qsl_h = [SBx(ph3, f"qsl_h{i}", [128, 7 * 512], BF16)
                         for i in range(2)]
                qsl_l = [SBx(ph3, f"qsl_l{i}", [128, 7 * 512], BF16)
                         for i in range(2)]
                qow_h = [SBx(ph3, f"qow_h{i}", [128, OWN], BF16)
                         for i in range(2)]
                qow_l = [SBx(ph3, f"qow_l{i}", [128, OWN], BF16)
                         for i in range(2)]
                b_qsl_h = [B_(), B_()]
                b_qsl_l = [B_(), B_()]
                b_qow_h = [B_(), B_()]
                b_qow_l = [B_(), B_()]
                p32 = [SBx(ph3, f"p32_{i}", [128, 512]) for i in range(3)]
                b_p32 = [B_() for _ in range(3)]
                pt_h = [SBx(ph3, f"pt_h{i}", [128, 512], BF16) for i in range(3)]
                pt_l = [SBx(ph3, f"pt_l{i}", [128, 512], BF16) for i in range(3)]
                b_pth = [B_() for _ in range(3)]
                b_ptl = [B_() for _ in range(3)]
                SLOTW = 132
                slots = SBx(ph3, "slots", [128, 36 * SLOTW])
                b_slots = B_()
                oacc2 = [SBx(ph3, f"oacc{i}", [128, SLOTW]) for i in range(2)]
                b_oacc2 = [B_(), B_()]
                rdiv = SBx(ph3, "rdiv", [128, 4])
                b_rdiv = B_()
                ofst_h2 = [SBx(ph3, f"ofst_h{i}", [128, 128], BF16)
                           for i in range(2)]
                ofst_l2 = [SBx(ph3, f"ofst_l{i}", [128, 128], BF16)
                           for i in range(2)]
                b_ofst_h2 = [B_(), B_()]
                b_ofst_l2 = [B_(), B_()]

                sbank = [pbank[4], pbank[5], pbb[0]]
                b_sbank = [b_pb[4], b_pb[5], b_pbb[0]]
                psJ = [pbank[0], pbank[1], pbank[2], pbank[3]]
                b_psJ = [b_pb[0], b_pb[1], b_pb[2], b_pb[3]]

                pb.op("vector", "memset",
                      bass.AP(v_hi, 128, [[NKT * VST, 128], [VST, NKT]]), 1.0,
                      bufw=(b_v_h,))
                pb.op("vector", "memset",
                      bass.AP(v_lo, 128, [[NKT * VST, 128], [VST, NKT]]), 0.0,
                      bufw=(b_v_l,))

                wkvb_hv = wkvb_hi.ap().rearrange("(c p) w -> p c w", p=128)
                wkvb_lv = wkvb_lo.ap().rearrange("(c p) w -> p c w", p=128)
                qsp_hi_flat = qsp_hi.ap().rearrange("h r w -> (h r) w")
                qsp_lo_flat = qsp_lo.ap().rearrange("h r w -> (h r) w")

                def prefetch_head(h, par):
                    l1 = pb.op("sync", "dma_start",
                               out=knw_h[par].ap().rearrange(
                                   "p (c w) -> p c w", c=KC),
                               in_=wkvb_hv[:, :, h * KVD:h * KVD + 64],
                               inc=True, bufw=(b_knw_h[par],), bufr=(b_wkvb,))
                    l2 = pb.op("sync", "dma_start",
                               out=knw_l[par].ap().rearrange(
                                   "p (c w) -> p c w", c=KC),
                               in_=wkvb_lv[:, :, h * KVD:h * KVD + 64],
                               inc=True, bufw=(b_knw_l[par],), bufr=(b_wkvb,))
                    l3 = pb.op("sync", "dma_start",
                               out=vw_h[par].ap().rearrange(
                                   "p (c w) -> p c w", c=KC),
                               in_=wkvb_hv[:, :, h * KVD + 64:(h + 1) * KVD],
                               inc=True, bufw=(b_vw_h[par],), bufr=(b_wkvb,))
                    l4 = pb.op("sync", "dma_start",
                               out=vw_l[par].ap().rearrange(
                                   "p (c w) -> p c w", c=KC),
                               in_=wkvb_lv[:, :, h * KVD + 64:(h + 1) * KVD],
                               inc=True, bufw=(b_vw_l[par],), bufr=(b_wkvb,))
                    for j in range(7):
                        pb.op("gpsimd", "indirect_dma_start",
                              out=qsl_h[par][:, j * 512:j * 512 + TPB * 128],
                              out_offset=None, in_=qsp_hi_flat,
                              in_offset=IndirectOffsetOnAxis(
                                  ap=qsel[:, h * 7 + j:h * 7 + j + 1], axis=0),
                              inc=True, bufw=(b_qsl_h[par],),
                              bufr=(b_qsp, b_consts))
                        pb.op("gpsimd", "indirect_dma_start",
                              out=qsl_l[par][:, j * 512:j * 512 + TPB * 128],
                              out_offset=None, in_=qsp_lo_flat,
                              in_offset=IndirectOffsetOnAxis(
                                  ap=qsel[:, h * 7 + j:h * 7 + j + 1], axis=0),
                              inc=True, bufw=(b_qsl_l[par],),
                              bufr=(b_qsp, b_consts))
                    for side in range(2):
                        pb.op("sync", "dma_start",
                              out=qow_h[par][:, side * 512:(side + 1) * 512],
                              in_=qsp_hi[h, side * 128:(side + 1) * 128, :],
                              inc=True, bufw=(b_qow_h[par],), bufr=(b_qsp,))
                        pb.op("sync", "dma_start",
                              out=qow_l[par][:, side * 512:(side + 1) * 512],
                              in_=qsp_lo[h, side * 128:(side + 1) * 128, :],
                              inc=True, bufw=(b_qow_l[par],), bufr=(b_qsp,))

                spctr = [0]

                def score_chunk(j, kc, par):
                    own = j >= 7
                    side = j - 7 if own else None
                    if own:
                        qc = side * 512 + kc * 128
                        w = 512 - kc * 128
                        qh = qow_h[par][:, qc:qc + w]
                        ql = qow_l[par][:, qc:qc + w]
                        qbufs = (b_qow_h[par], b_qow_l[par])
                    else:
                        w = 512
                        qh = qsl_h[par][:, j * 512:(j + 1) * 512]
                        ql = qsl_l[par][:, j * 512:(j + 1) * 512]
                        qbufs = (b_qsl_h[par], b_qsl_l[par])
                    kcol = (j * TPB + kc) * 128
                    sp = spctr[0] % 3
                    spctr[0] += 1
                    ps, b_ps = sbank[sp], b_sbank[sp]
                    pb.op("tensor", "matmul", ps[:, 0:w],
                          kpeT_hi[:, kcol:kcol + 128], qh,
                          start=True, stop=False,
                          bufd=qbufs + (b_kn_h, b_kn_l),
                          bufwar=(b_ps,))
                    pb.op("tensor", "matmul", ps[:, 0:w],
                          kpeT_hi[:, kcol:kcol + 128], ql,
                          start=False, stop=False)
                    smm = pb.op("tensor", "matmul", ps[:, 0:w],
                                kpeT_lo[:, kcol:kcol + 128], qh,
                                start=False, stop=True, inc=True,
                                bufw=(b_ps,),
                                bufr=qbufs + (b_kn_h, b_kn_l))
                    if own:
                        smm = pb.op("vector", "tensor_add", ps[:, 0:128],
                                    ps[:, 0:128], tri[:], deps=(smm,),
                                    bufw=(b_ps,), bufr=(b_consts,))
                    pp = spctr[0] % 3
                    ex = pb.op("scalar", "activation", p32[pp][:, 0:w],
                               ps[:, 0:w], AF.Exp, bias=negb[:, 0:1],
                               scale=1.0, deps=(smm,),
                               bufw=(b_p32[pp],), bufr=(b_ps, b_consts))
                    hh, ll = evac_hilo(p32[pp][:, 0:w], pt_h[pp][:, 0:w],
                                       pt_l[pp][:, 0:w], (b_p32[pp],),
                                       b_pth[pp], b_ptl[pp], deps=(ex,))
                    return (j, kc, w, pp, hh, ll)

                def pv_chunk(st):
                    j, kc, w, pp, hh, ll = st
                    own = j >= 7
                    vt = j * TPB + kc
                    vs_h = v_hi[:, vt * VST:vt * VST + 129]
                    vs_l = v_lo[:, vt * VST:vt * VST + 129]
                    jpar = j % 2
                    for it in range(kc if own else 0, TPB):
                        qoff = (it - kc) * 128 if own else it * 128
                        bi = jpar * 2 + it // 2
                        bk, b_bk = psJ[bi], b_psJ[bi]
                        col = (it % 2) * SLOTW
                        reg = bk[:, col:col + 129]
                        bstart = kc == 0 and it % 2 == 0
                        bstop = (it % 2 == 1
                                 and kc == (it if own else TPB - 1))
                        pb.op("tensor", "matmul", reg,
                              pt_h[pp][:, qoff:qoff + 128], vs_h,
                              start=bstart, stop=False, deps=(hh, ll),
                              bufd=(b_v_h, b_v_l),
                              bufwar=(b_bk,) if bstart else ())
                        pb.op("tensor", "matmul", reg,
                              pt_h[pp][:, qoff:qoff + 128], vs_l,
                              start=False, stop=False)
                        pb.op("tensor", "matmul", reg,
                              pt_l[pp][:, qoff:qoff + 128], vs_h,
                              start=False, stop=bstop, inc=bstop,
                              bufw=(b_bk,) if bstop else (),
                              bufr=(b_pth[pp], b_ptl[pp], b_v_h, b_v_l)
                              if bstop else ())
                    if kc == TPB - 1:
                        for it in range(TPB):
                            bi = jpar * 2 + it // 2
                            col = (it % 2) * SLOTW
                            pb.op("vector", "tensor_copy",
                                  slots[:, (j * 4 + it) * SLOTW:
                                        (j * 4 + it) * SLOTW + 129],
                                  psJ[bi][:, col:col + 129],
                                  deps=(b_psJ[bi].writer,),
                                  bufw=(b_slots,), bufr=(b_psJ[bi],))

                prefetch_head(0, 0)
                for h in range(H if phases >= 4 else 0):
                    par = h % 2
                    l12 = (b_knw_h[par].writer, b_knw_l[par].writer)
                    l34 = (b_vw_h[par].writer, b_vw_l[par].writer)
                    for t9 in range(9):
                        sp = t9 % 2
                        ps, b_ps = sbank[sp], b_sbank[sp]
                        for rc in range(KC):
                            first = rc == 0
                            last = rc == KC - 1
                            cp_sl = slice(rc * NKEY + t9 * 512,
                                          rc * NKEY + t9 * 512 + 512)
                            pb.op("tensor", "matmul", ps[0:64, 0:512],
                                  knw_h[par][:, rc * 64:(rc + 1) * 64],
                                  cpk_hi[:, cp_sl], start=first, stop=False,
                                  deps=l12,
                                  bufd=(b_cpk_h, b_cpk_l),
                                  bufwar=(b_ps,) if first else ())
                            pb.op("tensor", "matmul", ps[0:64, 0:512],
                                  knw_h[par][:, rc * 64:(rc + 1) * 64],
                                  cpk_lo[:, cp_sl], start=False, stop=False)
                            pb.op("tensor", "matmul", ps[0:64, 0:512],
                                  knw_l[par][:, rc * 64:(rc + 1) * 64],
                                  cpk_hi[:, cp_sl], start=False, stop=last,
                                  inc=last,
                                  bufw=(b_ps,) if last else (),
                                  bufr=(b_knw_h[par], b_knw_l[par],
                                        b_cpk_h, b_cpk_l) if last else ())
                        evac_hilo(ps[0:64, 0:512],
                                  kpeT_hi[0:64, t9 * 512:(t9 + 1) * 512],
                                  kpeT_lo[0:64, t9 * 512:(t9 + 1) * 512],
                                  (b_ps,), b_kn_h, b_kn_l,
                                  deps=(b_ps.writer,))
                    for t9 in range(9):
                        sp = t9 % 2
                        ps, b_ps = sbank[sp], b_sbank[sp]
                        for rc in range(KC):
                            first = rc == 0
                            last = rc == KC - 1
                            cp_sl = slice(rc * NKEY + t9 * 512,
                                          rc * NKEY + t9 * 512 + 512)
                            pb.op("tensor", "matmul", ps[:, 0:512],
                                  vw_h[par][:, rc * VD:(rc + 1) * VD],
                                  cpk_hi[:, cp_sl], start=first, stop=False,
                                  deps=l34,
                                  bufd=(b_cpk_h, b_cpk_l),
                                  bufwar=(b_ps,) if first else ())
                            pb.op("tensor", "matmul", ps[:, 0:512],
                                  vw_h[par][:, rc * VD:(rc + 1) * VD],
                                  cpk_lo[:, cp_sl], start=False, stop=False)
                            pb.op("tensor", "matmul", ps[:, 0:512],
                                  vw_l[par][:, rc * VD:(rc + 1) * VD],
                                  cpk_hi[:, cp_sl], start=False, stop=last,
                                  inc=last,
                                  bufw=(b_ps,) if last else (),
                                  bufr=(b_vw_h[par], b_vw_l[par],
                                        b_cpk_h, b_cpk_l) if last else ())
                        sv9 = pb.op("scalar", "copy", vstage[sp][:, 0:512],
                                    ps[:, 0:512], deps=(b_ps.writer,),
                                    bufw=(b_vst[sp],), bufr=(b_ps,))
                        for kc9 in range(4):
                            vt = t9 * 4 + kc9
                            tb9 = vt % 2
                            stv = pb.op("tensor", "transpose",
                                        pbb[tb9][:, 0:128],
                                        vstage[sp][:, kc9 * 128:
                                                   (kc9 + 1) * 128],
                                        id32[:], deps=(sv9,), inc=True,
                                        bufw=(b_pbb[tb9],),
                                        bufr=(b_vst[sp], b_consts))
                            evac_hilo(pbb[tb9][:, 0:128],
                                      v_hi[:, vt * VST:vt * VST + VD],
                                      v_lo[:, vt * VST:vt * VST + VD],
                                      (b_pbb[tb9],), b_v_h, b_v_l,
                                      deps=(stv,))
                    if h + 1 < H:
                        prefetch_head(h + 1, 1 - par)
                    pend = []
                    for j in range(9):
                        for kc in range(TPB):
                            pend.append(score_chunk(j, kc, par))
                            if len(pend) > 2:
                                pv_chunk(pend.pop(0))
                    while pend:
                        pv_chunk(pend.pop(0))
                    for side in range(2):
                        oacc, b_oacc = oacc2[side], b_oacc2[side]
                        ofst_h, b_ofst_h = ofst_h2[side], b_ofst_h2[side]
                        ofst_l, b_ofst_l = ofst_l2[side], b_ofst_l2[side]
                        selx = sela if side == 0 else selb
                        for it in range(TPB):
                            towm = side * TPB + it
                            a0 = pb.op("vector", "tensor_scalar",
                                       oacc[:, 0:129],
                                       slots[:, it * SLOTW:it * SLOTW + 129],
                                       selx[:, 0:1], None, op0=ALU.mult,
                                       bufw=(b_oacc,),
                                       bufr=(b_slots, b_consts))
                            for j in range(1, 9):
                                sc = (j * 4 + it) * SLOTW
                                a0 = pb.op("vector", "scalar_tensor_tensor",
                                           oacc[:, 0:129],
                                           slots[:, sc:sc + 129],
                                           selx[:, j:j + 1], oacc[:, 0:129],
                                           op0=ALU.mult, op1=ALU.add,
                                           deps=(a0,), bufw=(b_oacc,),
                                           bufr=(b_slots, b_consts))
                            rc1 = pb.op("vector", "reciprocal",
                                        rdiv[:, side:side + 1],
                                        oacc[:, 128:129], deps=(a0,),
                                        bufw=(b_rdiv,), bufr=(b_oacc,))
                            dv = pb.op("vector", "tensor_scalar",
                                       oacc[:, 0:128], oacc[:, 0:128],
                                       rdiv[:, side:side + 1], None,
                                       op0=ALU.mult, deps=(rc1,),
                                       bufw=(b_oacc,), bufr=(b_rdiv,))
                            sto = pb.op("tensor", "transpose",
                                        pbb[side][:, 0:128], oacc[:, 0:128],
                                        id32[:], deps=(dv,), inc=True,
                                        bufw=(b_pbb[side],),
                                        bufr=(b_oacc, b_consts))
                            ohh, oll = evac_hilo(pbb[side][:, 0:128],
                                                 ofst_h[:], ofst_l[:],
                                                 (b_pbb[side],), b_ofst_h,
                                                 b_ofst_l, deps=(sto,))
                            pb.op("gpsimd", "dma_start",
                                  out=osp_hi[h, towm], in_=ofst_h[:],
                                  deps=(ohh,), inc=True, bufw=(b_osp,),
                                  bufr=(b_ofst_h,))
                            pb.op("gpsimd", "dma_start",
                                  out=osp_lo[h, towm], in_=ofst_l[:],
                                  deps=(oll,), inc=True, bufw=(b_osp,),
                                  bufr=(b_ofst_l,))
            pb.barrier()

        with ExitStack() as ph4:
            pk4 = [PSx(ph4, f"pk4_{i}", [128, 512]) for i in range(8)]
            b_pk4 = [B_() for _ in range(8)]
            attn4 = SBx(ph4, "attn4", [128, TQG * D])
            b_attn4 = B_()
            wo_f = [SBx(ph4, f"wo_f{i}", [128, DQW]) for i in range(2)]
            wo_h = [SBx(ph4, f"wo_h{i}", [128, DQW], BF16) for i in range(2)]
            wo_l = [SBx(ph4, f"wo_l{i}", [128, DQW], BF16) for i in range(2)]
            b_wo_f = [B_(), B_()]
            b_wo_h = [B_(), B_()]
            b_wo_l = [B_(), B_()]
            ofm_h = [SBx(ph4, f"ofm_h{i}", [128, TQG * 128], BF16) for i in range(2)]
            ofm_l = [SBx(ph4, f"ofm_l{i}", [128, TQG * 128], BF16) for i in range(2)]
            b_ofm_h = [B_(), B_()]
            b_ofm_l = [B_(), B_()]
            x2b = SBx(ph4, "x2b", [128, D])
            xh2 = SBx(ph4, "xh2", [128, D])
            ffnb = SBx(ph4, "ffnb", [128, D])
            b_x2, b_xh2, b_ffn = B_(), B_(), B_()
            xfm2_h = SBx(ph4, "xfm2_h", [128, D], BF16)
            xfm2_l = SBx(ph4, "xfm2_l", [128, D], BF16)
            b_xfm2_h, b_xfm2_l = B_(), B_()
            xb4 = SBx(ph4, "xb4", [128, D])
            b_xb4 = B_()
            eb_h = SBx(ph4, "eb_h", [E, D], BF16)
            eb_l = SBx(ph4, "eb_l", [E, D], BF16)
            b_ebf, b_eb_h, b_eb_l = B_(), B_(), B_()
            rw_h = SBx(ph4, "rw_h", [128, DC * E], BF16)
            rw_l = SBx(ph4, "rw_l", [128, DC * E], BF16)
            b_rwf, b_rwh, b_rwl = B_(), B_(), B_()
            rb_f = SBx(ph4, "rb_f", [1, E])
            rb_h = SBx(ph4, "rb_h", [1, E], BF16)
            rb_l = SBx(ph4, "rb_l", [1, E], BF16)
            b_rbf, b_rbh, b_rbl = B_(), B_(), B_()
            pex = SBx(ph4, "pex", [128, E])
            pnr = SBx(ph4, "pnr", [128, E])
            wdn = SBx(ph4, "wdn", [128, E])
            top8 = SBx(ph4, "top8", [128, 8])
            b_pex, b_pnr, b_wdn, b_top8 = B_(), B_(), B_(), B_()
            wd_h = SBx(ph4, "wd_h", [128, E], BF16)
            wd_l = SBx(ph4, "wd_l", [128, E], BF16)
            b_wdh, b_wdl = B_(), B_()
            wdt_h = SBx(ph4, "wdt_h", [64, 128], BF16)
            wdt_l = SBx(ph4, "wdt_l", [64, 128], BF16)
            b_wdth, b_wdtl = B_(), B_()
            st4 = SBx(ph4, "st4", [128, 8 * 6])
            mv4 = SBx(ph4, "mv4", [128, 8])
            b_st4, b_mv4 = B_(), B_()

            for c0 in range(0, D, DQW):
                le = pb.op("sync", "dma_start", out=wo_f[0][0:E, 0:DQW],
                           in_=EB[:, c0:c0 + DQW], inc=True, bufw=(b_wo_f[0],))
                evac_hilo(wo_f[0][0:E, 0:DQW], eb_h[0:E, c0:c0 + DQW],
                          eb_l[0:E, c0:c0 + DQW], (b_wo_f[0],), b_eb_h, b_eb_l,
                          deps=(le,))
            rwt_v = RWT.ap().rearrange("(c p) w -> p c w", p=128)
            CPG = max(1, DQW // E)
            for cg in range(0, DC, CPG):
                ng = min(CPG, DC - cg)
                lr = pb.op("sync", "dma_start",
                           out=bass.AP(wo_f[0], 0, [[DQW, 128], [E, ng], [1, E]]),
                           in_=rwt_v[:, cg:cg + ng, :],
                           inc=True, bufw=(b_wo_f[0],))
                evac_hilo(wo_f[0][:, 0:ng * E], rw_h[:, cg * E:(cg + ng) * E],
                          rw_l[:, cg * E:(cg + ng) * E], (b_wo_f[0],),
                          b_rwh, b_rwl, deps=(lr,))
            lb = pb.op("sync", "dma_start", out=rb_f[:], in_=RB[:], inc=True,
                       bufw=(b_rbf,))
            evac_hilo(rb_f[:], rb_h[:], rb_l[:], (b_rbf,), b_rbh, b_rbl,
                      deps=(lb,))

            def ln4(src_ap_fn, ncol, src_bufs, eps, deps, mcol):
                nch = (ncol + 511) // 512
                sts = None
                for ci in range(nch):
                    w = min(512, ncol - ci * 512)
                    sts = pb.op("vector", "bn_stats", st4[:, ci * 6:(ci + 1) * 6],
                                src_ap_fn(ci * 512, w), deps=tuple(deps),
                                bufw=(b_st4,), bufr=src_bufs)
                sa = pb.op("vector", "bn_aggr", mv4[:, mcol:mcol + 2],
                           st4[:, 0:6 * nch], deps=(sts,), bufw=(b_mv4,),
                           bufr=(b_st4,))
                s0 = pb.op("vector", "tensor_scalar_add",
                           mv4[:, mcol + 3:mcol + 4],
                           mv4[:, mcol + 1:mcol + 2], float(eps), deps=(sa,),
                           bufw=(b_mv4,), bufr=(b_mv4,))
                s1 = pb.op("scalar", "activation", mv4[:, mcol + 3:mcol + 4],
                           mv4[:, mcol + 3:mcol + 4], AF.Ln,
                           scale=1.0, deps=(s0,), bufw=(b_mv4,), bufr=(b_mv4,))
                s2 = pb.op("scalar", "activation", mv4[:, mcol + 3:mcol + 4],
                           mv4[:, mcol + 3:mcol + 4], AF.Exp,
                           scale=-0.5, deps=(s1,), bufw=(b_mv4,), bufr=(b_mv4,))
                return s2

            NDT = DQW // 512
            for tq in range(OWNT // TQG if phases >= 5 else 0):
                for dq in range(D // DQW):
                    for hv in range(H):
                        wb = hv % 2
                        lw = pb.op("sync", "dma_start", out=wo_f[wb][:],
                                   in_=WOT[hv * 128:(hv + 1) * 128,
                                           dq * DQW:(dq + 1) * DQW],
                                   inc=True, bufw=(b_wo_f[wb],))
                        wh, wl = evac_hilo(wo_f[wb][:], wo_h[wb][:], wo_l[wb][:],
                                           (b_wo_f[wb],), b_wo_h[wb], b_wo_l[wb],
                                           deps=(lw,))
                        lo1 = pb.op("scalar", "dma_start",
                                    out=ofm_h[wb].ap().rearrange(
                                        "p (t w) -> p t w", t=TQG),
                                    in_=osp_hi.ap()[hv, tq * TQG:(tq + 1) * TQG]
                                    .rearrange("t p w -> p t w"),
                                    inc=True, bufw=(b_ofm_h[wb],), bufr=(b_osp,))
                        lo2 = pb.op("scalar", "dma_start",
                                    out=ofm_l[wb].ap().rearrange(
                                        "p (t w) -> p t w", t=TQG),
                                    in_=osp_lo.ap()[hv, tq * TQG:(tq + 1) * TQG]
                                    .rearrange("t p w -> p t w"),
                                    inc=True, bufw=(b_ofm_l[wb],), bufr=(b_osp,))
                        for t4 in range(TQG):
                            for ndt in range(NDT):
                                bk = t4 * NDT + ndt
                                first = hv == 0
                                last = hv == H - 1
                                pb.op("tensor", "matmul", pk4[bk][:, 0:512],
                                      ofm_h[wb][:, t4 * 128:(t4 + 1) * 128],
                                      wo_h[wb][:, ndt * 512:(ndt + 1) * 512],
                                      start=first, stop=False,
                                      deps=(wh, wl, lo1, lo2),
                                      bufwar=(b_pk4[bk],) if first else ())
                                pb.op("tensor", "matmul", pk4[bk][:, 0:512],
                                      ofm_h[wb][:, t4 * 128:(t4 + 1) * 128],
                                      wo_l[wb][:, ndt * 512:(ndt + 1) * 512],
                                      start=False, stop=False)
                                lastb = ndt == NDT - 1 and t4 == TQG - 1
                                pb.op("tensor", "matmul", pk4[bk][:, 0:512],
                                      ofm_l[wb][:, t4 * 128:(t4 + 1) * 128],
                                      wo_h[wb][:, ndt * 512:(ndt + 1) * 512],
                                      start=False, stop=last, inc=lastb or last,
                                      bufw=(b_pk4[bk],) if last else (),
                                      bufr=(b_ofm_h[wb], b_ofm_l[wb],
                                            b_wo_h[wb], b_wo_l[wb])
                                      if lastb else ())
                    for t4 in range(TQG):
                        for ndt in range(NDT):
                            bk = t4 * NDT + ndt
                            pb.op("scalar", "copy",
                                  attn4[:, t4 * D + dq * DQW + ndt * 512:
                                        t4 * D + dq * DQW + (ndt + 1) * 512],
                                  pk4[bk][:, 0:512], deps=(b_pk4[bk].writer,),
                                  bufw=(b_attn4,), bufr=(b_pk4[bk],))
                for t4 in range(TQG):
                    towm = tq * TQG + t4

                    def att(o, w, _t=t4):
                        return attn4[:, _t * D + o:_t * D + o + w]

                    s2 = ln4(att, D, (b_attn4,), EPS_LN, (b_attn4.writer,), 0)
                    lx = pb.op("sync", "dma_start", out=xb4[:],
                               in_=XK[(7 * TPB + towm) * 128:
                                      (7 * TPB + towm + 1) * 128, :],
                               inc=True, bufw=(b_xb4,))
                    t1 = pb.op("vector", "tensor_scalar", x2b[:], att(0, D),
                               mv4[:, 0:1], mv4[:, 3:4], op0=ALU.subtract,
                               op1=ALU.mult, deps=(s2,),
                               bufw=(b_x2,), bufr=(b_attn4, b_mv4))
                    t2 = pb.op("vector", "tensor_add", x2b[:], x2b[:], xb4[:],
                               deps=(t1, lx), bufw=(b_x2,), bufr=(b_xb4,))
                    s3 = ln4(lambda o, w: x2b[:, o:o + w], D, (b_x2,),
                             EPS_LN, (t2,), 4)
                    t3 = pb.op("vector", "tensor_scalar", xh2[:], x2b[:],
                               mv4[:, 4:5], mv4[:, 7:8], op0=ALU.subtract,
                               op1=ALU.mult, deps=(s3,),
                               bufw=(b_xh2,), bufr=(b_x2, b_mv4))
                    for dc in range(DC):
                        pt2 = 4 + dc % 2
                        stx = pb.op("tensor", "transpose", pk4[pt2][:, 0:128],
                                    xh2[:, dc * 128:(dc + 1) * 128], id32[:],
                                    deps=(t3,), inc=True,
                                    bufw=(b_pk4[pt2],),
                                    bufr=(b_xh2, b_consts))
                        fh, fl = evac_hilo(pk4[pt2][:, 0:128],
                                           xfm2_h[:, dc * 128:(dc + 1) * 128],
                                           xfm2_l[:, dc * 128:(dc + 1) * 128],
                                           (b_pk4[pt2],), b_xfm2_h, b_xfm2_l,
                                           deps=(stx,))
                        first = dc == 0
                        pb.op("tensor", "matmul", pk4[0][:, 0:E],
                              xfm2_h[:, dc * 128:(dc + 1) * 128],
                              rw_h[:, dc * E:(dc + 1) * E], start=first,
                              stop=False, deps=(fh, fl, b_rwh.writer,
                                                b_rwl.writer),
                              bufwar=(b_pk4[0],) if first else ())
                        pb.op("tensor", "matmul", pk4[0][:, 0:E],
                              xfm2_h[:, dc * 128:(dc + 1) * 128],
                              rw_l[:, dc * E:(dc + 1) * E], start=False,
                              stop=False)
                        pb.op("tensor", "matmul", pk4[0][:, 0:E],
                              xfm2_l[:, dc * 128:(dc + 1) * 128],
                              rw_h[:, dc * E:(dc + 1) * E], start=False,
                              stop=False)
                    pb.op("tensor", "matmul", pk4[0][:, 0:E], onesb[:],
                          rb_h[:], start=False, stop=False,
                          deps=(b_rbh.writer, b_rbl.writer, b_consts.writer))
                    lmm = pb.op("tensor", "matmul", pk4[0][:, 0:E], onesb[:],
                                rb_l[:], start=False, stop=True, inc=True,
                                bufw=(b_pk4[0],),
                                bufr=(b_xfm2_h, b_xfm2_l, b_rwh, b_rwl,
                                      b_rbh, b_rbl))
                    nm = pb.op("vector", "reduce_max", mv4[:, 0:1],
                               pk4[0][:, 0:E], axis=AX.X, negate=True,
                               deps=(lmm,), bufw=(b_mv4,), bufr=(b_pk4[0],))
                    ex = pb.op("scalar", "activation", pex[:], pk4[0][:, 0:E],
                               AF.Exp, bias=mv4[:, 0:1], scale=1.0, deps=(nm,),
                               bufw=(b_pex,), bufr=(b_pk4[0], b_mv4))
                    sm = pb.op("vector", "reduce_sum", mv4[:, 1:2], pex[:],
                               axis=AX.X, deps=(ex,), bufw=(b_mv4,),
                               bufr=(b_pex,))
                    iv = pb.op("vector", "reciprocal", mv4[:, 2:3], mv4[:, 1:2],
                               deps=(sm,), bufw=(b_mv4,))
                    pn = pb.op("scalar", "mul", pnr[:], pex[:], mv4[:, 2:3],
                               deps=(iv, ex), bufw=(b_pnr,),
                               bufr=(b_pex, b_mv4))
                    t8 = pb.op("vector", "max", top8[:], pnr[:], deps=(pn,),
                               bufw=(b_top8,), bufr=(b_pnr,))
                    msk = pb.op("vector", "tensor_single_scalar", wdn[:], pnr[:],
                                top8[:, 7:8], op=ALU.is_ge, deps=(t8,),
                                bufw=(b_wdn,), bufr=(b_pnr, b_top8))
                    wdm = pb.op("vector", "tensor_mul", wdn[:], wdn[:], pnr[:],
                                deps=(msk,), bufw=(b_wdn,), bufr=(b_pnr,))
                    ws = pb.op("vector", "reduce_sum", mv4[:, 3:4], wdn[:],
                               axis=AX.X, deps=(wdm,), bufw=(b_mv4,),
                               bufr=(b_wdn,))
                    sw = pb.op("tensor", "transpose", pk4[1][0:E, 0:128],
                               wdn[:], id32[:], deps=(wdm,), inc=True,
                               bufw=(b_pk4[1],), bufr=(b_wdn, b_consts))
                    ew1, ew2 = evac_hilo(pk4[1][0:E, 0:128],
                                         wdt_h[0:E, :], wdt_l[0:E, :],
                                         (b_pk4[1],), b_wdth, b_wdtl,
                                         deps=(sw,))
                    for ndt in range(D // 512):
                        bk = 2 + ndt % 2
                        pb.op("tensor", "matmul", pk4[bk][:, 0:512],
                              wdt_h[0:E, :], eb_h[0:E, ndt * 512:(ndt + 1) * 512],
                              start=True, stop=False,
                              deps=(ew1, ew2, b_eb_h.writer, b_eb_l.writer),
                              bufwar=(b_pk4[bk],))
                        pb.op("tensor", "matmul", pk4[bk][:, 0:512],
                              wdt_h[0:E, :], eb_l[0:E, ndt * 512:(ndt + 1) * 512],
                              start=False, stop=False)
                        mmb = pb.op("tensor", "matmul", pk4[bk][:, 0:512],
                                    wdt_l[0:E, :],
                                    eb_h[0:E, ndt * 512:(ndt + 1) * 512],
                                    start=False, stop=True, inc=True,
                                    bufw=(b_pk4[bk],),
                                    bufr=(b_wdth, b_wdtl, b_eb_h, b_eb_l))
                        pb.op("vector", "scalar_tensor_tensor",
                              ffnb[:, ndt * 512:(ndt + 1) * 512],
                              xh2[:, ndt * 512:(ndt + 1) * 512], mv4[:, 3:4],
                              pk4[bk][:, 0:512], op0=ALU.mult, op1=ALU.add,
                              deps=(mmb, ws), bufw=(b_ffn,),
                              bufr=(b_xh2, b_mv4, b_pk4[bk]))
                    s4 = ln4(lambda o, w: ffnb[:, o:o + w], D, (b_ffn,),
                             EPS_LN, (b_ffn.writer,), 4)
                    y1 = pb.op("vector", "tensor_scalar", xh2[:], ffnb[:],
                               mv4[:, 4:5], mv4[:, 7:8], op0=ALU.subtract,
                               op1=ALU.mult, deps=(s4,),
                               bufw=(b_xh2,), bufr=(b_ffn, b_mv4))
                    y2 = pb.op("vector", "tensor_add", xh2[:], xh2[:], x2b[:],
                               deps=(y1,), bufw=(b_xh2,), bufr=(b_x2,))
                    pb.op("scalar", "dma_start",
                          out=Y[towm * 128:(towm + 1) * 128, :],
                          in_=xh2[:], deps=(y2,), inc=True,
                          bufw=(b_y,), bufr=(b_xh2,))
            if phases < 5:
                z = pb.op("vector", "memset", xh2[:], 0.0, bufw=(b_xh2,))
                for towm in range(OWNT):
                    pb.op("scalar", "dma_start",
                          out=Y[towm * 128:(towm + 1) * 128, :], in_=xh2[:],
                          deps=(z,), inc=True, bufw=(b_y,), bufr=(b_xh2,))
            for sk, v in pb.cnt.items():
                if v > 0 and (sk.startswith("scalarq") or sk.startswith("gpsimd")
                              or sk.startswith("sync")):
                    pb._wait("gpsimd", sk, v)

        for e, v in pb.cnt.items():
            assert v < 60000, (e, v)

        blk = stack.enter_context(nc.Block())
        pb.realize(blk)

    nc.finalize()
    return nc



def make_shared_inputs(ins, cfg):
    g = derived(cfg)
    RC, KC, E = g["RC"], g["KC"], cfg["E"]
    cos, sin = yarn_cos_sin(cfg["S"], dim=64)
    tri = np.zeros((128, 128), np.float32)
    for r in range(128):
        tri[r, :r] = NEG
    shared = dict(
        wqat=np.ascontiguousarray(np.asarray(ins["L0_attn_wq_a"], np.float32).T),
        wqbt=np.ascontiguousarray(np.asarray(ins["L0_attn_wq_b"], np.float32).T),
        wkvat=np.ascontiguousarray(np.asarray(ins["L0_attn_wkv_a"], np.float32).T),
        wkvbt=np.ascontiguousarray(np.asarray(ins["L0_attn_wkv_b"], np.float32).T),
        wot=np.ascontiguousarray(np.asarray(ins["L0_attn_wo"], np.float32).T),
        rwt=np.ascontiguousarray(np.asarray(ins["L0_ffn_router_weight"], np.float32).T),
        rb=np.ascontiguousarray(np.asarray(ins["L0_ffn_router_bias"], np.float32)
                                .reshape(1, E)),
        eb=np.ascontiguousarray(np.asarray(ins["L0_ffn_expert_bias"], np.float32)),
        qaw=np.ascontiguousarray(np.asarray(ins["L0_attn_q_a_ln_w"], np.float32)
                                 .reshape(RC, 128)),
        kvw=np.ascontiguousarray(np.asarray(ins["L0_attn_kv_a_ln_w"], np.float32)
                                 .reshape(KC, 128)),
        tri=tri,
        id32=np.eye(128, dtype=np.float32),
        idbf=np.eye(128, dtype=BF16NP),
        onesb=np.ones((1, 128), dtype=BF16NP),
        _cos32=np.ascontiguousarray(cos[:, :32]),
        _sin32=np.ascontiguousarray(sin[:, :32]),
    )
    return shared


def make_core_inputs(c, ins, cfg, shared):
    g = derived(cfg)
    BS = g["BS"]
    x = np.asarray(ins["L0_attn_x"], np.float32)
    blocks = core_blocks(c)
    xk = np.concatenate([x[b, j * BS:(j + 1) * BS] for b, j in blocks], axis=0)
    pos = np.concatenate([np.arange(j * BS, (j + 1) * BS) for b, j in blocks])
    sela = np.zeros((128, 9), np.float32)
    selb = np.zeros((128, 9), np.float32)
    for j in range(7):
        if j < c:
            sela[:, j] = 1.0
        else:
            selb[:, j] = 1.0
    sela[:, 7] = 1.0
    selb[:, 8] = 1.0
    H = cfg["H"]
    qsel = np.zeros((128, H * 7), np.int32)
    for h in range(H):
        for j in range(7):
            side = 0 if j < c else 1
            qsel[:, h * 7 + j] = h * 256 + side * 128 + np.arange(128)
    d = {k: v for k, v in shared.items() if not k.startswith("_")}
    d.update(
        xk=np.ascontiguousarray(xk, np.float32),
        cosk=np.ascontiguousarray(shared["_cos32"][pos]),
        sink=np.ascontiguousarray(shared["_sin32"][pos]),
        sela=sela, selb=selb, qsel=qsel,
    )
    return d


_PROG_CACHE = {}


def get_program():
    if "nc" not in _PROG_CACHE:
        _PROG_CACHE["nc"] = build_program(REAL)
    return _PROG_CACHE["nc"]


def kernel(**inputs):
    cfg = REAL
    g = derived(cfg)
    nc = get_program()
    shared = make_shared_inputs(inputs, cfg)
    in_maps = [make_core_inputs(c, inputs, cfg, shared) for c in range(8)]
    res = run_bass_kernel_spmd(nc, in_maps, core_ids=list(range(8)))
    BS = g["BS"]
    y = np.empty((2, cfg["S"], cfg["D"]), np.float32)
    for c in range(8):
        yc = np.asarray(res.results[c]["y"])
        y[0, c * BS:(c + 1) * BS] = yc[0:BS]
        y[1, (7 - c) * BS:(8 - c) * BS] = yc[BS:2 * BS]
    return y
